# version 30
# baseline (speedup 1.0000x reference)
import numpy as np
import ml_dtypes
import concourse.bass as bass
import concourse.mybir as mybir
from concourse.bass_utils import run_bass_kernel_spmd
from contextlib import ExitStack

F32 = mybir.dt.float32
BF16 = mybir.dt.bfloat16
AF = mybir.ActivationFunctionType
ALU = mybir.AluOpType
AXX = mybir.AxisListType.X

D = 1024
NFC = 8
NT = 4096
TCW = 512
NTC = NT // TCW
II = 2816
NIC = II // 128
NH = 16
PL = 1366
NPT = 11
QLEN = (1366, 1365, 1365)
SEG = (0, 1366, 2731)
EPS = 1e-5
QMAXF = 127.0
BANDW = 136
ENGINES = ("tensor", "vector", "scalar", "gpsimd", "sync")


def qchunks(n):
    out, c = [], 0
    while c < n:
        out.append((c, min(512, n - c)))
        c += 512
    return out


class Sem:
    __slots__ = ("h", "v", "name")

    def __init__(self, h, name):
        self.h, self.v, self.name = h, 0, name


class B:
    def __init__(self, nc, stk):
        self.nc = nc
        self.stk = stk
        self.q = {k: [] for k in ENGINES}

    def sem(self, name):
        return Sem(self.stk.enter_context(self.nc.semaphore(name)), name)

    def ins(self, eng, meth, *args, inc=None, incn=1, **kw):
        self.q[eng].append((meth, args, kw, inc.h if inc else None, incn))
        if inc is not None:
            inc.v += incn

    def wait(self, eng, sem, val=None):
        v = sem.v if val is None else val
        assert v <= sem.v, f"wait beyond recorded increments: {sem.name} {v}>{sem.v}"
        self.q[eng].append(("wait_ge", (sem.h, v), {}, None, 0))

    def dma(self, eng, out, in_, sem):
        self.ins(eng, "dma_start", inc=sem, incn=16, out=out, in_=in_)

    def mm(self, out, lhsT, rhs, start, stop, inc=None):
        self.ins("tensor", "matmul", out, lhsT, rhs,
                 start=start, stop=stop, inc=inc)

    def drain(self, eng):
        self.q[eng].append(("drain", (), {}, None, 0))

    def mark(self, eng, sem):
        self.drain(eng)
        self.ins(eng, "nop", inc=sem)

    def barrier(self, sem):
        for eng in ENGINES:
            self.mark(eng, sem)
        tgt = sem.v
        for eng in ENGINES:
            self.wait(eng, sem, tgt)

    def emit(self):
        with self.nc.Block() as blk:
            for eng in ENGINES:
                getattr(blk, eng)(self._runner(eng))

    def _runner(self, eng):
        entries = self.q[eng]

        def run(e):
            for meth, args, kw, inch, incn in entries:
                r = getattr(e, meth)(*args, **kw)
                if inch is not None:
                    r.then_inc(inch, incn)
        return run


def strided_cols(t, start, step, count, width=1, nrows=128, row0=0):
    ap = t[row0:row0 + nrows, start:start + width]
    return bass.AP(ap.tensor, ap.offset, [ap.ap[0], [step, count], [1, width]])


def build_nc(ncores=8):
    nc = bass.Bass("TRN2", target_bir_lowering=False)
    d = {}
    for nm, shp, dt, kind in (
            ("xT", [D, NT], F32, "ExternalInput"),
            ("wqp", [D, D], F32, "ExternalInput"),
            ("wkp", [D, D], F32, "ExternalInput"),
            ("wvp", [D, D], F32, "ExternalInput"),
            ("wg_row", [II, D], F32, "ExternalInput"),
            ("wu_row", [II, D], F32, "ExternalInput"),
            ("wd_row", [D, II], F32, "ExternalInput"),
            ("clips", [1, 2], F32, "ExternalInput"),
            ("cband", [128, BANDW], BF16, "ExternalInput"),
            ("cident", [128, 128], BF16, "ExternalInput"),
            ("cvec", [1, PL], BF16, "ExternalInput"),
            ("cones", [128, 1], BF16, "ExternalInput"),
            ("outT", [D, NT], F32, "ExternalOutput"),
            ("mqg", [D, II], BF16, "Internal"),
            ("mqu", [D, II], BF16, "Internal"),
            ("mqgR", [II, D], BF16, "Internal"),
            ("mquR", [II, D], BF16, "Internal"),
            ("mqdR", [D, II], BF16, "Internal"),
            ("mqdC", [NFC, 128, NIC * 128], BF16, "Internal"),
            ("hscr", [II, NT], BF16, "Internal"),
            ("cc_in", [1, 128], F32, "Internal"),
            ("cc_out", [1, 128], F32, "Internal"),
            ("rows", [5, NT], F32, "Internal")):
        d[nm] = nc.dram_tensor(nm, shp, dt, kind=kind).ap()

    stk = ExitStack()
    b = B(nc, stk)

    g_x = b.sem("g_x")
    g_wr = (b.sem("g_wra"), b.sem("g_wrb"))
    g_wt2 = (b.sem("g_wta"), b.sem("g_wtb"))
    g_tp = (b.sem("g_tp0"), b.sem("g_tp1"))
    g_dw = b.sem("g_dw")
    g_dt = (b.sem("g_dt0"), b.sem("g_dt1"))
    t_cc = b.sem("t_cc")
    g_cc = b.sem("g_cc")
    g_hs = (b.sem("g_hsa"), b.sem("g_hsb"))
    g_hr3 = (b.sem("g_hr0"), b.sem("g_hr1"), b.sem("g_hr2"))
    g_row = b.sem("g_row")
    g_qk = (b.sem("g_qka"), b.sem("g_qkb"))
    g_wv = b.sem("g_wv")
    g_wv2 = b.sem("g_wv2")
    y_w = (b.sem("y_wa"), b.sem("y_wb"))
    y_o = (b.sem("y_oa"), b.sem("y_ob"), b.sem("y_oc"))
    y_cn = b.sem("y_cn")
    y_la = (b.sem("y_laa"), b.sem("y_lab"))
    y_sc = b.sem("y_sc")
    y_rr = b.sem("y_rr")
    y_m = (b.sem("y_ma"), b.sem("y_mb"))
    c_cc = b.sem("c_cc")
    t_st = b.sem("t_st")
    v_st = b.sem("v_st")
    s_sq = b.sem("s_sq")
    t_squ = b.sem("t_squ")
    v_ln = b.sem("v_ln")
    t_gu = b.sem("t_gu")
    t_uu = b.sem("t_uu")
    s_si = b.sem("s_si")
    v_h = b.sem("v_h")
    v_mx = b.sem("v_mx")
    t_wfree = b.sem("t_wfree")
    t_dn = b.sem("t_dn")
    v_dn = b.sem("v_dn")
    v_hq = b.sem("v_hq")
    v_qa = b.sem("v_qa")
    s_cv = b.sem("s_cv")
    t_qk = b.sem("t_qk")
    v_qk = b.sem("v_qk")
    t_sc = b.sem("t_sc")
    s_ex = b.sem("s_ex")
    t_av = b.sem("t_av")
    v_rc = b.sem("v_rc")
    v_oe = b.sem("v_oe")
    v_xa = b.sem("v_xa")
    g_xa = b.sem("g_xa")
    v_wr = b.sem("v_wr")
    v_wq = b.sem("v_wq")
    g_wb = b.sem("g_wb")
    bar = b.sem("bar")

    xs = stk.enter_context(nc.sbuf_tensor("xs", [128, NFC * NT], BF16))
    bufB = stk.enter_context(nc.sbuf_tensor("bufB", [128, 3 * NFC * PL], BF16))
    rows = stk.enter_context(nc.sbuf_tensor("rows_sb", [128, 1680], F32))
    scal = stk.enter_context(nc.sbuf_tensor("scal", [128, 160], F32))
    cband = stk.enter_context(nc.sbuf_tensor("cband_sb", [128, BANDW], BF16))
    cident = stk.enter_context(nc.sbuf_tensor("cident_sb", [128, 128], BF16))
    cones = stk.enter_context(nc.sbuf_tensor("cones_sb", [128, 1], BF16))
    ps = stk.enter_context(nc.psum_tensor("ps", [128, 4096], F32))

    def x_ch(fc):
        return xs[:, fc * NT:(fc + 1) * NT]

    SC_RX, SC_SX, SC_RH, SC_SH, SC_HMX, SC_TMP = 0, 1, 2, 3, 4, 5
    SC_WG, SC_WU, SC_WD = 8, 30, 52
    SC_SVG, SC_SVU, SC_SVD = 60, 82, 104
    SC_RTMP = 112
    SC_Q8 = 135
    SC_HALF = 7

    b.dma("sync", cband[:, :], d["cband"][:, :], y_cn)
    b.dma("sync", cident[:, :], d["cident"][:, :], y_cn)
    b.dma("sync", cones[:, :], d["cones"][:, :], y_cn)
    b.dma("sync", rows[0:1, 1540:1542], d["clips"][:, :], y_cn)
    for fc in range(NFC):
        b.dma("gpsimd", x_ch(fc), d["xT"][fc * 128:(fc + 1) * 128, :], g_x)

    b.ins("vector", "memset", scal[:, SC_Q8:SC_Q8 + 1], 0.125)
    b.drain("vector")
    b.ins("vector", "memset", scal[:, SC_HALF:SC_HALF + 1], 0.5)
    b.drain("vector")
    b.wait("tensor", y_cn)
    b.wait("scalar", y_cn)

    def row_bcast_ap(r, col, nrows, width):
        return bass.AP(d["rows"].tensor, d["rows"].offset + r * NT + col,
                       [[0, nrows], [1, width]])

    def rec_weight_quant(wfscr, wq2, m2rep, fint, fint2, mdc, hstg,
                         hooks, down_out=None):
        chunks = []
        for wi, spec in enumerate((("wg_row", "mqgR", NIC, D, SC_WG),
                                   ("wu_row", "mquR", NIC, D, SC_WU),
                                   ("wd_row", "mqdR", NFC, II, SC_WD))):
            for ic in range(spec[2]):
                chunks.append((wi,) + spec + (ic,))
        NCH = len(chunks)
        tasks = []

        def load(K):
            wi, rowt, outR, nch, wdt, scol, ic = chunks[K]
            par = K % 2
            if wdt > 1024:
                half = wdt // 2
                for eng in ("sync", "gpsimd"):
                    b.wait(eng, v_wq)
                    b.wait(eng, s_cv)
                    b.wait(eng, v_qa)
                b.dma("sync", wfscr[:, 0:half],
                      d[rowt][ic * 128:(ic + 1) * 128, 0:half], g_wt2[0])
                b.dma("gpsimd", wfscr[:, half:wdt],
                      d[rowt][ic * 128:(ic + 1) * 128, half:wdt], g_dw)
                return
            wrow = wfscr[:, 0:wdt] if par == 0 else wq2[:, 0:wdt]
            eng = "sync" if par == 0 else "gpsimd"
            if K >= 2:
                b.wait(eng, v_wq)
                b.wait(eng, s_cv)
            else:
                b.wait(eng, v_mx)
            b.dma(eng, wrow, d[rowt][ic * 128:(ic + 1) * 128, :],
                  g_wt2[par])

        def compute(K):
            wi, rowt, outR, nch, wdt, scol, ic = chunks[K]
            par = K % 2
            wide = wdt > 1024
            wrow = wfscr[:, 0:wdt] if (par == 0 or wide) else wq2[:, 0:wdt]
            qsl = mdc[:, par * II:par * II + wdt]
            rt = scal[:, SC_RTMP + par:SC_RTMP + par + 1]
            sc = scal[:, scol + ic:scol + ic + 1]
            if wide:
                b.wait("vector", g_wt2[0])
                b.wait("vector", g_dw)
            else:
                b.wait("vector", g_wt2[par])
            b.ins("vector", "tensor_reduce", sc, wrow, AXX, ALU.max,
                  apply_absolute_value=True)
            b.drain("vector")
            b.ins("vector", "tensor_single_scalar", rt, sc, 1.0 / QMAXF,
                  ALU.mult)
            b.drain("vector")
            b.ins("vector", "reciprocal", rt, rt, inc=v_wr)
            b.drain("vector")
            b.wait("vector", y_m[par])
            b.wait("scalar", v_wr)
            nsub = (wdt + 511) // 512
            qa0, sc0 = v_qa.v, s_cv.v
            for si in range(nsub):
                c0 = si * 512
                cw = min(512, wdt - c0)
                sp = si % 2
                u = m2rep[:, sp * 512:sp * 512 + cw]
                ti = (fint if sp == 0 else fint2)[:, 0:cw]
                if si >= 2:
                    b.wait("scalar", v_qa, qa0 + si - 1)
                b.ins("scalar", "activation", u, wrow[:, c0:c0 + cw],
                      AF.Identity, scale=rt,
                      bias=scal[:, SC_HALF:SC_HALF + 1])
                b.drain("scalar")
                b.ins("scalar", "copy", ti, u, inc=s_cv)
                b.drain("scalar")
                b.wait("vector", s_cv, sc0 + si + 1)
                b.ins("vector", "tensor_tensor", u, ti, u, ALU.is_gt)
                b.drain("vector")
                b.ins("vector", "tensor_tensor", qsl[:, c0:c0 + cw], ti, u,
                      ALU.subtract,
                      inc=(v_wq if si == nsub - 1 else v_qa))
                if si == nsub - 1:
                    b.mark("vector", v_qa)
            b.wait("sync", v_wq)
            b.dma("sync", d[outR][ic * 128:(ic + 1) * 128, :], qsl, y_m[par])
            if wi == 0 and ic == nch - 1:
                b.wait("sync", y_m[0])
                b.wait("sync", y_m[1])
                tasks.extend((outR, "mqg", fc) for fc in range(NFC))

        def transpose_task():
            outR, outT, fc = tasks.pop(0)
            tt = hstg[:, (fc % 2) * II:(fc % 2) * II + II]
            b.wait("sync", g_hs[fc % 2])
            b.ins("sync", "dma_start", out=tt,
                  in_=d[outR][:, fc * 128:(fc + 1) * 128],
                  transpose=True, inc=g_tp[fc % 2], incn=16)
            b.wait("gpsimd", g_tp[fc % 2])
            b.dma("gpsimd", d[outT][fc * 128:(fc + 1) * 128, :], tt,
                  g_hs[fc % 2])

        if down_out is not None:
            down_out["load"] = lambda k: load(44 + k)
            down_out["compute"] = lambda k: compute(44 + k)
            NCH = 44
        b.wait("sync", v_mx)
        load(0)
        load(1)
        for K in range(NCH):
            if chunks[K][4] > 1024:
                load(K)
            compute(K)
            if tasks:
                transpose_task()
            if K == 43:
                b.wait("sync", y_m[0])
                b.wait("sync", y_m[1])
                tasks.extend(("mquR", "mqu", fc) for fc in range(NFC))
                while tasks:
                    transpose_task()
            for cb in hooks.get(K, ()):
                cb()
            if K + 2 < NCH and chunks[K + 2][4] <= 1024:
                load(K + 2)
        assert not tasks
        for e in ("sync", "vector"):
            b.wait(e, g_hs[0])
            b.wait(e, g_hs[1])

    def rec_down_transpose_one(mdc, ic2):
        tt = mdc[:, (ic2 % 2) * II:(ic2 % 2) * II + D]
        if ic2 < 2:
            b.wait("sync", y_m[0])
            b.wait("sync", y_m[1])
        else:
            b.wait("sync", g_dt[ic2 % 2])
        b.ins("sync", "dma_start", out=tt,
              in_=d["mqdR"][:, ic2 * 128:(ic2 + 1) * 128],
              transpose=True, inc=g_tp[ic2 % 2], incn=16)
        b.wait("gpsimd", g_tp[ic2 % 2])
        dst = bass.AP(d["mqdC"].tensor, d["mqdC"].offset + ic2 * 128,
                      [[II, 128], [128 * II, NFC], [1, 128]])
        b.dma("gpsimd", dst, tt.rearrange("p (fco j) -> p fco j", j=128),
              g_dt[ic2 % 2])

    def rec_ln_stats(sqpp):
        v_st0 = v_st.v
        for tc in range(NTC):
            su = ps[0:1, (tc % 2) * TCW:(tc % 2) * TCW + TCW]
            sq = ps[0:1, 1024 + (tc % 2) * TCW:1024 + (tc % 2) * TCW + TCW]
            if tc >= 2:
                b.wait("tensor", v_st, v_st0 + tc - 1)
            for fc in range(NFC):
                xsl = x_ch(fc)[:, tc * TCW:(tc + 1) * TCW]
                sqs = sqpp[:, (fc % 2) * TCW:(fc % 2) * TCW + TCW]
                if fc >= 2 or tc > 0:
                    b.wait("scalar", t_squ)
                b.ins("scalar", "activation", sqs, xsl, AF.Square, inc=s_sq)
                b.wait("tensor", s_sq)
                if fc < NFC - 1:
                    b.mm(su, cones[:, :], xsl, fc == 0, False)
                    b.mm(sq, cones[:, :], sqs, fc == 0, False, inc=t_squ)
                else:
                    b.mm(sq, cones[:, :], sqs, False, True, inc=t_squ)
                    b.mm(su, cones[:, :], xsl, False, True, inc=t_st)
            b.wait("vector", t_st)
            b.wait("vector", g_row)
            r0 = rows[0:1, 0:TCW]
            r1 = rows[0:1, TCW:2 * TCW]
            r2 = rows[0:1, 2 * TCW:3 * TCW]
            b.ins("vector", "tensor_single_scalar", r0, su, 1.0 / D, ALU.mult)
            b.ins("vector", "tensor_single_scalar", r2, sq, 1.0 / D, ALU.mult)
            b.drain("vector")
            b.ins("vector", "tensor_tensor", r1, r0, r0, ALU.mult)
            b.drain("vector")
            b.ins("vector", "tensor_tensor", r2, r2, r1, ALU.subtract)
            b.drain("vector")
            b.ins("vector", "tensor_single_scalar", r2, r2, EPS, ALU.add,
                  inc=v_ln)
            b.drain("vector")
            b.wait("scalar", v_ln)
            b.ins("scalar", "activation", r2, r2, AF.Sqrt, inc=s_sq)
            b.wait("vector", s_sq)
            b.ins("vector", "reciprocal", r1, r2)
            b.drain("vector")
            b.ins("vector", "tensor_tensor", r0, r0, r1, ALU.mult, inc=v_st)
            b.drain("vector")
            b.wait("sync", v_st)
            b.dma("sync", d["rows"][0:1, tc * TCW:(tc + 1) * TCW], r0, g_row)
            b.dma("sync", d["rows"][1:2, tc * TCW:(tc + 1) * TCW], r1, g_row)
        b.wait("sync", g_row)

    def rec_stats_chunk(sqpp, c0, cw):
        su = ps[0:1, 3072:3072 + cw]
        sq = ps[0:1, 3584:3584 + cw]
        s_sq0, t_squ0 = s_sq.v, t_squ.v
        b.wait("tensor", v_st)

        def sqr(fc):
            sqs = sqpp[:, (fc % 2) * TCW:(fc % 2) * TCW + cw]
            b.wait("vector", t_squ,
                   t_squ0 + max(0, fc - 1) if fc >= 2 else t_squ0)
            b.ins("vector", "tensor_tensor", sqs,
                  x_ch(fc)[:, c0:c0 + cw], x_ch(fc)[:, c0:c0 + cw],
                  ALU.mult, inc=s_sq)
            b.drain("vector")

        sqr(0)
        sqr(1)
        for fc in range(NFC):
            xsl = x_ch(fc)[:, c0:c0 + cw]
            sqs = sqpp[:, (fc % 2) * TCW:(fc % 2) * TCW + cw]
            b.wait("tensor", s_sq, s_sq0 + fc + 1)
            if fc < NFC - 1:
                b.mm(su, cones[:, :], xsl, fc == 0, False)
                b.mm(sq, cones[:, :], sqs, fc == 0, False, inc=t_squ)
            else:
                b.mm(sq, cones[:, :], sqs, False, True, inc=t_squ)
                b.mm(su, cones[:, :], xsl, False, True, inc=t_st)
            if fc + 2 < NFC:
                sqr(fc + 2)
        b.wait("vector", t_st)
        b.wait("vector", g_row)
        r0 = rows[0:1, 0:TCW]
        r1 = rows[0:1, TCW:2 * TCW]
        r2 = rows[0:1, 2 * TCW:3 * TCW]
        b.ins("vector", "tensor_single_scalar", r0[:, 0:cw], su, 1.0 / D,
              ALU.mult)
        b.ins("vector", "tensor_single_scalar", r2[:, 0:cw], sq, 1.0 / D,
              ALU.mult)
        b.drain("vector")
        b.ins("vector", "tensor_tensor", r1[:, 0:cw], r0[:, 0:cw],
              r0[:, 0:cw], ALU.mult)
        b.drain("vector")
        b.ins("vector", "tensor_tensor", r2[:, 0:cw], r2[:, 0:cw],
              r1[:, 0:cw], ALU.subtract)
        b.drain("vector")
        b.ins("vector", "tensor_single_scalar", r2[:, 0:cw], r2[:, 0:cw],
              EPS, ALU.add, inc=v_ln)
        b.drain("vector")
        b.wait("scalar", v_ln)
        b.ins("scalar", "activation", r2[:, 0:cw], r2[:, 0:cw], AF.Sqrt,
              inc=s_sq)
        b.wait("vector", s_sq)
        b.ins("vector", "reciprocal", r1[:, 0:cw], r2[:, 0:cw])
        b.drain("vector")
        b.ins("vector", "tensor_tensor", r0[:, 0:cw], r0[:, 0:cw],
              r1[:, 0:cw], ALU.mult, inc=v_st)
        b.drain("vector")
        b.wait("sync", v_st)
        b.dma("sync", d["rows"][3:4, c0:c0 + cw], r0[:, 0:cw], g_row)
        b.dma("sync", d["rows"][4:5, c0:c0 + cw], r1[:, 0:cw], g_row)

    def rec_ln_apply(m2rep, rsrep, scr, seg0, width, ycol, ystride,
                     rm=0, rr=1):
        b.wait("sync", g_row)
        chs = qchunks(width)

        def rload(t):
            c0r, cw = chs[t]
            p = t % 2
            b.wait("sync", v_ln)
            b.dma("sync", m2rep[:, p * TCW:p * TCW + cw],
                  row_bcast_ap(rm, seg0 + c0r, 128, cw), y_la[p])
            b.dma("sync", rsrep[:, p * TCW:p * TCW + cw],
                  row_bcast_ap(rr, seg0 + c0r, 128, cw), y_la[p])

        rload(0)
        if len(chs) > 1:
            rload(1)
        for t, (c0r, cw) in enumerate(chs):
            c0 = seg0 + c0r
            p = t % 2
            b.wait("vector", y_la[p])
            for fc in range(NFC):
                xsl = x_ch(fc)[:, c0:c0 + cw]
                ysl = bufB[:, ycol + fc * ystride + c0r:
                           ycol + fc * ystride + c0r + cw]
                b.ins("vector", "tensor_tensor", scr[:, 0:cw], xsl,
                      rsrep[:, p * TCW:p * TCW + cw], ALU.mult)
                b.drain("vector")
                b.ins("vector", "tensor_tensor", ysl, scr[:, 0:cw],
                      m2rep[:, p * TCW:p * TCW + cw], ALU.subtract,
                      inc=(v_ln if fc == NFC - 1 else None))
                b.drain("vector")
            if t + 2 < len(chs):
                rload(t + 2)

    def rec_allreduce_issue(mx_col):
        b.wait("gpsimd", v_mx)
        b.dma("gpsimd", d["cc_in"][:, :], scal[:, mx_col:mx_col + 1], g_cc)
        b.wait("gpsimd", g_cc)
        b.ins("gpsimd", "collective_compute", "AllReduce", ALU.max,
              [list(range(ncores))], [d["cc_in"][:, :]],
              [d["cc_out"][:, :]], inc=c_cc)

    def rec_allreduce_finish(rcol, scol, clip_col):
        b.wait("scalar", c_cc)
        b.dma("scalar", rows[0:1, 1552:1680], d["cc_out"][:, :], t_cc)
        b.wait("vector", t_cc)
        b.ins("vector", "tensor_reduce", rows[0:1, 1544:1545],
              rows[0:1, 1552:1680], AXX, ALU.max, inc=v_ln)
        b.wait("scalar", v_ln)
        b.wait("scalar", y_cn, 64)
        b.ins("scalar", "activation", rows[0:1, 1545:1546],
              rows[0:1, 1540 + clip_col:1541 + clip_col], AF.Sigmoid, inc=s_sq)
        b.wait("vector", s_sq)
        b.ins("vector", "tensor_tensor", rows[0:1, 1546:1547],
              rows[0:1, 1544:1545], rows[0:1, 1545:1546], ALU.mult)
        b.drain("vector")
        b.ins("vector", "tensor_single_scalar", rows[0:1, 1546:1547],
              rows[0:1, 1546:1547], 1.0 / QMAXF, ALU.mult)
        b.drain("vector")
        b.ins("vector", "reciprocal", rows[0:1, 1547:1548],
              rows[0:1, 1546:1547], inc=v_ln)
        b.wait("scalar", v_ln)
        b.dma("scalar", d["rows"][4:5, 129:131], rows[0:1, 1546:1548], t_cc)
        b.wait("scalar", t_cc)
        b.dma("scalar", scal[:, scol:scol + 1],
              row_bcast_ap(4, 129, 128, 1), y_sc)
        b.dma("scalar", scal[:, rcol:rcol + 1],
              row_bcast_ap(4, 130, 128, 1), y_sc)
        b.wait("vector", y_sc)
        b.wait("scalar", y_sc)

    def rec_allreduce_scale(mx_col, rcol, scol, clip_col):
        rec_allreduce_issue(mx_col)
        rec_allreduce_finish(rcol, scol, clip_col)

    def rec_quant_inplace(cols, rcol, wfscr, wint, wint2, sem):
        subs = []
        for bt, col0, width in cols:
            for c0 in range(0, width, 1024):
                subs.append((bt, col0 + c0, min(1024, width - c0), False))
        subs[-1] = subs[-1][:3] + (True,)
        v_qa0, s_cv0 = v_qa.v, s_cv.v

        def s_part(k):
            bt, c0, cw, last = subs[k]
            par = k % 2
            wc = wfscr[:, par * 1024:par * 1024 + cw]
            ti = (wint if par == 0 else wint2)[:, 0:cw]
            b.wait("scalar", v_qa, v_qa0 + max(0, k - 1))
            b.ins("scalar", "activation", wc, bt[:, c0:c0 + cw], AF.Identity,
                  scale=scal[:, rcol:rcol + 1],
                  bias=scal[:, SC_HALF:SC_HALF + 1])
            b.drain("scalar")
            b.ins("scalar", "copy", ti, wc, inc=s_cv)
            b.drain("scalar")

        def v_part2(k):
            bt, c0, cw, last = subs[k]
            par = k % 2
            wc = wfscr[:, par * 1024:par * 1024 + cw]
            ti = (wint if par == 0 else wint2)[:, 0:cw]
            b.wait("vector", s_cv, s_cv0 + k + 1)
            b.ins("vector", "tensor_tensor", wc, ti, wc, ALU.is_gt)
            b.drain("vector")
            b.ins("vector", "tensor_tensor", bt[:, c0:c0 + cw], ti, wc,
                  ALU.subtract)
            b.drain("vector")
            b.ins("vector", "tensor_scalar", bt[:, c0:c0 + cw],
                  bt[:, c0:c0 + cw], QMAXF, -128.0, ALU.min, ALU.max,
                  inc=(sem if last else v_qa))
            if last:
                b.mark("vector", v_qa)
        for k in range(len(subs)):
            s_part(k)
            if k >= 1:
                v_part2(k - 1)
        v_part2(len(subs) - 1)

    def rec_ffn(last):
        tag = '2' if last else '1'
        fs = ExitStack()
        m2rep = fs.enter_context(nc.sbuf_tensor("f_m2" + tag,
                                                [128, 2 * TCW], F32))
        rsrep = fs.enter_context(nc.sbuf_tensor("f_rs" + tag,
                                                [128, 2 * TCW], F32))
        wfscr = fs.enter_context(nc.sbuf_tensor("f_wf" + tag, [128, II], F32))
        fint = fs.enter_context(nc.sbuf_tensor("f_int" + tag, [128, 1024],
                                               mybir.dt.int32))
        hstg = fs.enter_context(nc.sbuf_tensor("f_h" + tag, [128, 2 * NT], BF16))
        silu = fs.enter_context(nc.sbuf_tensor("f_si" + tag, [128, 2 * TCW], BF16))
        sqpp = silu
        mgut = fs.enter_context(nc.sbuf_tensor("f_w" + tag, [128, 2 * 2048], BF16))
        mdc = fs.enter_context(nc.sbuf_tensor("f_wd" + tag, [128, 2 * II], BF16))
        outst = (fs.enter_context(nc.sbuf_tensor("f_o" + tag,
                                                  [128, 3 * TCW], F32))
                 if last else None)
        wq2 = (fs.enter_context(nc.sbuf_tensor("f_wq2", [128, 1024], F32))
               if not last else None)
        fint2 = fs.enter_context(nc.sbuf_tensor("f_int2" + tag, [128, 1024],
                                                mybir.dt.int32))

        b.wait("scalar", g_x)
        b.wait("tensor", g_x)
        b.wait("vector", g_x)
        if not last:
            rec_ln_stats(sqpp)
            rec_ln_apply(m2rep, rsrep, wfscr, 0, NT, 0, NT)
        else:
            rec_ln_apply(m2rep, rsrep, wfscr, 0, NT, 0, NT, rm=3, rr=4)
        b.wait("vector", v_ln)
        b.ins("vector", "tensor_reduce", scal[:, SC_TMP:SC_TMP + 1],
              bufB[:, 0:NFC * NT], AXX, ALU.max, apply_absolute_value=True,
              inc=v_mx)
        b.drain("vector")
        rec_allreduce_issue(SC_TMP)
        st_c = {}

        def rec_xscale():
            rec_allreduce_finish(SC_RX, SC_SX, 0)
            b.ins("vector", "tensor_scalar", scal[:, SC_SVG:SC_SVG + NIC],
                  scal[:, SC_WG:SC_WG + NIC], scal[:, SC_SX:SC_SX + 1],
                  1.0 / QMAXF, ALU.mult, ALU.mult)
            b.drain("vector")

        def rec_xquant():
            b.ins("vector", "tensor_scalar", scal[:, SC_SVU:SC_SVU + NIC],
                  scal[:, SC_WU:SC_WU + NIC], scal[:, SC_SX:SC_SX + 1],
                  1.0 / QMAXF, ALU.mult, ALU.mult)
            b.drain("vector")
            st_c["vlnb"] = v_ln.v
            st_c["xqb"] = v_qa.v
            rec_quant_inplace([(bufB, fc * NT + tq * 1024, 1024)
                               for tq in range(NT // 1024)
                               for fc in range(NFC)],
                              SC_RX, wfscr, fint, fint2, v_ln)
            b.ins("vector", "memset", scal[:, SC_HMX:SC_HMX + 1], 0.0)

        wq_down = {}
        if not last:
            rec_weight_quant(wfscr, wq2, m2rep, fint, fint2, mdc, hstg,
                             {21: (rec_xscale,), 43: (rec_xquant,)},
                             wq_down)
        else:
            rec_xscale()
            rec_xquant()
        vlnb, xqb = st_c["vlnb"], st_c["xqb"]
        b.wait("tensor", v_ln, vlnb)

        t_gu0, t_uu0, s_si0, v_h0 = t_gu.v, t_uu.v, s_si.v, v_h.v
        mg_r = d["mqg"].rearrange("(fc p) i -> p fc i", p=128)
        mu_r = d["mqu"].rearrange("(fc p) i -> p fc i", p=128)
        for ic in range(NIC):
            par = ic % 2
            if ic >= 2:
                b.wait("sync", t_uu, t_uu0 + NTC * (ic - 1))
            wg_dst = mgut[:, par * 2048:par * 2048 + 1024].rearrange(
                "p (fc j) -> p fc j", j=128)
            wu_dst = mgut[:, par * 2048 + 1024:par * 2048 + 2048].rearrange(
                "p (fc j) -> p fc j", j=128)
            b.dma("sync", wg_dst, mg_r[:, :, ic * 128:(ic + 1) * 128],
                  y_w[ic % 2])
            b.dma("sync", wu_dst, mu_r[:, :, ic * 128:(ic + 1) * 128],
                  y_w[ic % 2])
            if not last:
                if ic % 2 == 0 and ic // 2 < NFC:
                    wq_down["load"](ic // 2)
                elif ic % 2 == 1 and ic // 2 < NFC:
                    wq_down["compute"](ic // 2)
                if ic >= 16:
                    for j_ in range(4):
                        t_ = (ic - 16) * 4 + j_
                        if t_ < NIC:
                            rec_down_transpose_one(mdc, t_)
            b.wait("tensor", y_w[ic % 2])
            for tc in range(NTC):
                idx = ic * NTC + tc
                tpar = tc % 4
                pg = ps[:, tpar * TCW:tpar * TCW + TCW]
                pu = ps[:, 2048 + tpar * TCW:2048 + tpar * TCW + TCW]
                if ic == 0:
                    b.wait("tensor", v_qa, xqb + NFC * (tc // 2 + 1))
                if idx >= 4:
                    b.wait("tensor", v_h, v_h0 + idx - 3)
                for fc in range(NFC):
                    b.mm(pg, mgut[:, par * 2048 + fc * 128:par * 2048 + fc * 128 + 128],
                         bufB[:, fc * NT + tc * TCW:fc * NT + tc * TCW + TCW],
                         fc == 0, fc == NFC - 1,
                         inc=(t_gu if fc == NFC - 1 else None))
                for fc in range(NFC):
                    b.mm(pu, mgut[:, par * 2048 + 1024 + fc * 128:
                                  par * 2048 + 1024 + fc * 128 + 128],
                         bufB[:, fc * NT + tc * TCW:fc * NT + tc * TCW + TCW],
                         fc == 0, fc == NFC - 1,
                         inc=(t_uu if fc == NFC - 1 else None))
                b.wait("scalar", t_gu, t_gu0 + idx + 1)
                b.wait("scalar", v_h, v_h0 + max(0, idx - 1))
                b.ins("scalar", "activation",
                      silu[:, (tc % 2) * TCW:(tc % 2) * TCW + TCW],
                      pg, AF.Silu, scale=scal[:, SC_SVG + ic:SC_SVG + ic + 1],
                      inc=s_si)
                b.wait("vector", s_si, s_si0 + idx + 1)
                b.wait("vector", t_uu, t_uu0 + idx + 1)
                if tc == 0 and ic >= 2:
                    b.wait("vector", g_hs[ic % 2])
                b.ins("vector", "scalar_tensor_tensor",
                      hstg[:, par * NT + tc * TCW:par * NT + tc * TCW + TCW],
                      pu, scal[:, SC_SVU + ic:SC_SVU + ic + 1],
                      silu[:, (tc % 2) * TCW:(tc % 2) * TCW + TCW],
                      ALU.mult, ALU.mult, inc=v_h)
            b.drain("vector")
            b.ins("vector", "tensor_reduce", scal[:, SC_TMP:SC_TMP + 1],
                  hstg[:, par * NT:par * NT + NT], AXX, ALU.max,
                  apply_absolute_value=True)
            b.drain("vector")
            b.ins("vector", "tensor_tensor", scal[:, SC_HMX:SC_HMX + 1],
                  scal[:, SC_HMX:SC_HMX + 1], scal[:, SC_TMP:SC_TMP + 1],
                  ALU.max, inc=v_mx)
            b.drain("vector")
            b.wait("gpsimd", v_h, v_h0 + (ic + 1) * NTC)
            b.dma("gpsimd", d["hscr"][ic * 128:(ic + 1) * 128, :],
                  hstg[:, par * NT:par * NT + NT], g_hs[ic % 2])

        t_dn0, v_dn0 = t_dn.v, v_dn.v
        h_r = d["hscr"].rearrange("(ic p) c -> p ic c", p=128)
        HQT = NIC * TCW
        wblk = [(mdc, 0), (mdc, II), (hstg, 0), (hstg, II), (mgut, 0),
                (bufB, 2 * HQT), (bufB, 2 * HQT + II), (bufB, 2 * HQT + 2 * II)]
        for eng in ("sync", "gpsimd"):
            b.wait(eng, t_uu, t_uu0 + NIC * NTC)
            b.wait(eng, v_mx)
            b.wait(eng, y_m[0])
            b.wait(eng, y_m[1])
            b.wait(eng, g_hs[0])
            b.wait(eng, g_hs[1])
            b.wait(eng, g_dt[0])
            b.wait(eng, g_dt[1])
        for fco in range(NFC):
            bt, c0 = wblk[fco]
            if fco % 2 == 0:
                b.dma("sync", bt[:, c0:c0 + II], d["mqdC"][fco], y_w[0])
            else:
                b.dma("gpsimd", bt[:, c0:c0 + II], d["mqdC"][fco], g_dw)
        for s in (0, 1):
            dstS = bufB[:, s * HQT:s * HQT + HQT].rearrange(
                "p (ic j) -> p ic j", j=TCW)
            b.dma("sync", dstS, h_r[:, :, s * TCW:(s + 1) * TCW], y_la[s])

        rec_allreduce_scale(SC_HMX, SC_RH, SC_SH, 1)
        b.ins("vector", "tensor_scalar", scal[:, SC_SVD:SC_SVD + NFC],
              scal[:, SC_WD:SC_WD + NFC], scal[:, SC_SH:SC_SH + 1],
              0.5 / QMAXF, ALU.mult, ALU.mult)
        b.drain("vector")

        qb = [0] * NTC

        def stage_quant(s):
            ssem = y_la[s] if s < 2 else g_hr3[s % 2]
            b.wait("vector", ssem)
            b.wait("scalar", ssem)
            qb[s] = v_qa.v
            rec_quant_inplace([(bufB, (s % 2) * HQT, HQT)], SC_RH,
                              wfscr, fint, fint2, v_hq)

        stage_quant(0)
        for tc in range(NTC):
            par = tc % 2
            for ic in range(NIC):
                if ic % 2 == 0:
                    b.wait("tensor", v_qa, qb[tc] + ic // 2 + 1)
                for fco in range(NFC):
                    bt, c0 = wblk[fco]
                    if ic == 0:
                        if tc == 0:
                            if fco == 0:
                                b.wait("tensor", y_w[0])
                                b.wait("tensor", g_dw)
                                b.wait("tensor", s_si, s_si0 + NIC * NTC)
                                b.wait("tensor", v_h, v_h0 + NIC * NTC)
                        else:
                            b.wait("tensor", v_dn,
                                   v_dn0 + (tc - 1) * NFC + fco + 1)
                    lastmm = ic == NIC - 1
                    b.mm(ps[:, fco * TCW:fco * TCW + TCW],
                         bt[:, c0 + ic * 128:c0 + ic * 128 + 128],
                         bufB[:, par * HQT + ic * TCW:par * HQT + ic * TCW + TCW],
                         ic == 0, lastmm, inc=(t_dn if lastmm else None))
            for fco in range(NFC):
                b.wait("vector", t_dn, t_dn0 + tc * NFC + fco + 1)
                pd = ps[:, fco * TCW:fco * TCW + TCW]
                xsl = x_ch(fco)[:, tc * TCW:(tc + 1) * TCW]
                if not last:
                    b.ins("vector", "scalar_tensor_tensor", xsl, pd,
                          scal[:, SC_SVD + fco:SC_SVD + fco + 1], xsl,
                          ALU.mult, ALU.add, inc=v_dn)
                    b.drain("vector")
                else:
                    u_ = tc * NFC + fco
                    opar = u_ % 3
                    if u_ >= 3:
                        b.wait("vector", y_o[opar], 16 * ((u_ - 3) // 3 + 1))
                    b.ins("vector", "scalar_tensor_tensor",
                          outst[:, opar * TCW:opar * TCW + TCW], pd,
                          scal[:, SC_SVD + fco:SC_SVD + fco + 1], xsl,
                          ALU.mult, ALU.add, inc=v_dn)
                    b.drain("vector")
                    b.wait("sync", v_dn, v_dn0 + u_ + 1)
                    b.dma("sync", d["outT"][fco * 128:(fco + 1) * 128,
                                            tc * TCW:(tc + 1) * TCW],
                          outst[:, opar * TCW:opar * TCW + TCW],
                          y_o[opar])
            if tc + 2 < NTC:
                b.wait("gpsimd", t_dn, t_dn0 + (tc + 1) * NFC)
                dstS = bufB[:, (tc % 2) * HQT:(tc % 2) * HQT + HQT
                            ].rearrange("p (ic j) -> p ic j", j=TCW)
                b.dma("gpsimd", dstS,
                      h_r[:, :, (tc + 2) * TCW:(tc + 3) * TCW],
                      g_hr3[tc % 2])
            if tc + 1 < NTC:
                stage_quant(tc + 1)
        b.barrier(bar)
        fs.close()

    def pt_col(i):
        return i * 1040

    def rec_attn():
        as_ = ExitStack()
        sqpp = as_.enter_context(nc.sbuf_tensor("a_sq", [128, 2 * TCW], BF16))
        m2rep = as_.enter_context(nc.sbuf_tensor("a_m2", [128, 2 * TCW], F32))
        rsrep = as_.enter_context(nc.sbuf_tensor("a_rs", [128, 2 * TCW], F32))
        ascr = as_.enter_context(nc.sbuf_tensor("a_scr", [128, 512], F32))
        wqkt = as_.enter_context(nc.sbuf_tensor("a_wqk", [128, 2 * 1024], BF16))
        wvt = as_.enter_context(nc.sbuf_tensor("a_wv", [128, 8 * 512], BF16))
        vS = as_.enter_context(nc.sbuf_tensor("a_v", [128, NPT * 1040], BF16))
        Epp = as_.enter_context(nc.sbuf_tensor("a_e", [128, 2 * PL], BF16))
        arr = as_.enter_context(nc.sbuf_tensor("a_rr", [128, PL], F32))
        Osb = as_.enter_context(nc.sbuf_tensor("a_o", [128, 2 * PL], BF16))
        t2b = as_.enter_context(nc.sbuf_tensor("a_t2", [128, PL], BF16))
        cv64 = as_.enter_context(nc.sbuf_tensor("a_cv64", [128, PL], BF16))

        b.dma("sync", cv64[64:65, :], d["cvec"][0:1, :], y_cn)
        rec_ln_stats(sqpp)
        wq_r = d["wqp"].rearrange("(fc p) c -> p fc c", p=128)
        wk_r = d["wkp"].rearrange("(fc p) c -> p fc c", p=128)
        wv_r = d["wvp"].rearrange("(fc p) c -> p fc c", p=128)
        Y2C, QSC, KSC = 0, NFC * PL, 2 * NFC * PL

        for s in range(3):
            qlen = QLEN[s]
            seg = SEG[s]
            if s == 0:
                rec_ln_apply(m2rep, rsrep, ascr, seg, qlen, Y2C, PL)
            b.wait("vector", v_ln)
            qk_list = [(li, wr, outc, qd)
                       for li, (wr, outc) in enumerate(((wq_r, QSC),
                                                        (wk_r, KSC)))
                       for qd in range(NFC)]

            def qk_load(j):
                li, wr, outc, qd = qk_list[j]
                par = j % 2
                if j >= 2 or s >= 1:
                    b.wait("gpsimd", t_qk)
                wdst = wqkt[:, par * 1024:par * 1024 + 1024].rearrange(
                    "p (fc j) -> p fc j", j=128)
                b.dma("gpsimd", wdst, wr[:, :, qd * 128:(qd + 1) * 128],
                      g_qk[par])

            qk_load(0)
            qk_load(1)
            for j, (li, wr, outc, qd) in enumerate(qk_list):
                par = j % 2
                b.wait("tensor", g_qk[par])
                b.wait("tensor", v_ln)
                for ci, (q0, qw) in enumerate(qchunks(qlen)):
                    ppar = ci % 2
                    pq = ps[:, ppar * TCW:ppar * TCW + TCW]
                    b.wait("tensor", v_qk)
                    for fc in range(NFC):
                        b.mm(pq[:, 0:qw],
                             wqkt[:, par * 1024 + fc * 128:
                                  par * 1024 + fc * 128 + 128],
                             bufB[:, Y2C + fc * PL + q0:Y2C + fc * PL + q0 + qw],
                             fc == 0, fc == NFC - 1,
                             inc=(t_qk if fc == NFC - 1 else None))
                    b.wait("vector", t_qk)
                    b.ins("vector", "tensor_copy",
                          bufB[:, outc + qd * PL + q0:outc + qd * PL + q0 + qw],
                          pq[:, 0:qw], inc=v_qk)
                if j + 2 < len(qk_list):
                    qk_load(j + 2)
            if s >= 1:
                b.ins("vector", "memset",
                      strided_cols(bufB, KSC + 1365, PL, NFC), 0.0)
                b.drain("vector")
            b.ins("vector", "memset", vS[64:128, 10 * 1040:11 * 1040], 0.0)
            b.drain("vector")
            b.ins("vector", "memset",
                  strided_cols(vS, 64, 65, NPT * 16), 1.0, inc=v_qk)
            b.drain("vector")
            def vbuf(vc, fc):
                if vc == 0:
                    return wvt[:, fc * TCW:(fc + 1) * TCW]
                bt = Epp if fc < 4 else Osb
                return bt[:, (fc % 4) * TCW:(fc % 4) * TCW + TCW]

            for vc in range(2):
                for fc in range(NFC):
                    b.dma("gpsimd", vbuf(vc, fc),
                          wv_r[:, fc, vc * TCW:(vc + 1) * TCW],
                          g_wv if vc == 0 else g_wv2)
            for vc in range(2):
                b.wait("tensor", g_wv if vc == 0 else g_wv2)
                for pt in range(NPT):
                    sw = min(128, qlen - pt * 128)
                    pv = ps[:, 1024 + (pt % 2) * TCW:1024 + (pt % 2) * TCW + TCW]
                    b.wait("tensor", v_qk)
                    for fc in range(NFC):
                        b.mm(pv[0:sw, :],
                             bufB[:, Y2C + fc * PL + pt * 128:
                                  Y2C + fc * PL + pt * 128 + sw],
                             vbuf(vc, fc),
                             fc == 0, fc == NFC - 1,
                             inc=(t_qk if fc == NFC - 1 else None))
                    b.wait("vector", t_qk)
                    vdst = vS[0:sw, pt_col(pt) + 65 * 8 * vc:
                              pt_col(pt) + 65 * 8 * vc + 520].rearrange(
                        "p (t j) -> p t j", j=65)[:, :, 0:64]
                    b.ins("vector", "tensor_copy", vdst,
                          pv[0:sw, :].rearrange("p (t j) -> p t j", j=64),
                          inc=v_qk)
            b.wait("tensor", v_qk)
            if s + 1 < 3:
                b.wait("vector", t_qk)
                rec_ln_apply(m2rep, rsrep, ascr, SEG[s + 1], QLEN[s + 1],
                             Y2C, PL)
            strips = [(0, 1024), (1024, qlen - 1024)]
            pending = []
            for h in range(NH):
                ch = h // 2
                poff = 64 * (h % 2)
                opar = h % 2
                osb = Osb[0:65, opar * PL:opar * PL + PL]
                for si_, (q0, qw) in enumerate(strips):
                    if si_ == 1 and pending:
                        pending.pop(0)()
                    s_ex0 = s_ex.v
                    AVp = ps[0:65, 2048:2048 + qw]

                    def rec_scores(i, q0=q0, qw=qw, s_ex0=s_ex0):
                        pw = 128 if i < NPT - 1 else 86
                        spar = i % 2
                        b.wait("tensor", s_ex, s_ex0 + max(0, i - 1))
                        Sp = ps[:, spar * 1024:spar * 1024 + 1024]
                        lo = max(q0, 128 * i - 2)
                        hi = min(q0 + qw, 128 * i + 130)
                        bp = []
                        c = lo
                        while c < hi:
                            nxt = min(hi, q0 + ((c - q0) // 512 + 1) * 512)
                            bp.append((c, nxt))
                            c = nxt
                        bbank = set((c0 - q0) // 512 for c0, _ in bp)
                        mms = []
                        for lq0, lqw in qchunks(qw):
                            mms.append((
                                Sp[0:pw, lq0:lq0 + lqw],
                                bufB[poff:poff + 64,
                                     KSC + ch * PL + 128 * i:
                                     KSC + ch * PL + 128 * i + pw],
                                bufB[poff:poff + 64,
                                     QSC + ch * PL + q0 + lq0:
                                     QSC + ch * PL + q0 + lq0 + lqw],
                                True, (lq0 // 512) not in bbank))
                        for c0, c1 in bp:
                            mms.append((
                                Sp[0:pw, c0 - q0:c1 - q0], cident[0:pw, 0:pw],
                                cband[0:pw, c0 - (128 * i - 2):
                                      c1 - (128 * i - 2)],
                                False, True))
                        for mj, (o_, l_, r_, st, sp) in enumerate(mms):
                            b.mm(o_, l_, r_, st, sp,
                                 inc=(t_sc if mj == len(mms) - 1 else None))

                    def rec_exp(i, qw=qw):
                        pw = 128 if i < NPT - 1 else 86
                        spar = i % 2
                        b.wait("scalar", t_av)
                        b.wait("scalar", t_sc)
                        b.ins("scalar", "activation",
                              Epp[0:pw, (i % 2) * PL:(i % 2) * PL + qw],
                              ps[0:pw, spar * 1024:spar * 1024 + qw], AF.Exp,
                              scale=scal[0:pw, SC_Q8:SC_Q8 + 1], inc=s_ex)

                    def rec_av(i, q0=q0, qw=qw, s_ex0=s_ex0, AVp=AVp):
                        pw = 128 if i < NPT - 1 else 86
                        b.wait("tensor", s_ex, s_ex0 + i + 1)
                        mms = []
                        if i == 0:
                            b.wait("tensor", v_oe)
                        for lq0, lqw in qchunks(qw):
                            mms.append((
                                AVp[:, lq0:lq0 + lqw],
                                vS[0:pw, pt_col(i) + 65 * h:
                                   pt_col(i) + 65 * h + 65],
                                Epp[0:pw, (i % 2) * PL + lq0:
                                    (i % 2) * PL + lq0 + lqw],
                                i == 0, i == NPT - 1))
                        for mj, (o_, l_, r_, st, sp) in enumerate(mms):
                            b.mm(o_, l_, r_, st, sp,
                                 inc=(t_av if mj == len(mms) - 1 else None))

                    for i in range(NPT):
                        rec_scores(i)
                        rec_exp(i)
                        if i >= 1:
                            rec_av(i - 1)
                    rec_av(NPT - 1)
                    b.wait("scalar", t_av)
                    b.wait("scalar", v_xa)
                    b.wait("scalar", v_rc)
                    b.ins("scalar", "copy", osb[:, q0:q0 + qw],
                          AVp[:, 0:qw], inc=v_oe)
                    b.drain("scalar")
                b.wait("vector", y_cn)
                b.wait("vector", v_oe)
                b.ins("vector", "tensor_tensor", t2b[64:65, 0:qlen],
                      osb[64:65, 0:qlen], cv64[64:65, 0:qlen], ALU.add)
                b.drain("vector")
                b.wait("vector", g_row)
                b.ins("vector", "reciprocal", rows[64:65, 0:qlen],
                      t2b[64:65, 0:qlen], inc=v_rc)
                b.wait("sync", v_rc)
                b.dma("sync", d["rows"][2:3, 0:qlen], rows[64:65, 0:qlen],
                      g_row)
                b.wait("sync", g_row)
                b.wait("sync", v_rc)
                b.dma("sync", arr[0:64, 0:qlen],
                      row_bcast_ap(2, 0, 64, qlen), y_rr)

                voe_t = v_oe.v

                def chain_tail(h=h, ch=ch, poff=poff, osb=osb, qlen=qlen,
                               seg=seg):
                    b.wait("vector", v_oe, voe_t)
                    b.wait("vector", y_rr)
                    b.wait("vector", g_xa)
                    b.ins("vector", "tensor_tensor", t2b[0:64, 0:qlen],
                          osb[0:64, 0:qlen], arr[0:64, 0:qlen], ALU.mult)
                    b.drain("vector")
                    xsl = x_ch(ch)[poff:poff + 64, seg:seg + qlen]
                    if poff == 0:
                        b.ins("vector", "tensor_tensor", xsl, xsl,
                              t2b[0:64, 0:qlen], ALU.add, inc=v_xa)
                        b.drain("vector")
                    else:
                        b.mark("vector", v_rc)
                        b.wait("gpsimd", v_rc)
                        b.ins("gpsimd", "dma_start", out=xsl,
                              in_=t2b[0:64, 0:qlen], accum_op=ALU.add,
                              inc=g_xa, incn=16)
                pending.append(chain_tail)
                if s >= 1 and h in (3, 7, 11):
                    k3 = (3, 7, 11).index(h)
                    sseg, sql = SEG[s - 1], QLEN[s - 1]
                    c0s, cws = qchunks(sql)[k3]
                    rec_stats_chunk(sqpp, sseg + c0s, cws)
            while pending:
                pending.pop(0)()
            b.wait("gpsimd", g_xa)
            if s == 2:
                b.wait("tensor", g_xa)
                b.wait("vector", v_xa)
                for c0s, cws in qchunks(qlen):
                    rec_stats_chunk(sqpp, seg + c0s, cws)
            b.barrier(bar)
        as_.close()

    rec_ffn(False)
    rec_attn()
    rec_ffn(True)
    for s4 in y_o:
        b.wait("sync", s4)
    b.barrier(bar)

    b.emit()
    stk.close()
    return nc


_PERM = np.concatenate([np.arange(s, NT, 3) for s in range(3)])
_IPERM = np.argsort(_PERM)


def _consts():
    bf = ml_dtypes.bfloat16
    band = np.zeros((128, BANDW), np.float32)
    for p in range(128):
        band[p, p:p + 5] = 8.0 * np.log(2.0)
    cv = np.zeros((1, PL), np.float32)
    cv[0, 0], cv[0, 1], cv[0, PL - 2], cv[0, PL - 1] = 2, 1, 1, 2

    return {
        "cband": band.astype(bf),
        "cident": np.eye(128, dtype=np.float32).astype(bf),
        "cvec": cv.astype(bf),
                "cones": np.ones((128, 1), np.float32).astype(bf),
    }


def kernel(L0_attn_x, L0_attn_W_q, L0_attn_W_k, L0_attn_W_v,
           L0_ffn_clip_x, L0_ffn_clip_down,
           L0_ffn_w_gate, L0_ffn_w_up, L0_ffn_w_down):
    nc = build_nc()
    x = np.asarray(L0_attn_x, dtype=np.float32)
    shared = {
        "wqp": np.ascontiguousarray(np.asarray(L0_attn_W_q, np.float32)),
        "wkp": np.ascontiguousarray(np.asarray(L0_attn_W_k, np.float32)),
        "wvp": np.ascontiguousarray(np.asarray(L0_attn_W_v, np.float32)),
        "wg_row": np.ascontiguousarray(np.asarray(L0_ffn_w_gate, np.float32)),
        "wu_row": np.ascontiguousarray(np.asarray(L0_ffn_w_up, np.float32)),
        "wd_row": np.ascontiguousarray(np.asarray(L0_ffn_w_down, np.float32)),
        "clips": np.array([[float(np.asarray(L0_ffn_clip_x).reshape(-1)[0]),
                            float(np.asarray(L0_ffn_clip_down).reshape(-1)[0])]],
                          np.float32),
    }
    shared.update(_consts())
    in_maps = []
    for bi in range(8):
        m = dict(shared)
        m["xT"] = np.ascontiguousarray(x[bi][_PERM].T)
        in_maps.append(m)
    res = run_bass_kernel_spmd(nc, in_maps, core_ids=list(range(8)))
    out = np.empty((8, NT, D), np.float32)
    for bi in range(8):
        out[bi] = res.results[bi]["outT"].T[_IPERM]
    return out


# version 31
# speedup vs baseline: 1.0058x; 1.0058x over previous
import numpy as np
import ml_dtypes
import concourse.bass as bass
import concourse.mybir as mybir
from concourse.bass_utils import run_bass_kernel_spmd
from contextlib import ExitStack

F32 = mybir.dt.float32
BF16 = mybir.dt.bfloat16
AF = mybir.ActivationFunctionType
ALU = mybir.AluOpType
AXX = mybir.AxisListType.X

D = 1024
NFC = 8
NT = 4096
TCW = 512
NTC = NT // TCW
II = 2816
NIC = II // 128
NH = 16
PL = 1366
NPT = 11
QLEN = (1366, 1365, 1365)
SEG = (0, 1366, 2731)
EPS = 1e-5
QMAXF = 127.0
BANDW = 136
ENGINES = ("tensor", "vector", "scalar", "gpsimd", "sync")


def qchunks(n):
    out, c = [], 0
    while c < n:
        out.append((c, min(512, n - c)))
        c += 512
    return out


class Sem:
    __slots__ = ("h", "v", "name")

    def __init__(self, h, name):
        self.h, self.v, self.name = h, 0, name


class B:
    def __init__(self, nc, stk):
        self.nc = nc
        self.stk = stk
        self.q = {k: [] for k in ENGINES}

    def sem(self, name):
        return Sem(self.stk.enter_context(self.nc.semaphore(name)), name)

    def ins(self, eng, meth, *args, inc=None, incn=1, **kw):
        self.q[eng].append((meth, args, kw, inc.h if inc else None, incn))
        if inc is not None:
            inc.v += incn

    def wait(self, eng, sem, val=None):
        v = sem.v if val is None else val
        assert v <= sem.v, f"wait beyond recorded increments: {sem.name} {v}>{sem.v}"
        self.q[eng].append(("wait_ge", (sem.h, v), {}, None, 0))

    def dma(self, eng, out, in_, sem):
        self.ins(eng, "dma_start", inc=sem, incn=16, out=out, in_=in_)

    def mm(self, out, lhsT, rhs, start, stop, inc=None):
        self.ins("tensor", "matmul", out, lhsT, rhs,
                 start=start, stop=stop, inc=inc)

    def drain(self, eng):
        self.q[eng].append(("drain", (), {}, None, 0))

    def mark(self, eng, sem):
        self.drain(eng)
        self.ins(eng, "nop", inc=sem)

    def barrier(self, sem):
        for eng in ENGINES:
            self.mark(eng, sem)
        tgt = sem.v
        for eng in ENGINES:
            self.wait(eng, sem, tgt)

    def emit(self):
        with self.nc.Block() as blk:
            for eng in ENGINES:
                getattr(blk, eng)(self._runner(eng))

    def _runner(self, eng):
        entries = self.q[eng]

        def run(e):
            for meth, args, kw, inch, incn in entries:
                r = getattr(e, meth)(*args, **kw)
                if inch is not None:
                    r.then_inc(inch, incn)
        return run


def strided_cols(t, start, step, count, width=1, nrows=128, row0=0):
    ap = t[row0:row0 + nrows, start:start + width]
    return bass.AP(ap.tensor, ap.offset, [ap.ap[0], [step, count], [1, width]])


def build_nc(ncores=8):
    nc = bass.Bass("TRN2", target_bir_lowering=False)
    d = {}
    for nm, shp, dt, kind in (
            ("xT", [D, NT], F32, "ExternalInput"),
            ("wqp", [D, D], F32, "ExternalInput"),
            ("wkp", [D, D], F32, "ExternalInput"),
            ("wvp", [D, D], F32, "ExternalInput"),
            ("wg_row", [II, D], F32, "ExternalInput"),
            ("wu_row", [II, D], F32, "ExternalInput"),
            ("wd_row", [D, II], F32, "ExternalInput"),
            ("clips", [1, 2], F32, "ExternalInput"),
            ("cband", [128, BANDW], BF16, "ExternalInput"),
            ("cident", [128, 128], BF16, "ExternalInput"),
            ("cvec", [1, PL], BF16, "ExternalInput"),
            ("cones", [128, 1], BF16, "ExternalInput"),
            ("outT", [D, NT], F32, "ExternalOutput"),
            ("mqg", [D, II], BF16, "Internal"),
            ("mqu", [D, II], BF16, "Internal"),
            ("mqgR", [II, D], BF16, "Internal"),
            ("mquR", [II, D], BF16, "Internal"),
            ("mqdR", [D, II], BF16, "Internal"),
            ("mqdC", [NFC, 128, NIC * 128], BF16, "Internal"),
            ("hscr", [II, NT], BF16, "Internal"),
            ("cc_in", [1, 128], F32, "Internal"),
            ("cc_out", [1, 128], F32, "Internal"),
            ("rows", [5, NT], F32, "Internal")):
        d[nm] = nc.dram_tensor(nm, shp, dt, kind=kind).ap()

    stk = ExitStack()
    b = B(nc, stk)

    g_x = b.sem("g_x")
    g_wr = (b.sem("g_wra"), b.sem("g_wrb"))
    g_wt2 = (b.sem("g_wta"), b.sem("g_wtb"))
    g_tp = (b.sem("g_tp0"), b.sem("g_tp1"))
    g_dw = b.sem("g_dw")
    g_dt = (b.sem("g_dt0"), b.sem("g_dt1"))
    t_cc = b.sem("t_cc")
    g_cc = b.sem("g_cc")
    g_hs = (b.sem("g_hsa"), b.sem("g_hsb"))
    g_hr3 = (b.sem("g_hr0"), b.sem("g_hr1"), b.sem("g_hr2"))
    g_row = b.sem("g_row")
    g_qk = (b.sem("g_qka"), b.sem("g_qkb"))
    g_wv = b.sem("g_wv")
    g_wv2 = b.sem("g_wv2")
    y_w = (b.sem("y_wa"), b.sem("y_wb"))
    y_o = (b.sem("y_oa"), b.sem("y_ob"), b.sem("y_oc"))
    y_cn = b.sem("y_cn")
    y_la = (b.sem("y_laa"), b.sem("y_lab"))
    y_sc = b.sem("y_sc")
    y_rr = b.sem("y_rr")
    y_m = (b.sem("y_ma"), b.sem("y_mb"))
    c_cc = b.sem("c_cc")
    t_st = b.sem("t_st")
    v_st = b.sem("v_st")
    s_sq = b.sem("s_sq")
    t_squ = b.sem("t_squ")
    v_ln = b.sem("v_ln")
    t_gu = b.sem("t_gu")
    t_uu = b.sem("t_uu")
    s_si = b.sem("s_si")
    v_h = b.sem("v_h")
    v_mx = b.sem("v_mx")
    t_wfree = b.sem("t_wfree")
    t_dn = b.sem("t_dn")
    v_dn = b.sem("v_dn")
    v_hq = b.sem("v_hq")
    v_qa = b.sem("v_qa")
    s_cv = b.sem("s_cv")
    t_qk = b.sem("t_qk")
    v_qk = b.sem("v_qk")
    t_sc = b.sem("t_sc")
    s_ex = b.sem("s_ex")
    t_av = b.sem("t_av")
    v_rc = b.sem("v_rc")
    v_oe = b.sem("v_oe")
    v_xa = b.sem("v_xa")
    g_xa = b.sem("g_xa")
    v_wr = b.sem("v_wr")
    v_wq = b.sem("v_wq")
    g_wb = b.sem("g_wb")
    bar = b.sem("bar")

    xs = stk.enter_context(nc.sbuf_tensor("xs", [128, NFC * NT], BF16))
    bufB = stk.enter_context(nc.sbuf_tensor("bufB", [128, 3 * NFC * PL], BF16))
    rows = stk.enter_context(nc.sbuf_tensor("rows_sb", [128, 1680], F32))
    scal = stk.enter_context(nc.sbuf_tensor("scal", [128, 160], F32))
    cband = stk.enter_context(nc.sbuf_tensor("cband_sb", [128, BANDW], BF16))
    cident = stk.enter_context(nc.sbuf_tensor("cident_sb", [128, 128], BF16))
    cones = stk.enter_context(nc.sbuf_tensor("cones_sb", [128, 1], BF16))
    ps = stk.enter_context(nc.psum_tensor("ps", [128, 4096], F32))

    def x_ch(fc):
        return xs[:, fc * NT:(fc + 1) * NT]

    SC_RX, SC_SX, SC_RH, SC_SH, SC_HMX, SC_TMP = 0, 1, 2, 3, 4, 5
    SC_WG, SC_WU, SC_WD = 8, 30, 52
    SC_SVG, SC_SVU, SC_SVD = 60, 82, 104
    SC_RTMP = 112
    SC_Q8 = 135
    SC_HALF = 7

    b.dma("sync", cband[:, :], d["cband"][:, :], y_cn)
    b.dma("sync", cident[:, :], d["cident"][:, :], y_cn)
    b.dma("sync", cones[:, :], d["cones"][:, :], y_cn)
    b.dma("sync", rows[0:1, 1540:1542], d["clips"][:, :], y_cn)
    for fc in range(NFC):
        b.dma("gpsimd", x_ch(fc), d["xT"][fc * 128:(fc + 1) * 128, :], g_x)

    b.ins("vector", "memset", scal[:, SC_Q8:SC_Q8 + 1], 0.125)
    b.drain("vector")
    b.ins("vector", "memset", scal[:, SC_HALF:SC_HALF + 1], 0.5)
    b.drain("vector")
    b.wait("tensor", y_cn)
    b.wait("scalar", y_cn)

    def row_bcast_ap(r, col, nrows, width):
        return bass.AP(d["rows"].tensor, d["rows"].offset + r * NT + col,
                       [[0, nrows], [1, width]])

    def rec_weight_quant(wfscr, wq2, m2rep, fint, fint2, mdc, hstg,
                         hooks, down_out=None):
        chunks = []
        for wi, spec in enumerate((("wg_row", "mqgR", NIC, D, SC_WG),
                                   ("wu_row", "mquR", NIC, D, SC_WU),
                                   ("wd_row", "mqdR", NFC, II, SC_WD))):
            for ic in range(spec[2]):
                chunks.append((wi,) + spec + (ic,))
        NCH = len(chunks)
        tasks = []

        def load(K):
            wi, rowt, outR, nch, wdt, scol, ic = chunks[K]
            par = K % 2
            if wdt > 1024:
                half = wdt // 2
                for eng in ("sync", "gpsimd"):
                    b.wait(eng, v_wq)
                    b.wait(eng, s_cv)
                    b.wait(eng, v_qa)
                b.dma("sync", wfscr[:, 0:half],
                      d[rowt][ic * 128:(ic + 1) * 128, 0:half], g_wt2[0])
                b.dma("gpsimd", wfscr[:, half:wdt],
                      d[rowt][ic * 128:(ic + 1) * 128, half:wdt], g_dw)
                return
            wrow = wfscr[:, 0:wdt] if par == 0 else wq2[:, 0:wdt]
            eng = "sync" if par == 0 else "gpsimd"
            if K >= 2:
                b.wait(eng, v_wq)
                b.wait(eng, s_cv)
            else:
                b.wait(eng, v_mx)
            b.dma(eng, wrow, d[rowt][ic * 128:(ic + 1) * 128, :],
                  g_wt2[par])

        def compute(K):
            wi, rowt, outR, nch, wdt, scol, ic = chunks[K]
            par = K % 2
            wide = wdt > 1024
            wrow = wfscr[:, 0:wdt] if (par == 0 or wide) else wq2[:, 0:wdt]
            qsl = mdc[:, par * II:par * II + wdt]
            rt = scal[:, SC_RTMP + par:SC_RTMP + par + 1]
            sc = scal[:, scol + ic:scol + ic + 1]
            if wide:
                b.wait("vector", g_wt2[0])
                b.wait("vector", g_dw)
            else:
                b.wait("vector", g_wt2[par])
            b.ins("vector", "tensor_reduce", sc, wrow, AXX, ALU.max,
                  apply_absolute_value=True)
            b.drain("vector")
            b.ins("vector", "tensor_single_scalar", rt, sc, 1.0 / QMAXF,
                  ALU.mult)
            b.drain("vector")
            b.ins("vector", "reciprocal", rt, rt, inc=v_wr)
            b.drain("vector")
            b.wait("vector", y_m[par])
            b.wait("scalar", v_wr)
            nsub = (wdt + 511) // 512
            qa0, sc0 = v_qa.v, s_cv.v
            for si in range(nsub):
                c0 = si * 512
                cw = min(512, wdt - c0)
                sp = si % 2
                u = m2rep[:, sp * 512:sp * 512 + cw]
                ti = (fint if sp == 0 else fint2)[:, 0:cw]
                if si >= 2:
                    b.wait("scalar", v_qa, qa0 + si - 1)
                b.ins("scalar", "activation", u, wrow[:, c0:c0 + cw],
                      AF.Identity, scale=rt,
                      bias=scal[:, SC_HALF:SC_HALF + 1])
                b.drain("scalar")
                b.ins("scalar", "copy", ti, u, inc=s_cv)
                b.drain("scalar")
                b.wait("vector", s_cv, sc0 + si + 1)
                b.ins("vector", "tensor_tensor", u, ti, u, ALU.is_gt)
                b.drain("vector")
                b.ins("vector", "tensor_tensor", qsl[:, c0:c0 + cw], ti, u,
                      ALU.subtract,
                      inc=(v_wq if si == nsub - 1 else v_qa))
                if si == nsub - 1:
                    b.mark("vector", v_qa)
            b.wait("sync", v_wq)
            b.dma("sync", d[outR][ic * 128:(ic + 1) * 128, :], qsl, y_m[par])
            if wi == 0 and ic == nch - 1:
                b.wait("sync", y_m[0])
                b.wait("sync", y_m[1])
                tasks.extend((outR, "mqg", fc) for fc in range(NFC))

        def transpose_task():
            outR, outT, fc = tasks.pop(0)
            tt = hstg[:, (fc % 2) * II:(fc % 2) * II + II]
            b.wait("sync", g_hs[fc % 2])
            b.ins("sync", "dma_start", out=tt,
                  in_=d[outR][:, fc * 128:(fc + 1) * 128],
                  transpose=True, inc=g_tp[fc % 2], incn=16)
            b.wait("gpsimd", g_tp[fc % 2])
            b.dma("gpsimd", d[outT][fc * 128:(fc + 1) * 128, :], tt,
                  g_hs[fc % 2])

        if down_out is not None:
            down_out["load"] = lambda k: load(44 + k)
            down_out["compute"] = lambda k: compute(44 + k)
            NCH = 44
        b.wait("sync", v_mx)
        load(0)
        load(1)
        for K in range(NCH):
            if chunks[K][4] > 1024:
                load(K)
            compute(K)
            if tasks:
                transpose_task()
            if K == 43:
                b.wait("sync", y_m[0])
                b.wait("sync", y_m[1])
                tasks.extend(("mquR", "mqu", fc) for fc in range(NFC))
                while tasks:
                    transpose_task()
            for cb in hooks.get(K, ()):
                cb()
            if K + 2 < NCH and chunks[K + 2][4] <= 1024:
                load(K + 2)
        assert not tasks
        for e in ("sync", "vector"):
            b.wait(e, g_hs[0])
            b.wait(e, g_hs[1])

    def rec_down_transpose_one(mdc, ic2):
        tt = mdc[:, (ic2 % 2) * II:(ic2 % 2) * II + D]
        if ic2 < 2:
            b.wait("sync", y_m[0])
            b.wait("sync", y_m[1])
        else:
            b.wait("sync", g_dt[ic2 % 2])
        b.ins("sync", "dma_start", out=tt,
              in_=d["mqdR"][:, ic2 * 128:(ic2 + 1) * 128],
              transpose=True, inc=g_tp[ic2 % 2], incn=16)
        b.wait("gpsimd", g_tp[ic2 % 2])
        dst = bass.AP(d["mqdC"].tensor, d["mqdC"].offset + ic2 * 128,
                      [[II, 128], [128 * II, NFC], [1, 128]])
        b.dma("gpsimd", dst, tt.rearrange("p (fco j) -> p fco j", j=128),
              g_dt[ic2 % 2])

    def rec_ln_stats(sqpp):
        v_st0 = v_st.v
        for tc in range(NTC):
            su = ps[0:1, (tc % 2) * TCW:(tc % 2) * TCW + TCW]
            sq = ps[0:1, 1024 + (tc % 2) * TCW:1024 + (tc % 2) * TCW + TCW]
            if tc >= 2:
                b.wait("tensor", v_st, v_st0 + tc - 1)
            for fc in range(NFC):
                xsl = x_ch(fc)[:, tc * TCW:(tc + 1) * TCW]
                sqs = sqpp[:, (fc % 2) * TCW:(fc % 2) * TCW + TCW]
                if fc >= 2 or tc > 0:
                    b.wait("scalar", t_squ)
                b.ins("scalar", "activation", sqs, xsl, AF.Square, inc=s_sq)
                b.wait("tensor", s_sq)
                if fc < NFC - 1:
                    b.mm(su, cones[:, :], xsl, fc == 0, False)
                    b.mm(sq, cones[:, :], sqs, fc == 0, False, inc=t_squ)
                else:
                    b.mm(sq, cones[:, :], sqs, False, True, inc=t_squ)
                    b.mm(su, cones[:, :], xsl, False, True, inc=t_st)
            b.wait("vector", t_st)
            b.wait("vector", g_row)
            r0 = rows[0:1, 0:TCW]
            r1 = rows[0:1, TCW:2 * TCW]
            r2 = rows[0:1, 2 * TCW:3 * TCW]
            b.ins("vector", "tensor_single_scalar", r0, su, 1.0 / D, ALU.mult)
            b.ins("vector", "tensor_single_scalar", r2, sq, 1.0 / D, ALU.mult)
            b.drain("vector")
            b.ins("vector", "tensor_tensor", r1, r0, r0, ALU.mult)
            b.drain("vector")
            b.ins("vector", "tensor_tensor", r2, r2, r1, ALU.subtract)
            b.drain("vector")
            b.ins("vector", "tensor_single_scalar", r2, r2, EPS, ALU.add,
                  inc=v_ln)
            b.drain("vector")
            b.wait("scalar", v_ln)
            b.ins("scalar", "activation", r2, r2, AF.Sqrt, inc=s_sq)
            b.wait("vector", s_sq)
            b.ins("vector", "reciprocal", r1, r2)
            b.drain("vector")
            b.ins("vector", "tensor_tensor", r0, r0, r1, ALU.mult, inc=v_st)
            b.drain("vector")
            b.wait("sync", v_st)
            b.dma("sync", d["rows"][0:1, tc * TCW:(tc + 1) * TCW], r0, g_row)
            b.dma("sync", d["rows"][1:2, tc * TCW:(tc + 1) * TCW], r1, g_row)
        b.wait("sync", g_row)

    def rec_stats_chunk(sqpp, c0, cw):
        su = ps[0:1, 3072:3072 + cw]
        sq = ps[0:1, 3584:3584 + cw]
        s_sq0, t_squ0 = s_sq.v, t_squ.v
        b.wait("tensor", v_st)

        def sqr(fc):
            sqs = sqpp[:, (fc % 2) * TCW:(fc % 2) * TCW + cw]
            b.wait("vector", t_squ,
                   t_squ0 + max(0, fc - 1) if fc >= 2 else t_squ0)
            b.ins("vector", "tensor_tensor", sqs,
                  x_ch(fc)[:, c0:c0 + cw], x_ch(fc)[:, c0:c0 + cw],
                  ALU.mult, inc=s_sq)
            b.drain("vector")

        sqr(0)
        sqr(1)
        for fc in range(NFC):
            xsl = x_ch(fc)[:, c0:c0 + cw]
            sqs = sqpp[:, (fc % 2) * TCW:(fc % 2) * TCW + cw]
            b.wait("tensor", s_sq, s_sq0 + fc + 1)
            if fc < NFC - 1:
                b.mm(su, cones[:, :], xsl, fc == 0, False)
                b.mm(sq, cones[:, :], sqs, fc == 0, False, inc=t_squ)
            else:
                b.mm(sq, cones[:, :], sqs, False, True, inc=t_squ)
                b.mm(su, cones[:, :], xsl, False, True, inc=t_st)
            if fc + 2 < NFC:
                sqr(fc + 2)
        b.wait("vector", t_st)
        b.wait("vector", g_row)
        r0 = rows[0:1, 0:TCW]
        r1 = rows[0:1, TCW:2 * TCW]
        r2 = rows[0:1, 2 * TCW:3 * TCW]
        b.ins("vector", "tensor_single_scalar", r0[:, 0:cw], su, 1.0 / D,
              ALU.mult)
        b.ins("vector", "tensor_single_scalar", r2[:, 0:cw], sq, 1.0 / D,
              ALU.mult)
        b.drain("vector")
        b.ins("vector", "tensor_tensor", r1[:, 0:cw], r0[:, 0:cw],
              r0[:, 0:cw], ALU.mult)
        b.drain("vector")
        b.ins("vector", "tensor_tensor", r2[:, 0:cw], r2[:, 0:cw],
              r1[:, 0:cw], ALU.subtract)
        b.drain("vector")
        b.ins("vector", "tensor_single_scalar", r2[:, 0:cw], r2[:, 0:cw],
              EPS, ALU.add, inc=v_ln)
        b.drain("vector")
        b.wait("scalar", v_ln)
        b.ins("scalar", "activation", r2[:, 0:cw], r2[:, 0:cw], AF.Sqrt,
              inc=s_sq)
        b.wait("vector", s_sq)
        b.ins("vector", "reciprocal", r1[:, 0:cw], r2[:, 0:cw])
        b.drain("vector")
        b.ins("vector", "tensor_tensor", r0[:, 0:cw], r0[:, 0:cw],
              r1[:, 0:cw], ALU.mult, inc=v_st)
        b.drain("vector")
        b.wait("sync", v_st)
        b.dma("sync", d["rows"][3:4, c0:c0 + cw], r0[:, 0:cw], g_row)
        b.dma("sync", d["rows"][4:5, c0:c0 + cw], r1[:, 0:cw], g_row)

    def rec_ln_apply(m2rep, rsrep, scr, seg0, width, ycol, ystride,
                     rm=0, rr=1):
        b.wait("sync", g_row)
        chs = qchunks(width)

        def rload(t):
            c0r, cw = chs[t]
            p = t % 2
            b.wait("sync", v_ln)
            b.dma("sync", m2rep[:, p * TCW:p * TCW + cw],
                  row_bcast_ap(rm, seg0 + c0r, 128, cw), y_la[p])
            b.dma("sync", rsrep[:, p * TCW:p * TCW + cw],
                  row_bcast_ap(rr, seg0 + c0r, 128, cw), y_la[p])

        rload(0)
        if len(chs) > 1:
            rload(1)
        for t, (c0r, cw) in enumerate(chs):
            c0 = seg0 + c0r
            p = t % 2
            b.wait("vector", y_la[p])
            for fc in range(NFC):
                xsl = x_ch(fc)[:, c0:c0 + cw]
                ysl = bufB[:, ycol + fc * ystride + c0r:
                           ycol + fc * ystride + c0r + cw]
                b.ins("vector", "tensor_tensor", scr[:, 0:cw], xsl,
                      rsrep[:, p * TCW:p * TCW + cw], ALU.mult)
                b.drain("vector")
                b.ins("vector", "tensor_tensor", ysl, scr[:, 0:cw],
                      m2rep[:, p * TCW:p * TCW + cw], ALU.subtract,
                      inc=(v_ln if fc == NFC - 1 else None))
                b.drain("vector")
            if t + 2 < len(chs):
                rload(t + 2)

    def rec_allreduce_issue(mx_col):
        b.wait("gpsimd", v_mx)
        b.dma("gpsimd", d["cc_in"][:, :], scal[:, mx_col:mx_col + 1], g_cc)
        b.wait("gpsimd", g_cc)
        b.ins("gpsimd", "collective_compute", "AllReduce", ALU.max,
              [list(range(ncores))], [d["cc_in"][:, :]],
              [d["cc_out"][:, :]], inc=c_cc)

    def rec_allreduce_finish(rcol, scol, clip_col):
        b.wait("scalar", c_cc)
        b.dma("scalar", rows[0:1, 1552:1680], d["cc_out"][:, :], t_cc)
        b.wait("vector", t_cc)
        b.ins("vector", "tensor_reduce", rows[0:1, 1544:1545],
              rows[0:1, 1552:1680], AXX, ALU.max, inc=v_ln)
        b.wait("scalar", v_ln)
        b.wait("scalar", y_cn, 64)
        b.ins("scalar", "activation", rows[0:1, 1545:1546],
              rows[0:1, 1540 + clip_col:1541 + clip_col], AF.Sigmoid, inc=s_sq)
        b.wait("vector", s_sq)
        b.ins("vector", "tensor_tensor", rows[0:1, 1546:1547],
              rows[0:1, 1544:1545], rows[0:1, 1545:1546], ALU.mult)
        b.drain("vector")
        b.ins("vector", "tensor_single_scalar", rows[0:1, 1546:1547],
              rows[0:1, 1546:1547], 1.0 / QMAXF, ALU.mult)
        b.drain("vector")
        b.ins("vector", "reciprocal", rows[0:1, 1547:1548],
              rows[0:1, 1546:1547], inc=v_ln)
        b.wait("scalar", v_ln)
        b.dma("scalar", d["rows"][4:5, 129:131], rows[0:1, 1546:1548], t_cc)
        b.wait("scalar", t_cc)
        b.dma("scalar", scal[:, scol:scol + 1],
              row_bcast_ap(4, 129, 128, 1), y_sc)
        b.dma("scalar", scal[:, rcol:rcol + 1],
              row_bcast_ap(4, 130, 128, 1), y_sc)
        b.wait("vector", y_sc)
        b.wait("scalar", y_sc)

    def rec_allreduce_scale(mx_col, rcol, scol, clip_col):
        rec_allreduce_issue(mx_col)
        rec_allreduce_finish(rcol, scol, clip_col)

    def rec_quant_inplace(cols, rcol, wfscr, wint, wint2, sem):
        subs = []
        for bt, col0, width in cols:
            for c0 in range(0, width, 1024):
                subs.append((bt, col0 + c0, min(1024, width - c0), False))
        subs[-1] = subs[-1][:3] + (True,)
        v_qa0, s_cv0 = v_qa.v, s_cv.v

        def s_part(k):
            bt, c0, cw, last = subs[k]
            par = k % 2
            wc = wfscr[:, par * 1024:par * 1024 + cw]
            ti = (wint if par == 0 else wint2)[:, 0:cw]
            b.wait("scalar", v_qa, v_qa0 + max(0, k - 1))
            b.ins("scalar", "activation", wc, bt[:, c0:c0 + cw], AF.Identity,
                  scale=scal[:, rcol:rcol + 1],
                  bias=scal[:, SC_HALF:SC_HALF + 1])
            b.drain("scalar")
            b.ins("scalar", "copy", ti, wc, inc=s_cv)
            b.drain("scalar")

        def v_part2(k):
            bt, c0, cw, last = subs[k]
            par = k % 2
            wc = wfscr[:, par * 1024:par * 1024 + cw]
            ti = (wint if par == 0 else wint2)[:, 0:cw]
            b.wait("vector", s_cv, s_cv0 + k + 1)
            b.ins("vector", "tensor_tensor", wc, ti, wc, ALU.is_gt)
            b.drain("vector")
            b.ins("vector", "tensor_tensor", bt[:, c0:c0 + cw], ti, wc,
                  ALU.subtract)
            b.drain("vector")
            b.ins("vector", "tensor_scalar", bt[:, c0:c0 + cw],
                  bt[:, c0:c0 + cw], QMAXF, -128.0, ALU.min, ALU.max,
                  inc=(sem if last else v_qa))
            if last:
                b.mark("vector", v_qa)
        for k in range(len(subs)):
            s_part(k)
            if k >= 1:
                v_part2(k - 1)
        v_part2(len(subs) - 1)

    def rec_ffn(last):
        tag = '2' if last else '1'
        fs = ExitStack()
        m2rep = fs.enter_context(nc.sbuf_tensor("f_m2" + tag,
                                                [128, 2 * TCW], F32))
        rsrep = fs.enter_context(nc.sbuf_tensor("f_rs" + tag,
                                                [128, 2 * TCW], F32))
        wfscr = fs.enter_context(nc.sbuf_tensor("f_wf" + tag, [128, II], F32))
        fint = fs.enter_context(nc.sbuf_tensor("f_int" + tag, [128, 1024],
                                               mybir.dt.int32))
        hstg = fs.enter_context(nc.sbuf_tensor("f_h" + tag, [128, 2 * NT], BF16))
        silu = fs.enter_context(nc.sbuf_tensor("f_si" + tag, [128, 2 * TCW], BF16))
        sqpp = silu
        mgut = fs.enter_context(nc.sbuf_tensor("f_w" + tag, [128, 2 * 2048], BF16))
        mdc = fs.enter_context(nc.sbuf_tensor("f_wd" + tag, [128, 2 * II], BF16))
        outst = (fs.enter_context(nc.sbuf_tensor("f_o" + tag,
                                                  [128, 3 * TCW], F32))
                 if last else None)
        wq2 = (fs.enter_context(nc.sbuf_tensor("f_wq2", [128, 1024], F32))
               if not last else None)
        fint2 = fs.enter_context(nc.sbuf_tensor("f_int2" + tag, [128, 1024],
                                                mybir.dt.int32))

        b.wait("scalar", g_x)
        b.wait("tensor", g_x)
        b.wait("vector", g_x)
        if not last:
            rec_ln_stats(sqpp)
            rec_ln_apply(m2rep, rsrep, wfscr, 0, NT, 0, NT)
        else:
            rec_ln_apply(m2rep, rsrep, wfscr, 0, NT, 0, NT, rm=3, rr=4)
        b.wait("vector", v_ln)
        b.ins("vector", "tensor_reduce", scal[:, SC_TMP:SC_TMP + 1],
              bufB[:, 0:NFC * NT], AXX, ALU.max, apply_absolute_value=True,
              inc=v_mx)
        b.drain("vector")
        rec_allreduce_issue(SC_TMP)
        st_c = {}

        def rec_xscale():
            rec_allreduce_finish(SC_RX, SC_SX, 0)
            b.ins("vector", "tensor_scalar", scal[:, SC_SVG:SC_SVG + NIC],
                  scal[:, SC_WG:SC_WG + NIC], scal[:, SC_SX:SC_SX + 1],
                  1.0 / QMAXF, ALU.mult, ALU.mult)
            b.drain("vector")

        def rec_xquant():
            b.ins("vector", "tensor_scalar", scal[:, SC_SVU:SC_SVU + NIC],
                  scal[:, SC_WU:SC_WU + NIC], scal[:, SC_SX:SC_SX + 1],
                  1.0 / QMAXF, ALU.mult, ALU.mult)
            b.drain("vector")
            st_c["vlnb"] = v_ln.v
            st_c["xqb"] = v_qa.v
            rec_quant_inplace([(bufB, fc * NT + tq * 1024, 1024)
                               for tq in range(NT // 1024)
                               for fc in range(NFC)],
                              SC_RX, wfscr, fint, fint2, v_ln)
            b.ins("vector", "memset", scal[:, SC_HMX:SC_HMX + 1], 0.0)

        wq_down = {}
        if not last:
            rec_weight_quant(wfscr, wq2, m2rep, fint, fint2, mdc, hstg,
                             {21: (rec_xscale,), 43: (rec_xquant,)},
                             wq_down)
        else:
            rec_xscale()
            rec_xquant()
        vlnb, xqb = st_c["vlnb"], st_c["xqb"]
        b.wait("tensor", v_ln, vlnb)

        t_gu0, t_uu0, s_si0, v_h0 = t_gu.v, t_uu.v, s_si.v, v_h.v
        mg_r = d["mqg"].rearrange("(fc p) i -> p fc i", p=128)
        mu_r = d["mqu"].rearrange("(fc p) i -> p fc i", p=128)
        for ic in range(NIC):
            par = ic % 2
            if ic >= 2:
                b.wait("sync", t_uu, t_uu0 + NTC * (ic - 1))
            wg_dst = mgut[:, par * 2048:par * 2048 + 1024].rearrange(
                "p (fc j) -> p fc j", j=128)
            wu_dst = mgut[:, par * 2048 + 1024:par * 2048 + 2048].rearrange(
                "p (fc j) -> p fc j", j=128)
            b.dma("sync", wg_dst, mg_r[:, :, ic * 128:(ic + 1) * 128],
                  y_w[ic % 2])
            b.dma("sync", wu_dst, mu_r[:, :, ic * 128:(ic + 1) * 128],
                  y_w[ic % 2])
            if not last:
                if ic % 2 == 0 and ic // 2 < NFC:
                    wq_down["load"](ic // 2)
                elif ic % 2 == 1 and ic // 2 < NFC:
                    wq_down["compute"](ic // 2)
                if ic >= 16:
                    for j_ in range(4):
                        t_ = (ic - 16) * 4 + j_
                        if t_ < NIC:
                            rec_down_transpose_one(mdc, t_)
            b.wait("tensor", y_w[ic % 2])
            for tc in range(NTC):
                idx = ic * NTC + tc
                tpar = tc % 4
                pg = ps[:, tpar * TCW:tpar * TCW + TCW]
                pu = ps[:, 2048 + tpar * TCW:2048 + tpar * TCW + TCW]
                if ic == 0:
                    b.wait("tensor", v_qa, xqb + NFC * (tc // 2 + 1))
                if idx >= 4:
                    b.wait("tensor", v_h, v_h0 + idx - 3)
                for fc in range(NFC):
                    b.mm(pg, mgut[:, par * 2048 + fc * 128:par * 2048 + fc * 128 + 128],
                         bufB[:, fc * NT + tc * TCW:fc * NT + tc * TCW + TCW],
                         fc == 0, fc == NFC - 1,
                         inc=(t_gu if fc == NFC - 1 else None))
                for fc in range(NFC):
                    b.mm(pu, mgut[:, par * 2048 + 1024 + fc * 128:
                                  par * 2048 + 1024 + fc * 128 + 128],
                         bufB[:, fc * NT + tc * TCW:fc * NT + tc * TCW + TCW],
                         fc == 0, fc == NFC - 1,
                         inc=(t_uu if fc == NFC - 1 else None))
                b.wait("scalar", t_gu, t_gu0 + idx + 1)
                b.wait("scalar", v_h, v_h0 + max(0, idx - 1))
                b.ins("scalar", "activation",
                      silu[:, (tc % 2) * TCW:(tc % 2) * TCW + TCW],
                      pg, AF.Silu, scale=scal[:, SC_SVG + ic:SC_SVG + ic + 1],
                      inc=s_si)
                b.wait("vector", s_si, s_si0 + idx + 1)
                b.wait("vector", t_uu, t_uu0 + idx + 1)
                if tc == 0 and ic >= 2:
                    b.wait("vector", g_hs[ic % 2])
                b.ins("vector", "scalar_tensor_tensor",
                      hstg[:, par * NT + tc * TCW:par * NT + tc * TCW + TCW],
                      pu, scal[:, SC_SVU + ic:SC_SVU + ic + 1],
                      silu[:, (tc % 2) * TCW:(tc % 2) * TCW + TCW],
                      ALU.mult, ALU.mult, inc=v_h)
            b.drain("vector")
            b.ins("vector", "tensor_reduce", scal[:, SC_TMP:SC_TMP + 1],
                  hstg[:, par * NT:par * NT + NT], AXX, ALU.max,
                  apply_absolute_value=True)
            b.drain("vector")
            b.ins("vector", "tensor_tensor", scal[:, SC_HMX:SC_HMX + 1],
                  scal[:, SC_HMX:SC_HMX + 1], scal[:, SC_TMP:SC_TMP + 1],
                  ALU.max, inc=v_mx)
            b.drain("vector")
            b.wait("gpsimd", v_h, v_h0 + (ic + 1) * NTC)
            b.dma("gpsimd", d["hscr"][ic * 128:(ic + 1) * 128, :],
                  hstg[:, par * NT:par * NT + NT], g_hs[ic % 2])

        t_dn0, v_dn0 = t_dn.v, v_dn.v
        h_r = d["hscr"].rearrange("(ic p) c -> p ic c", p=128)
        HQT = NIC * TCW
        wblk = [(mdc, 0), (mdc, II), (hstg, 0), (hstg, II), (mgut, 0),
                (bufB, 2 * HQT), (bufB, 2 * HQT + II), (bufB, 2 * HQT + 2 * II)]
        for eng in ("sync", "gpsimd"):
            b.wait(eng, t_uu, t_uu0 + NIC * NTC)
            b.wait(eng, v_mx)
            b.wait(eng, y_m[0])
            b.wait(eng, y_m[1])
            b.wait(eng, g_hs[0])
            b.wait(eng, g_hs[1])
            b.wait(eng, g_dt[0])
            b.wait(eng, g_dt[1])
        for fco in range(NFC):
            bt, c0 = wblk[fco]
            if fco % 2 == 0:
                b.dma("sync", bt[:, c0:c0 + II], d["mqdC"][fco], y_w[0])
            else:
                b.dma("gpsimd", bt[:, c0:c0 + II], d["mqdC"][fco], g_dw)
        for s in (0, 1):
            dstS = bufB[:, s * HQT:s * HQT + HQT].rearrange(
                "p (ic j) -> p ic j", j=TCW)
            b.dma("sync", dstS, h_r[:, :, s * TCW:(s + 1) * TCW], y_la[s])

        rec_allreduce_scale(SC_HMX, SC_RH, SC_SH, 1)
        b.ins("vector", "tensor_scalar", scal[:, SC_SVD:SC_SVD + NFC],
              scal[:, SC_WD:SC_WD + NFC], scal[:, SC_SH:SC_SH + 1],
              0.5 / QMAXF, ALU.mult, ALU.mult)
        b.drain("vector")

        qb = [0] * NTC

        def stage_quant(s):
            ssem = y_la[s] if s < 2 else g_hr3[s % 2]
            b.wait("vector", ssem)
            b.wait("scalar", ssem)
            qb[s] = v_qa.v
            rec_quant_inplace([(bufB, (s % 2) * HQT, HQT)], SC_RH,
                              wfscr, fint, fint2, v_hq)

        stage_quant(0)
        for tc in range(NTC):
            par = tc % 2
            for ic in range(NIC):
                if ic % 2 == 0:
                    b.wait("tensor", v_qa, qb[tc] + ic // 2 + 1)
                for fco in range(NFC):
                    bt, c0 = wblk[fco]
                    if ic == 0:
                        if tc == 0:
                            if fco == 0:
                                b.wait("tensor", y_w[0])
                                b.wait("tensor", g_dw)
                                b.wait("tensor", s_si, s_si0 + NIC * NTC)
                                b.wait("tensor", v_h, v_h0 + NIC * NTC)
                        else:
                            b.wait("tensor", v_dn,
                                   v_dn0 + (tc - 1) * NFC + fco + 1)
                    lastmm = ic == NIC - 1
                    b.mm(ps[:, fco * TCW:fco * TCW + TCW],
                         bt[:, c0 + ic * 128:c0 + ic * 128 + 128],
                         bufB[:, par * HQT + ic * TCW:par * HQT + ic * TCW + TCW],
                         ic == 0, lastmm, inc=(t_dn if lastmm else None))
            for fco in range(NFC):
                b.wait("vector", t_dn, t_dn0 + tc * NFC + fco + 1)
                pd = ps[:, fco * TCW:fco * TCW + TCW]
                xsl = x_ch(fco)[:, tc * TCW:(tc + 1) * TCW]
                if not last:
                    b.ins("vector", "scalar_tensor_tensor", xsl, pd,
                          scal[:, SC_SVD + fco:SC_SVD + fco + 1], xsl,
                          ALU.mult, ALU.add, inc=v_dn)
                    b.drain("vector")
                else:
                    u_ = tc * NFC + fco
                    opar = u_ % 3
                    if u_ >= 3:
                        b.wait("vector", y_o[opar], 16 * ((u_ - 3) // 3 + 1))
                    b.ins("vector", "scalar_tensor_tensor",
                          outst[:, opar * TCW:opar * TCW + TCW], pd,
                          scal[:, SC_SVD + fco:SC_SVD + fco + 1], xsl,
                          ALU.mult, ALU.add, inc=v_dn)
                    b.drain("vector")
                    b.wait("sync", v_dn, v_dn0 + u_ + 1)
                    b.dma("sync", d["outT"][fco * 128:(fco + 1) * 128,
                                            tc * TCW:(tc + 1) * TCW],
                          outst[:, opar * TCW:opar * TCW + TCW],
                          y_o[opar])
            if tc + 2 < NTC:
                b.wait("gpsimd", t_dn, t_dn0 + (tc + 1) * NFC)
                dstS = bufB[:, (tc % 2) * HQT:(tc % 2) * HQT + HQT
                            ].rearrange("p (ic j) -> p ic j", j=TCW)
                b.dma("gpsimd", dstS,
                      h_r[:, :, (tc + 2) * TCW:(tc + 3) * TCW],
                      g_hr3[tc % 2])
            if tc + 1 < NTC:
                stage_quant(tc + 1)
        b.barrier(bar)
        fs.close()

    def pt_col(i):
        return i * 1040

    def rec_attn():
        as_ = ExitStack()
        sqpp = as_.enter_context(nc.sbuf_tensor("a_sq", [128, 2 * TCW], BF16))
        m2rep = as_.enter_context(nc.sbuf_tensor("a_m2", [128, 2 * TCW], F32))
        rsrep = as_.enter_context(nc.sbuf_tensor("a_rs", [128, 2 * TCW], F32))
        ascr = as_.enter_context(nc.sbuf_tensor("a_scr", [128, 512], F32))
        wqkt = as_.enter_context(nc.sbuf_tensor("a_wqk", [128, 2 * 1024], BF16))
        wvt = as_.enter_context(nc.sbuf_tensor("a_wv", [128, 8 * 512], BF16))
        vS = as_.enter_context(nc.sbuf_tensor("a_v", [128, NPT * 1040], BF16))
        Epp = as_.enter_context(nc.sbuf_tensor("a_e", [128, 2 * PL], BF16))
        arr = as_.enter_context(nc.sbuf_tensor("a_rr", [128, PL], F32))
        Osb = as_.enter_context(nc.sbuf_tensor("a_o", [128, 2 * PL], BF16))
        t2b = as_.enter_context(nc.sbuf_tensor("a_t2", [128, PL], BF16))
        cv64 = as_.enter_context(nc.sbuf_tensor("a_cv64", [128, PL], BF16))

        b.dma("sync", cv64[64:65, :], d["cvec"][0:1, :], y_cn)
        rec_ln_stats(sqpp)
        wq_r = d["wqp"].rearrange("(fc p) c -> p fc c", p=128)
        wk_r = d["wkp"].rearrange("(fc p) c -> p fc c", p=128)
        wv_r = d["wvp"].rearrange("(fc p) c -> p fc c", p=128)
        Y2C, QSC, KSC = 0, NFC * PL, 2 * NFC * PL

        for s in range(3):
            qlen = QLEN[s]
            seg = SEG[s]
            if s == 0:
                rec_ln_apply(m2rep, rsrep, ascr, seg, qlen, Y2C, PL)
            b.wait("vector", v_ln)
            qk_list = [(li, wr, outc, qd)
                       for li, (wr, outc) in enumerate(((wq_r, QSC),
                                                        (wk_r, KSC)))
                       for qd in range(NFC)]

            def qk_load(j):
                li, wr, outc, qd = qk_list[j]
                par = j % 2
                if j >= 2 or s >= 1:
                    b.wait("gpsimd", t_qk)
                wdst = wqkt[:, par * 1024:par * 1024 + 1024].rearrange(
                    "p (fc j) -> p fc j", j=128)
                b.dma("gpsimd", wdst, wr[:, :, qd * 128:(qd + 1) * 128],
                      g_qk[par])

            qk_load(0)
            qk_load(1)
            for j, (li, wr, outc, qd) in enumerate(qk_list):
                par = j % 2
                b.wait("tensor", g_qk[par])
                b.wait("tensor", v_ln)
                for ci, (q0, qw) in enumerate(qchunks(qlen)):
                    ppar = ci % 2
                    pq = ps[:, ppar * TCW:ppar * TCW + TCW]
                    b.wait("tensor", v_qk)
                    for fc in range(NFC):
                        b.mm(pq[:, 0:qw],
                             wqkt[:, par * 1024 + fc * 128:
                                  par * 1024 + fc * 128 + 128],
                             bufB[:, Y2C + fc * PL + q0:Y2C + fc * PL + q0 + qw],
                             fc == 0, fc == NFC - 1,
                             inc=(t_qk if fc == NFC - 1 else None))
                    b.wait("vector", t_qk)
                    b.ins("vector", "tensor_copy",
                          bufB[:, outc + qd * PL + q0:outc + qd * PL + q0 + qw],
                          pq[:, 0:qw], inc=v_qk)
                if j + 2 < len(qk_list):
                    qk_load(j + 2)
            if s >= 1:
                b.ins("vector", "memset",
                      strided_cols(bufB, KSC + 1365, PL, NFC), 0.0)
                b.drain("vector")
            b.ins("vector", "memset", vS[64:128, 10 * 1040:11 * 1040], 0.0)
            b.drain("vector")
            b.ins("vector", "memset",
                  strided_cols(vS, 64, 65, NPT * 16), 1.0, inc=v_qk)
            b.drain("vector")
            def vbuf(vc, fc):
                if vc == 0:
                    return wvt[:, fc * TCW:(fc + 1) * TCW]
                bt = Epp if fc < 4 else Osb
                return bt[:, (fc % 4) * TCW:(fc % 4) * TCW + TCW]

            for vc in range(2):
                for fc in range(NFC):
                    b.dma("gpsimd", vbuf(vc, fc),
                          wv_r[:, fc, vc * TCW:(vc + 1) * TCW],
                          g_wv if vc == 0 else g_wv2)
            for vc in range(2):
                b.wait("tensor", g_wv if vc == 0 else g_wv2)
                for pt in range(NPT):
                    sw = min(128, qlen - pt * 128)
                    pv = ps[:, 1024 + (pt % 2) * TCW:1024 + (pt % 2) * TCW + TCW]
                    b.wait("tensor", v_qk)
                    for fc in range(NFC):
                        b.mm(pv[0:sw, :],
                             bufB[:, Y2C + fc * PL + pt * 128:
                                  Y2C + fc * PL + pt * 128 + sw],
                             vbuf(vc, fc),
                             fc == 0, fc == NFC - 1,
                             inc=(t_qk if fc == NFC - 1 else None))
                    b.wait("vector", t_qk)
                    vdst = vS[0:sw, pt_col(pt) + 65 * 8 * vc:
                              pt_col(pt) + 65 * 8 * vc + 520].rearrange(
                        "p (t j) -> p t j", j=65)[:, :, 0:64]
                    b.ins("vector", "tensor_copy", vdst,
                          pv[0:sw, :].rearrange("p (t j) -> p t j", j=64),
                          inc=v_qk)
            b.wait("tensor", v_qk)
            if s + 1 < 3:
                b.wait("vector", t_qk)
                rec_ln_apply(m2rep, rsrep, ascr, SEG[s + 1], QLEN[s + 1],
                             Y2C, PL)
            strips = [(0, 1024), (1024, qlen - 1024)]
            pending = []
            for h in range(NH):
                ch = h // 2
                poff = 64 * (h % 2)
                opar = h % 2
                osb = Osb[0:65, opar * PL:opar * PL + PL]
                for si_, (q0, qw) in enumerate(strips):
                    if si_ == 1 and pending:
                        pending.pop(0)()
                    s_ex0 = s_ex.v
                    AVp = ps[0:65, 2048:2048 + qw]

                    def rec_scores(i, q0=q0, qw=qw, s_ex0=s_ex0):
                        pw = 128 if i < NPT - 1 else 86
                        spar = i % 2
                        b.wait("tensor", s_ex, s_ex0 + max(0, i - 1))
                        Sp = ps[:, spar * 1024:spar * 1024 + 1024]
                        lo = max(q0, 128 * i - 2)
                        hi = min(q0 + qw, 128 * i + 130)
                        bp = []
                        c = lo
                        while c < hi:
                            nxt = min(hi, q0 + ((c - q0) // 512 + 1) * 512)
                            bp.append((c, nxt))
                            c = nxt
                        bbank = set((c0 - q0) // 512 for c0, _ in bp)
                        mms = []
                        for lq0, lqw in qchunks(qw):
                            mms.append((
                                Sp[0:pw, lq0:lq0 + lqw],
                                bufB[poff:poff + 64,
                                     KSC + ch * PL + 128 * i:
                                     KSC + ch * PL + 128 * i + pw],
                                bufB[poff:poff + 64,
                                     QSC + ch * PL + q0 + lq0:
                                     QSC + ch * PL + q0 + lq0 + lqw],
                                True, (lq0 // 512) not in bbank))
                        for c0, c1 in bp:
                            mms.append((
                                Sp[0:pw, c0 - q0:c1 - q0], cident[0:pw, 0:pw],
                                cband[0:pw, c0 - (128 * i - 2):
                                      c1 - (128 * i - 2)],
                                False, True))
                        for mj, (o_, l_, r_, st, sp) in enumerate(mms):
                            b.mm(o_, l_, r_, st, sp,
                                 inc=(t_sc if mj == len(mms) - 1 else None))

                    def rec_exp(i, qw=qw):
                        pw = 128 if i < NPT - 1 else 86
                        spar = i % 2
                        b.wait("scalar", t_av)
                        b.wait("scalar", t_sc)
                        b.ins("scalar", "activation",
                              Epp[0:pw, (i % 2) * PL:(i % 2) * PL + qw],
                              ps[0:pw, spar * 1024:spar * 1024 + qw], AF.Exp,
                              scale=scal[0:pw, SC_Q8:SC_Q8 + 1], inc=s_ex)

                    def rec_av(i, q0=q0, qw=qw, s_ex0=s_ex0, AVp=AVp):
                        pw = 128 if i < NPT - 1 else 86
                        b.wait("tensor", s_ex, s_ex0 + i + 1)
                        mms = []
                        if i == 0:
                            b.wait("tensor", v_oe)
                        for lq0, lqw in qchunks(qw):
                            mms.append((
                                AVp[:, lq0:lq0 + lqw],
                                vS[0:pw, pt_col(i) + 65 * h:
                                   pt_col(i) + 65 * h + 65],
                                Epp[0:pw, (i % 2) * PL + lq0:
                                    (i % 2) * PL + lq0 + lqw],
                                i == 0, i == NPT - 1))
                        for mj, (o_, l_, r_, st, sp) in enumerate(mms):
                            b.mm(o_, l_, r_, st, sp,
                                 inc=(t_av if mj == len(mms) - 1 else None))

                    for i in range(NPT):
                        rec_scores(i)
                        rec_exp(i)
                        if i >= 1:
                            rec_av(i - 1)
                    rec_av(NPT - 1)
                    b.wait("vector", t_av)
                    b.ins("vector", "tensor_copy", osb[:, q0:q0 + qw],
                          AVp[:, 0:qw], inc=v_oe)
                    b.drain("vector")
                b.wait("vector", y_cn)
                b.ins("vector", "tensor_tensor", t2b[64:65, 0:qlen],
                      osb[64:65, 0:qlen], cv64[64:65, 0:qlen], ALU.add)
                b.drain("vector")
                b.wait("vector", g_row)
                b.ins("vector", "reciprocal", rows[64:65, 0:qlen],
                      t2b[64:65, 0:qlen], inc=v_rc)
                b.wait("sync", v_rc)
                b.dma("sync", d["rows"][2:3, 0:qlen], rows[64:65, 0:qlen],
                      g_row)
                b.wait("sync", g_row)
                b.wait("sync", v_rc)
                b.dma("sync", arr[0:64, 0:qlen],
                      row_bcast_ap(2, 0, 64, qlen), y_rr)

                def chain_tail(h=h, ch=ch, poff=poff, osb=osb, qlen=qlen,
                               seg=seg):
                    b.wait("vector", y_rr)
                    b.wait("vector", g_xa)
                    b.ins("vector", "tensor_tensor", t2b[0:64, 0:qlen],
                          osb[0:64, 0:qlen], arr[0:64, 0:qlen], ALU.mult)
                    b.drain("vector")
                    xsl = x_ch(ch)[poff:poff + 64, seg:seg + qlen]
                    if poff == 0:
                        b.ins("vector", "tensor_tensor", xsl, xsl,
                              t2b[0:64, 0:qlen], ALU.add, inc=v_xa)
                        b.drain("vector")
                    else:
                        b.mark("vector", v_rc)
                        b.wait("gpsimd", v_rc)
                        b.ins("gpsimd", "dma_start", out=xsl,
                              in_=t2b[0:64, 0:qlen], accum_op=ALU.add,
                              inc=g_xa, incn=16)
                pending.append(chain_tail)
                if s >= 1 and h in (3, 7, 11):
                    k3 = (3, 7, 11).index(h)
                    sseg, sql = SEG[s - 1], QLEN[s - 1]
                    c0s, cws = qchunks(sql)[k3]
                    rec_stats_chunk(sqpp, sseg + c0s, cws)
            while pending:
                pending.pop(0)()
            b.wait("gpsimd", g_xa)
            if s == 2:
                b.wait("tensor", g_xa)
                b.wait("vector", v_xa)
                for c0s, cws in qchunks(qlen):
                    rec_stats_chunk(sqpp, seg + c0s, cws)
                b.barrier(bar)
            else:
                for e_, sm_ in (("gpsimd", t_av), ("gpsimd", v_xa),
                                ("gpsimd", v_rc), ("tensor", s_ex),
                                ("tensor", v_oe), ("tensor", v_xa),
                                ("tensor", g_xa), ("vector", t_av),
                                ("vector", g_xa)):
                    b.wait(e_, sm_)
        as_.close()

    rec_ffn(False)
    rec_attn()
    rec_ffn(True)
    for s4 in y_o:
        b.wait("sync", s4)
    b.barrier(bar)

    b.emit()
    stk.close()
    return nc


_PERM = np.concatenate([np.arange(s, NT, 3) for s in range(3)])
_IPERM = np.argsort(_PERM)


def _consts():
    bf = ml_dtypes.bfloat16
    band = np.zeros((128, BANDW), np.float32)
    for p in range(128):
        band[p, p:p + 5] = 8.0 * np.log(2.0)
    cv = np.zeros((1, PL), np.float32)
    cv[0, 0], cv[0, 1], cv[0, PL - 2], cv[0, PL - 1] = 2, 1, 1, 2

    return {
        "cband": band.astype(bf),
        "cident": np.eye(128, dtype=np.float32).astype(bf),
        "cvec": cv.astype(bf),
                "cones": np.ones((128, 1), np.float32).astype(bf),
    }


def kernel(L0_attn_x, L0_attn_W_q, L0_attn_W_k, L0_attn_W_v,
           L0_ffn_clip_x, L0_ffn_clip_down,
           L0_ffn_w_gate, L0_ffn_w_up, L0_ffn_w_down):
    nc = build_nc()
    x = np.asarray(L0_attn_x, dtype=np.float32)
    shared = {
        "wqp": np.ascontiguousarray(np.asarray(L0_attn_W_q, np.float32)),
        "wkp": np.ascontiguousarray(np.asarray(L0_attn_W_k, np.float32)),
        "wvp": np.ascontiguousarray(np.asarray(L0_attn_W_v, np.float32)),
        "wg_row": np.ascontiguousarray(np.asarray(L0_ffn_w_gate, np.float32)),
        "wu_row": np.ascontiguousarray(np.asarray(L0_ffn_w_up, np.float32)),
        "wd_row": np.ascontiguousarray(np.asarray(L0_ffn_w_down, np.float32)),
        "clips": np.array([[float(np.asarray(L0_ffn_clip_x).reshape(-1)[0]),
                            float(np.asarray(L0_ffn_clip_down).reshape(-1)[0])]],
                          np.float32),
    }
    shared.update(_consts())
    in_maps = []
    for bi in range(8):
        m = dict(shared)
        m["xT"] = np.ascontiguousarray(x[bi][_PERM].T)
        in_maps.append(m)
    res = run_bass_kernel_spmd(nc, in_maps, core_ids=list(range(8)))
    out = np.empty((8, NT, D), np.float32)
    for bi in range(8):
        out[bi] = res.results[bi]["outT"].T[_IPERM]
    return out


# version 32
# speedup vs baseline: 1.0102x; 1.0043x over previous
import numpy as np
import ml_dtypes
import concourse.bass as bass
import concourse.mybir as mybir
from concourse.bass_utils import run_bass_kernel_spmd
from contextlib import ExitStack

F32 = mybir.dt.float32
BF16 = mybir.dt.bfloat16
AF = mybir.ActivationFunctionType
ALU = mybir.AluOpType
AXX = mybir.AxisListType.X

D = 1024
NFC = 8
NT = 4096
TCW = 512
NTC = NT // TCW
II = 2816
NIC = II // 128
NH = 16
PL = 1366
NPT = 11
QLEN = (1366, 1365, 1365)
SEG = (0, 1366, 2731)
EPS = 1e-5
QMAXF = 127.0
BANDW = 136
ENGINES = ("tensor", "vector", "scalar", "gpsimd", "sync")


def qchunks(n):
    out, c = [], 0
    while c < n:
        out.append((c, min(512, n - c)))
        c += 512
    return out


class Sem:
    __slots__ = ("h", "v", "name")

    def __init__(self, h, name):
        self.h, self.v, self.name = h, 0, name


class B:
    def __init__(self, nc, stk):
        self.nc = nc
        self.stk = stk
        self.q = {k: [] for k in ENGINES}

    def sem(self, name):
        return Sem(self.stk.enter_context(self.nc.semaphore(name)), name)

    def ins(self, eng, meth, *args, inc=None, incn=1, **kw):
        self.q[eng].append((meth, args, kw, inc.h if inc else None, incn))
        if inc is not None:
            inc.v += incn

    def wait(self, eng, sem, val=None):
        v = sem.v if val is None else val
        assert v <= sem.v, f"wait beyond recorded increments: {sem.name} {v}>{sem.v}"
        self.q[eng].append(("wait_ge", (sem.h, v), {}, None, 0))

    def dma(self, eng, out, in_, sem):
        self.ins(eng, "dma_start", inc=sem, incn=16, out=out, in_=in_)

    def mm(self, out, lhsT, rhs, start, stop, inc=None):
        self.ins("tensor", "matmul", out, lhsT, rhs,
                 start=start, stop=stop, inc=inc)

    def drain(self, eng):
        self.q[eng].append(("drain", (), {}, None, 0))

    def mark(self, eng, sem):
        self.drain(eng)
        self.ins(eng, "nop", inc=sem)

    def barrier(self, sem):
        for eng in ENGINES:
            self.mark(eng, sem)
        tgt = sem.v
        for eng in ENGINES:
            self.wait(eng, sem, tgt)

    def emit(self):
        with self.nc.Block() as blk:
            for eng in ENGINES:
                getattr(blk, eng)(self._runner(eng))

    def _runner(self, eng):
        entries = self.q[eng]

        def run(e):
            for meth, args, kw, inch, incn in entries:
                r = getattr(e, meth)(*args, **kw)
                if inch is not None:
                    r.then_inc(inch, incn)
        return run


def strided_cols(t, start, step, count, width=1, nrows=128, row0=0):
    ap = t[row0:row0 + nrows, start:start + width]
    return bass.AP(ap.tensor, ap.offset, [ap.ap[0], [step, count], [1, width]])


def build_nc(ncores=8):
    nc = bass.Bass("TRN2", target_bir_lowering=False)
    d = {}
    for nm, shp, dt, kind in (
            ("xT", [D, NT], F32, "ExternalInput"),
            ("wqp", [D, D], F32, "ExternalInput"),
            ("wkp", [D, D], F32, "ExternalInput"),
            ("wvp", [D, D], F32, "ExternalInput"),
            ("wg_row", [II, D], F32, "ExternalInput"),
            ("wu_row", [II, D], F32, "ExternalInput"),
            ("wd_row", [D, II], F32, "ExternalInput"),
            ("clips", [1, 2], F32, "ExternalInput"),
            ("cband", [128, BANDW], BF16, "ExternalInput"),
            ("cident", [128, 128], BF16, "ExternalInput"),
            ("cvec", [1, PL], BF16, "ExternalInput"),
            ("cones", [128, 1], BF16, "ExternalInput"),
            ("outT", [D, NT], F32, "ExternalOutput"),
            ("mqg", [D, II], BF16, "Internal"),
            ("mqu", [D, II], BF16, "Internal"),
            ("mqgR", [II, D], BF16, "Internal"),
            ("mquR", [II, D], BF16, "Internal"),
            ("mqdR", [D, II], BF16, "Internal"),
            ("mqdC", [NFC, 128, NIC * 128], BF16, "Internal"),
            ("hscr", [II, NT], BF16, "Internal"),
            ("cc_in", [1, 128], F32, "Internal"),
            ("cc_out", [1, 128], F32, "Internal"),
            ("rows", [5, NT], F32, "Internal")):
        d[nm] = nc.dram_tensor(nm, shp, dt, kind=kind).ap()

    stk = ExitStack()
    b = B(nc, stk)

    g_x = b.sem("g_x")
    g_wr = (b.sem("g_wra"), b.sem("g_wrb"))
    g_wt2 = (b.sem("g_wta"), b.sem("g_wtb"))
    g_tp = (b.sem("g_tp0"), b.sem("g_tp1"))
    g_dw = b.sem("g_dw")
    g_dt = (b.sem("g_dt0"), b.sem("g_dt1"))
    t_cc = b.sem("t_cc")
    g_cc = b.sem("g_cc")
    g_hs = (b.sem("g_hsa"), b.sem("g_hsb"))
    g_hr3 = (b.sem("g_hr0"), b.sem("g_hr1"), b.sem("g_hr2"))
    g_row = b.sem("g_row")
    g_qk = (b.sem("g_qka"), b.sem("g_qkb"))
    g_wv = b.sem("g_wv")
    g_wv2 = b.sem("g_wv2")
    y_w = (b.sem("y_wa"), b.sem("y_wb"))
    y_o = (b.sem("y_oa"), b.sem("y_ob"), b.sem("y_oc"))
    y_cn = b.sem("y_cn")
    y_la = (b.sem("y_laa"), b.sem("y_lab"))
    y_sc = b.sem("y_sc")
    y_rr = b.sem("y_rr")
    y_m = (b.sem("y_ma"), b.sem("y_mb"))
    c_cc = b.sem("c_cc")
    t_st = b.sem("t_st")
    v_st = b.sem("v_st")
    s_sq = b.sem("s_sq")
    t_squ = b.sem("t_squ")
    v_ln = b.sem("v_ln")
    t_gu = b.sem("t_gu")
    t_uu = b.sem("t_uu")
    s_si = b.sem("s_si")
    v_h = b.sem("v_h")
    v_mx = b.sem("v_mx")
    t_wfree = b.sem("t_wfree")
    t_dn = b.sem("t_dn")
    v_dn = b.sem("v_dn")
    v_hq = b.sem("v_hq")
    v_qa = b.sem("v_qa")
    s_cv = b.sem("s_cv")
    t_qk = b.sem("t_qk")
    v_qk = b.sem("v_qk")
    t_sc = b.sem("t_sc")
    s_ex = b.sem("s_ex")
    t_av = b.sem("t_av")
    v_rc = b.sem("v_rc")
    v_oe = b.sem("v_oe")
    v_xa = b.sem("v_xa")
    g_xa = b.sem("g_xa")
    v_wr = b.sem("v_wr")
    v_wq = b.sem("v_wq")
    g_wb = b.sem("g_wb")
    bar = b.sem("bar")

    xs = stk.enter_context(nc.sbuf_tensor("xs", [128, NFC * NT], BF16))
    bufB = stk.enter_context(nc.sbuf_tensor("bufB", [128, 3 * NFC * PL], BF16))
    rows = stk.enter_context(nc.sbuf_tensor("rows_sb", [128, 1680], F32))
    scal = stk.enter_context(nc.sbuf_tensor("scal", [128, 160], F32))
    cband = stk.enter_context(nc.sbuf_tensor("cband_sb", [128, BANDW], BF16))
    cident = stk.enter_context(nc.sbuf_tensor("cident_sb", [128, 128], BF16))
    cones = stk.enter_context(nc.sbuf_tensor("cones_sb", [128, 1], BF16))
    ps = stk.enter_context(nc.psum_tensor("ps", [128, 4096], F32))

    def x_ch(fc):
        return xs[:, fc * NT:(fc + 1) * NT]

    SC_RX, SC_SX, SC_RH, SC_SH, SC_HMX, SC_TMP = 0, 1, 2, 3, 4, 5
    SC_WG, SC_WU, SC_WD = 8, 30, 52
    SC_SVG, SC_SVU, SC_SVD = 60, 82, 104
    SC_RTMP = 112
    SC_Q8 = 135
    SC_HALF = 7

    b.dma("sync", cband[:, :], d["cband"][:, :], y_cn)
    b.dma("sync", cident[:, :], d["cident"][:, :], y_cn)
    b.dma("sync", cones[:, :], d["cones"][:, :], y_cn)
    b.dma("sync", rows[0:1, 1540:1542], d["clips"][:, :], y_cn)
    for fc in range(NFC):
        b.dma("gpsimd", x_ch(fc), d["xT"][fc * 128:(fc + 1) * 128, :], g_x)

    b.ins("vector", "memset", scal[:, SC_Q8:SC_Q8 + 1], 0.125)
    b.drain("vector")
    b.ins("vector", "memset", scal[:, SC_HALF:SC_HALF + 1], 0.5)
    b.drain("vector")
    b.wait("tensor", y_cn)
    b.wait("scalar", y_cn)

    def row_bcast_ap(r, col, nrows, width):
        return bass.AP(d["rows"].tensor, d["rows"].offset + r * NT + col,
                       [[0, nrows], [1, width]])

    def rec_weight_quant(wfscr, wq2, m2rep, fint, fint2, mdc, hstg,
                         hooks, down_out=None):
        chunks = []
        for wi, spec in enumerate((("wg_row", "mqgR", NIC, D, SC_WG),
                                   ("wu_row", "mquR", NIC, D, SC_WU),
                                   ("wd_row", "mqdR", NFC, II, SC_WD))):
            for ic in range(spec[2]):
                chunks.append((wi,) + spec + (ic,))
        NCH = len(chunks)
        tasks = []

        def load(K):
            wi, rowt, outR, nch, wdt, scol, ic = chunks[K]
            par = K % 2
            if wdt > 1024:
                half = wdt // 2
                for eng in ("sync", "gpsimd"):
                    b.wait(eng, v_wq)
                    b.wait(eng, s_cv)
                    b.wait(eng, v_qa)
                b.dma("sync", wfscr[:, 0:half],
                      d[rowt][ic * 128:(ic + 1) * 128, 0:half], g_wt2[0])
                b.dma("gpsimd", wfscr[:, half:wdt],
                      d[rowt][ic * 128:(ic + 1) * 128, half:wdt], g_dw)
                return
            wrow = wfscr[:, 0:wdt] if par == 0 else wq2[:, 0:wdt]
            eng = "sync" if par == 0 else "gpsimd"
            if K >= 2:
                b.wait(eng, v_wq)
                b.wait(eng, s_cv)
            else:
                b.wait(eng, v_mx)
            b.dma(eng, wrow, d[rowt][ic * 128:(ic + 1) * 128, :],
                  g_wt2[par])

        def compute(K):
            wi, rowt, outR, nch, wdt, scol, ic = chunks[K]
            par = K % 2
            wide = wdt > 1024
            wrow = wfscr[:, 0:wdt] if (par == 0 or wide) else wq2[:, 0:wdt]
            qsl = mdc[:, par * II:par * II + wdt]
            rt = scal[:, SC_RTMP + par:SC_RTMP + par + 1]
            sc = scal[:, scol + ic:scol + ic + 1]
            if wide:
                b.wait("vector", g_wt2[0])
                b.wait("vector", g_dw)
            else:
                b.wait("vector", g_wt2[par])
            b.ins("vector", "tensor_reduce", sc, wrow, AXX, ALU.max,
                  apply_absolute_value=True)
            b.drain("vector")
            b.ins("vector", "tensor_single_scalar", rt, sc, 1.0 / QMAXF,
                  ALU.mult)
            b.drain("vector")
            b.ins("vector", "reciprocal", rt, rt, inc=v_wr)
            b.drain("vector")
            b.wait("vector", y_m[par])
            b.wait("scalar", v_wr)
            nsub = (wdt + 511) // 512
            qa0, sc0 = v_qa.v, s_cv.v
            for si in range(nsub):
                c0 = si * 512
                cw = min(512, wdt - c0)
                sp = si % 2
                u = m2rep[:, sp * 512:sp * 512 + cw]
                ti = (fint if sp == 0 else fint2)[:, 0:cw]
                if si >= 2:
                    b.wait("scalar", v_qa, qa0 + si - 1)
                b.ins("scalar", "activation", u, wrow[:, c0:c0 + cw],
                      AF.Identity, scale=rt,
                      bias=scal[:, SC_HALF:SC_HALF + 1])
                b.drain("scalar")
                b.ins("scalar", "copy", ti, u, inc=s_cv)
                b.drain("scalar")
                b.wait("vector", s_cv, sc0 + si + 1)
                b.ins("vector", "tensor_tensor", u, ti, u, ALU.is_gt)
                b.drain("vector")
                b.ins("vector", "tensor_tensor", qsl[:, c0:c0 + cw], ti, u,
                      ALU.subtract,
                      inc=(v_wq if si == nsub - 1 else v_qa))
                if si == nsub - 1:
                    b.mark("vector", v_qa)
            b.wait("sync", v_wq)
            b.dma("sync", d[outR][ic * 128:(ic + 1) * 128, :], qsl, y_m[par])
            if wi == 0 and ic == nch - 1:
                b.wait("sync", y_m[0])
                b.wait("sync", y_m[1])
                tasks.extend((outR, "mqg", fc) for fc in range(NFC))

        def transpose_task():
            outR, outT, fc = tasks.pop(0)
            tt = hstg[:, (fc % 2) * II:(fc % 2) * II + II]
            b.wait("sync", g_hs[fc % 2])
            b.ins("sync", "dma_start", out=tt,
                  in_=d[outR][:, fc * 128:(fc + 1) * 128],
                  transpose=True, inc=g_tp[fc % 2], incn=16)
            b.wait("gpsimd", g_tp[fc % 2])
            b.dma("gpsimd", d[outT][fc * 128:(fc + 1) * 128, :], tt,
                  g_hs[fc % 2])

        if down_out is not None:
            down_out["load"] = lambda k: load(44 + k)
            down_out["compute"] = lambda k: compute(44 + k)
            NCH = 44
        b.wait("sync", v_mx)
        load(0)
        load(1)
        for K in range(NCH):
            if chunks[K][4] > 1024:
                load(K)
            compute(K)
            if tasks:
                transpose_task()
            if K == 43:
                b.wait("sync", y_m[0])
                b.wait("sync", y_m[1])
                tasks.extend(("mquR", "mqu", fc) for fc in range(NFC))
                while tasks:
                    transpose_task()
            for cb in hooks.get(K, ()):
                cb()
            if K + 2 < NCH and chunks[K + 2][4] <= 1024:
                load(K + 2)
        assert not tasks
        for e in ("sync", "vector"):
            b.wait(e, g_hs[0])
            b.wait(e, g_hs[1])

    def rec_down_transpose_one(mdc, ic2):
        tt = mdc[:, (ic2 % 2) * II:(ic2 % 2) * II + D]
        if ic2 < 2:
            b.wait("sync", y_m[0])
            b.wait("sync", y_m[1])
        else:
            b.wait("sync", g_dt[ic2 % 2])
        b.ins("sync", "dma_start", out=tt,
              in_=d["mqdR"][:, ic2 * 128:(ic2 + 1) * 128],
              transpose=True, inc=g_tp[ic2 % 2], incn=16)
        b.wait("gpsimd", g_tp[ic2 % 2])
        dst = bass.AP(d["mqdC"].tensor, d["mqdC"].offset + ic2 * 128,
                      [[II, 128], [128 * II, NFC], [1, 128]])
        b.dma("gpsimd", dst, tt.rearrange("p (fco j) -> p fco j", j=128),
              g_dt[ic2 % 2])

    def rec_ln_stats(sqpp):
        v_st0 = v_st.v
        for tc in range(NTC):
            su = ps[0:1, (tc % 2) * TCW:(tc % 2) * TCW + TCW]
            sq = ps[0:1, 1024 + (tc % 2) * TCW:1024 + (tc % 2) * TCW + TCW]
            if tc >= 2:
                b.wait("tensor", v_st, v_st0 + tc - 1)
            for fc in range(NFC):
                xsl = x_ch(fc)[:, tc * TCW:(tc + 1) * TCW]
                sqs = sqpp[:, (fc % 2) * TCW:(fc % 2) * TCW + TCW]
                if fc >= 2 or tc > 0:
                    b.wait("scalar", t_squ)
                b.ins("scalar", "activation", sqs, xsl, AF.Square, inc=s_sq)
                b.wait("tensor", s_sq)
                if fc < NFC - 1:
                    b.mm(su, cones[:, :], xsl, fc == 0, False)
                    b.mm(sq, cones[:, :], sqs, fc == 0, False, inc=t_squ)
                else:
                    b.mm(sq, cones[:, :], sqs, False, True, inc=t_squ)
                    b.mm(su, cones[:, :], xsl, False, True, inc=t_st)
            b.wait("vector", t_st)
            b.wait("vector", g_row)
            r0 = rows[0:1, 0:TCW]
            r1 = rows[0:1, TCW:2 * TCW]
            r2 = rows[0:1, 2 * TCW:3 * TCW]
            b.ins("vector", "tensor_single_scalar", r0, su, 1.0 / D, ALU.mult)
            b.ins("vector", "tensor_single_scalar", r2, sq, 1.0 / D, ALU.mult)
            b.drain("vector")
            b.ins("vector", "tensor_tensor", r1, r0, r0, ALU.mult)
            b.drain("vector")
            b.ins("vector", "tensor_tensor", r2, r2, r1, ALU.subtract)
            b.drain("vector")
            b.ins("vector", "tensor_single_scalar", r2, r2, EPS, ALU.add,
                  inc=v_ln)
            b.drain("vector")
            b.wait("scalar", v_ln)
            b.ins("scalar", "activation", r2, r2, AF.Sqrt, inc=s_sq)
            b.wait("vector", s_sq)
            b.ins("vector", "reciprocal", r1, r2)
            b.drain("vector")
            b.ins("vector", "tensor_tensor", r0, r0, r1, ALU.mult, inc=v_st)
            b.drain("vector")
            b.wait("sync", v_st)
            b.dma("sync", d["rows"][0:1, tc * TCW:(tc + 1) * TCW], r0, g_row)
            b.dma("sync", d["rows"][1:2, tc * TCW:(tc + 1) * TCW], r1, g_row)
        b.wait("sync", g_row)

    def rec_stats_chunk(sqpp, c0, cw):
        su = ps[0:1, 3072:3072 + cw]
        sq = ps[0:1, 3584:3584 + cw]
        s_sq0, t_squ0 = s_sq.v, t_squ.v
        b.wait("tensor", v_xa)
        b.wait("tensor", g_xa)
        b.wait("vector", v_xa)
        b.wait("vector", g_xa)
        b.wait("tensor", v_st)

        def sqr(fc):
            sqs = sqpp[:, (fc % 2) * TCW:(fc % 2) * TCW + cw]
            b.wait("vector", t_squ,
                   t_squ0 + max(0, fc - 1) if fc >= 2 else t_squ0)
            b.ins("vector", "tensor_tensor", sqs,
                  x_ch(fc)[:, c0:c0 + cw], x_ch(fc)[:, c0:c0 + cw],
                  ALU.mult, inc=s_sq)
            b.drain("vector")

        sqr(0)
        sqr(1)
        for fc in range(NFC):
            xsl = x_ch(fc)[:, c0:c0 + cw]
            sqs = sqpp[:, (fc % 2) * TCW:(fc % 2) * TCW + cw]
            b.wait("tensor", s_sq, s_sq0 + fc + 1)
            if fc < NFC - 1:
                b.mm(su, cones[:, :], xsl, fc == 0, False)
                b.mm(sq, cones[:, :], sqs, fc == 0, False, inc=t_squ)
            else:
                b.mm(sq, cones[:, :], sqs, False, True, inc=t_squ)
                b.mm(su, cones[:, :], xsl, False, True, inc=t_st)
            if fc + 2 < NFC:
                sqr(fc + 2)
        b.wait("vector", t_st)
        b.wait("vector", g_row)
        r0 = rows[0:1, 0:TCW]
        r1 = rows[0:1, TCW:2 * TCW]
        r2 = rows[0:1, 2 * TCW:3 * TCW]
        b.ins("vector", "tensor_single_scalar", r0[:, 0:cw], su, 1.0 / D,
              ALU.mult)
        b.ins("vector", "tensor_single_scalar", r2[:, 0:cw], sq, 1.0 / D,
              ALU.mult)
        b.drain("vector")
        b.ins("vector", "tensor_tensor", r1[:, 0:cw], r0[:, 0:cw],
              r0[:, 0:cw], ALU.mult)
        b.drain("vector")
        b.ins("vector", "tensor_tensor", r2[:, 0:cw], r2[:, 0:cw],
              r1[:, 0:cw], ALU.subtract)
        b.drain("vector")
        b.ins("vector", "tensor_single_scalar", r2[:, 0:cw], r2[:, 0:cw],
              EPS, ALU.add, inc=v_ln)
        b.drain("vector")
        b.wait("scalar", v_ln)
        b.ins("scalar", "activation", r2[:, 0:cw], r2[:, 0:cw], AF.Sqrt,
              inc=s_sq)
        b.wait("vector", s_sq)
        b.ins("vector", "reciprocal", r1[:, 0:cw], r2[:, 0:cw])
        b.drain("vector")
        b.ins("vector", "tensor_tensor", r0[:, 0:cw], r0[:, 0:cw],
              r1[:, 0:cw], ALU.mult, inc=v_st)
        b.drain("vector")
        b.wait("sync", v_st)
        b.dma("sync", d["rows"][3:4, c0:c0 + cw], r0[:, 0:cw], g_row)
        b.dma("sync", d["rows"][4:5, c0:c0 + cw], r1[:, 0:cw], g_row)

    def rec_ln_apply(m2rep, rsrep, scr, seg0, width, ycol, ystride,
                     rm=0, rr=1):
        b.wait("sync", g_row)
        chs = qchunks(width)

        def rload(t):
            c0r, cw = chs[t]
            p = t % 2
            b.wait("sync", v_ln)
            b.dma("sync", m2rep[:, p * TCW:p * TCW + cw],
                  row_bcast_ap(rm, seg0 + c0r, 128, cw), y_la[p])
            b.dma("sync", rsrep[:, p * TCW:p * TCW + cw],
                  row_bcast_ap(rr, seg0 + c0r, 128, cw), y_la[p])

        rload(0)
        if len(chs) > 1:
            rload(1)
        for t, (c0r, cw) in enumerate(chs):
            c0 = seg0 + c0r
            p = t % 2
            b.wait("vector", y_la[p])
            for fc in range(NFC):
                xsl = x_ch(fc)[:, c0:c0 + cw]
                ysl = bufB[:, ycol + fc * ystride + c0r:
                           ycol + fc * ystride + c0r + cw]
                b.ins("vector", "tensor_tensor", scr[:, 0:cw], xsl,
                      rsrep[:, p * TCW:p * TCW + cw], ALU.mult)
                b.drain("vector")
                b.ins("vector", "tensor_tensor", ysl, scr[:, 0:cw],
                      m2rep[:, p * TCW:p * TCW + cw], ALU.subtract,
                      inc=(v_ln if fc == NFC - 1 else None))
                b.drain("vector")
            if t + 2 < len(chs):
                rload(t + 2)

    def rec_allreduce_issue(mx_col):
        b.wait("gpsimd", v_mx)
        b.dma("gpsimd", d["cc_in"][:, :], scal[:, mx_col:mx_col + 1], g_cc)
        b.wait("gpsimd", g_cc)
        b.ins("gpsimd", "collective_compute", "AllReduce", ALU.max,
              [list(range(ncores))], [d["cc_in"][:, :]],
              [d["cc_out"][:, :]], inc=c_cc)

    def rec_allreduce_finish(rcol, scol, clip_col):
        b.wait("scalar", c_cc)
        b.dma("scalar", rows[0:1, 1552:1680], d["cc_out"][:, :], t_cc)
        b.wait("vector", t_cc)
        b.ins("vector", "tensor_reduce", rows[0:1, 1544:1545],
              rows[0:1, 1552:1680], AXX, ALU.max, inc=v_ln)
        b.wait("scalar", v_ln)
        b.wait("scalar", y_cn, 64)
        b.ins("scalar", "activation", rows[0:1, 1545:1546],
              rows[0:1, 1540 + clip_col:1541 + clip_col], AF.Sigmoid, inc=s_sq)
        b.wait("vector", s_sq)
        b.ins("vector", "tensor_tensor", rows[0:1, 1546:1547],
              rows[0:1, 1544:1545], rows[0:1, 1545:1546], ALU.mult)
        b.drain("vector")
        b.ins("vector", "tensor_single_scalar", rows[0:1, 1546:1547],
              rows[0:1, 1546:1547], 1.0 / QMAXF, ALU.mult)
        b.drain("vector")
        b.ins("vector", "reciprocal", rows[0:1, 1547:1548],
              rows[0:1, 1546:1547], inc=v_ln)
        b.wait("scalar", v_ln)
        b.dma("scalar", d["rows"][4:5, 129:131], rows[0:1, 1546:1548], t_cc)
        b.wait("scalar", t_cc)
        b.dma("scalar", scal[:, scol:scol + 1],
              row_bcast_ap(4, 129, 128, 1), y_sc)
        b.dma("scalar", scal[:, rcol:rcol + 1],
              row_bcast_ap(4, 130, 128, 1), y_sc)
        b.wait("vector", y_sc)
        b.wait("scalar", y_sc)

    def rec_allreduce_scale(mx_col, rcol, scol, clip_col):
        rec_allreduce_issue(mx_col)
        rec_allreduce_finish(rcol, scol, clip_col)

    def rec_quant_inplace(cols, rcol, wfscr, wint, wint2, sem):
        subs = []
        for bt, col0, width in cols:
            for c0 in range(0, width, 1024):
                subs.append((bt, col0 + c0, min(1024, width - c0), False))
        subs[-1] = subs[-1][:3] + (True,)
        v_qa0, s_cv0 = v_qa.v, s_cv.v

        def s_part(k):
            bt, c0, cw, last = subs[k]
            par = k % 2
            wc = wfscr[:, par * 1024:par * 1024 + cw]
            ti = (wint if par == 0 else wint2)[:, 0:cw]
            b.wait("scalar", v_qa, v_qa0 + max(0, k - 1))
            b.ins("scalar", "activation", wc, bt[:, c0:c0 + cw], AF.Identity,
                  scale=scal[:, rcol:rcol + 1],
                  bias=scal[:, SC_HALF:SC_HALF + 1])
            b.drain("scalar")
            b.ins("scalar", "copy", ti, wc, inc=s_cv)
            b.drain("scalar")

        def v_part2(k):
            bt, c0, cw, last = subs[k]
            par = k % 2
            wc = wfscr[:, par * 1024:par * 1024 + cw]
            ti = (wint if par == 0 else wint2)[:, 0:cw]
            b.wait("vector", s_cv, s_cv0 + k + 1)
            b.ins("vector", "tensor_tensor", wc, ti, wc, ALU.is_gt)
            b.drain("vector")
            b.ins("vector", "tensor_tensor", bt[:, c0:c0 + cw], ti, wc,
                  ALU.subtract)
            b.drain("vector")
            b.ins("vector", "tensor_scalar", bt[:, c0:c0 + cw],
                  bt[:, c0:c0 + cw], QMAXF, -128.0, ALU.min, ALU.max,
                  inc=(sem if last else v_qa))
            if last:
                b.mark("vector", v_qa)
        for k in range(len(subs)):
            s_part(k)
            if k >= 1:
                v_part2(k - 1)
        v_part2(len(subs) - 1)

    def rec_ffn(last):
        tag = '2' if last else '1'
        fs = ExitStack()
        m2rep = fs.enter_context(nc.sbuf_tensor("f_m2" + tag,
                                                [128, 2 * TCW], F32))
        rsrep = fs.enter_context(nc.sbuf_tensor("f_rs" + tag,
                                                [128, 2 * TCW], F32))
        wfscr = fs.enter_context(nc.sbuf_tensor("f_wf" + tag, [128, II], F32))
        fint = fs.enter_context(nc.sbuf_tensor("f_int" + tag, [128, 1024],
                                               mybir.dt.int32))
        hstg = fs.enter_context(nc.sbuf_tensor("f_h" + tag, [128, 2 * NT], BF16))
        silu = fs.enter_context(nc.sbuf_tensor("f_si" + tag, [128, 2 * TCW], BF16))
        sqpp = silu
        mgut = fs.enter_context(nc.sbuf_tensor("f_w" + tag, [128, 2 * 2048], BF16))
        mdc = fs.enter_context(nc.sbuf_tensor("f_wd" + tag, [128, 2 * II], BF16))
        outst = (fs.enter_context(nc.sbuf_tensor("f_o" + tag,
                                                  [128, 3 * TCW], F32))
                 if last else None)
        wq2 = (fs.enter_context(nc.sbuf_tensor("f_wq2", [128, 1024], F32))
               if not last else None)
        fint2 = fs.enter_context(nc.sbuf_tensor("f_int2" + tag, [128, 1024],
                                                mybir.dt.int32))

        b.wait("scalar", g_x)
        b.wait("tensor", g_x)
        b.wait("vector", g_x)
        if not last:
            rec_ln_stats(sqpp)
            rec_ln_apply(m2rep, rsrep, wfscr, 0, NT, 0, NT)
        else:
            rec_ln_apply(m2rep, rsrep, wfscr, 0, NT, 0, NT, rm=3, rr=4)
        b.wait("vector", v_ln)
        b.ins("vector", "tensor_reduce", scal[:, SC_TMP:SC_TMP + 1],
              bufB[:, 0:NFC * NT], AXX, ALU.max, apply_absolute_value=True,
              inc=v_mx)
        b.drain("vector")
        rec_allreduce_issue(SC_TMP)
        st_c = {}

        def rec_xscale():
            rec_allreduce_finish(SC_RX, SC_SX, 0)
            b.ins("vector", "tensor_scalar", scal[:, SC_SVG:SC_SVG + NIC],
                  scal[:, SC_WG:SC_WG + NIC], scal[:, SC_SX:SC_SX + 1],
                  1.0 / QMAXF, ALU.mult, ALU.mult)
            b.drain("vector")

        def rec_xquant():
            b.ins("vector", "tensor_scalar", scal[:, SC_SVU:SC_SVU + NIC],
                  scal[:, SC_WU:SC_WU + NIC], scal[:, SC_SX:SC_SX + 1],
                  1.0 / QMAXF, ALU.mult, ALU.mult)
            b.drain("vector")
            st_c["vlnb"] = v_ln.v
            st_c["xqb"] = v_qa.v
            rec_quant_inplace([(bufB, fc * NT + tq * 1024, 1024)
                               for tq in range(NT // 1024)
                               for fc in range(NFC)],
                              SC_RX, wfscr, fint, fint2, v_ln)
            b.ins("vector", "memset", scal[:, SC_HMX:SC_HMX + 1], 0.0)

        wq_down = {}
        if not last:
            rec_weight_quant(wfscr, wq2, m2rep, fint, fint2, mdc, hstg,
                             {21: (rec_xscale,), 43: (rec_xquant,)},
                             wq_down)
        else:
            rec_xscale()
            rec_xquant()
        vlnb, xqb = st_c["vlnb"], st_c["xqb"]
        b.wait("tensor", v_ln, vlnb)

        t_gu0, t_uu0, s_si0, v_h0 = t_gu.v, t_uu.v, s_si.v, v_h.v
        mg_r = d["mqg"].rearrange("(fc p) i -> p fc i", p=128)
        mu_r = d["mqu"].rearrange("(fc p) i -> p fc i", p=128)
        for ic in range(NIC):
            par = ic % 2
            if ic >= 2:
                b.wait("sync", t_uu, t_uu0 + NTC * (ic - 1))
            wg_dst = mgut[:, par * 2048:par * 2048 + 1024].rearrange(
                "p (fc j) -> p fc j", j=128)
            wu_dst = mgut[:, par * 2048 + 1024:par * 2048 + 2048].rearrange(
                "p (fc j) -> p fc j", j=128)
            b.dma("sync", wg_dst, mg_r[:, :, ic * 128:(ic + 1) * 128],
                  y_w[ic % 2])
            b.dma("sync", wu_dst, mu_r[:, :, ic * 128:(ic + 1) * 128],
                  y_w[ic % 2])
            if not last:
                if ic % 2 == 0 and ic // 2 < NFC:
                    wq_down["load"](ic // 2)
                elif ic % 2 == 1 and ic // 2 < NFC:
                    wq_down["compute"](ic // 2)
                if ic >= 16:
                    for j_ in range(4):
                        t_ = (ic - 16) * 4 + j_
                        if t_ < NIC:
                            rec_down_transpose_one(mdc, t_)
            b.wait("tensor", y_w[ic % 2])
            for tc in range(NTC):
                idx = ic * NTC + tc
                tpar = tc % 4
                pg = ps[:, tpar * TCW:tpar * TCW + TCW]
                pu = ps[:, 2048 + tpar * TCW:2048 + tpar * TCW + TCW]
                if ic == 0:
                    b.wait("tensor", v_qa, xqb + NFC * (tc // 2 + 1))
                if idx >= 4:
                    b.wait("tensor", v_h, v_h0 + idx - 3)
                for fc in range(NFC):
                    b.mm(pg, mgut[:, par * 2048 + fc * 128:par * 2048 + fc * 128 + 128],
                         bufB[:, fc * NT + tc * TCW:fc * NT + tc * TCW + TCW],
                         fc == 0, fc == NFC - 1,
                         inc=(t_gu if fc == NFC - 1 else None))
                for fc in range(NFC):
                    b.mm(pu, mgut[:, par * 2048 + 1024 + fc * 128:
                                  par * 2048 + 1024 + fc * 128 + 128],
                         bufB[:, fc * NT + tc * TCW:fc * NT + tc * TCW + TCW],
                         fc == 0, fc == NFC - 1,
                         inc=(t_uu if fc == NFC - 1 else None))
                b.wait("scalar", t_gu, t_gu0 + idx + 1)
                b.wait("scalar", v_h, v_h0 + max(0, idx - 1))
                b.ins("scalar", "activation",
                      silu[:, (tc % 2) * TCW:(tc % 2) * TCW + TCW],
                      pg, AF.Silu, scale=scal[:, SC_SVG + ic:SC_SVG + ic + 1],
                      inc=s_si)
                b.wait("vector", s_si, s_si0 + idx + 1)
                b.wait("vector", t_uu, t_uu0 + idx + 1)
                if tc == 0 and ic >= 2:
                    b.wait("vector", g_hs[ic % 2])
                b.ins("vector", "scalar_tensor_tensor",
                      hstg[:, par * NT + tc * TCW:par * NT + tc * TCW + TCW],
                      pu, scal[:, SC_SVU + ic:SC_SVU + ic + 1],
                      silu[:, (tc % 2) * TCW:(tc % 2) * TCW + TCW],
                      ALU.mult, ALU.mult, inc=v_h)
            b.drain("vector")
            b.ins("vector", "tensor_reduce", scal[:, SC_TMP:SC_TMP + 1],
                  hstg[:, par * NT:par * NT + NT], AXX, ALU.max,
                  apply_absolute_value=True)
            b.drain("vector")
            b.ins("vector", "tensor_tensor", scal[:, SC_HMX:SC_HMX + 1],
                  scal[:, SC_HMX:SC_HMX + 1], scal[:, SC_TMP:SC_TMP + 1],
                  ALU.max, inc=v_mx)
            b.drain("vector")
            b.wait("gpsimd", v_h, v_h0 + (ic + 1) * NTC)
            b.dma("gpsimd", d["hscr"][ic * 128:(ic + 1) * 128, :],
                  hstg[:, par * NT:par * NT + NT], g_hs[ic % 2])

        t_dn0, v_dn0 = t_dn.v, v_dn.v
        h_r = d["hscr"].rearrange("(ic p) c -> p ic c", p=128)
        HQT = NIC * TCW
        wblk = [(mdc, 0), (mdc, II), (hstg, 0), (hstg, II), (mgut, 0),
                (bufB, 2 * HQT), (bufB, 2 * HQT + II), (bufB, 2 * HQT + 2 * II)]
        for eng in ("sync", "gpsimd"):
            b.wait(eng, t_uu, t_uu0 + NIC * NTC)
            b.wait(eng, v_mx)
            b.wait(eng, y_m[0])
            b.wait(eng, y_m[1])
            b.wait(eng, g_hs[0])
            b.wait(eng, g_hs[1])
            b.wait(eng, g_dt[0])
            b.wait(eng, g_dt[1])
        for fco in range(NFC):
            bt, c0 = wblk[fco]
            if fco % 2 == 0:
                b.dma("sync", bt[:, c0:c0 + II], d["mqdC"][fco], y_w[0])
            else:
                b.dma("gpsimd", bt[:, c0:c0 + II], d["mqdC"][fco], g_dw)
        for s in (0, 1):
            dstS = bufB[:, s * HQT:s * HQT + HQT].rearrange(
                "p (ic j) -> p ic j", j=TCW)
            b.dma("sync", dstS, h_r[:, :, s * TCW:(s + 1) * TCW], y_la[s])

        rec_allreduce_scale(SC_HMX, SC_RH, SC_SH, 1)
        b.ins("vector", "tensor_scalar", scal[:, SC_SVD:SC_SVD + NFC],
              scal[:, SC_WD:SC_WD + NFC], scal[:, SC_SH:SC_SH + 1],
              0.5 / QMAXF, ALU.mult, ALU.mult)
        b.drain("vector")

        qb = [0] * NTC

        def stage_quant(s):
            ssem = y_la[s] if s < 2 else g_hr3[s % 2]
            b.wait("vector", ssem)
            b.wait("scalar", ssem)
            qb[s] = v_qa.v
            rec_quant_inplace([(bufB, (s % 2) * HQT, HQT)], SC_RH,
                              wfscr, fint, fint2, v_hq)

        stage_quant(0)
        for tc in range(NTC):
            par = tc % 2
            for ic in range(NIC):
                if ic % 2 == 0:
                    b.wait("tensor", v_qa, qb[tc] + ic // 2 + 1)
                for fco in range(NFC):
                    bt, c0 = wblk[fco]
                    if ic == 0:
                        if tc == 0:
                            if fco == 0:
                                b.wait("tensor", y_w[0])
                                b.wait("tensor", g_dw)
                                b.wait("tensor", s_si, s_si0 + NIC * NTC)
                                b.wait("tensor", v_h, v_h0 + NIC * NTC)
                        else:
                            b.wait("tensor", v_dn,
                                   v_dn0 + (tc - 1) * NFC + fco + 1)
                    lastmm = ic == NIC - 1
                    b.mm(ps[:, fco * TCW:fco * TCW + TCW],
                         bt[:, c0 + ic * 128:c0 + ic * 128 + 128],
                         bufB[:, par * HQT + ic * TCW:par * HQT + ic * TCW + TCW],
                         ic == 0, lastmm, inc=(t_dn if lastmm else None))
            for fco in range(NFC):
                b.wait("vector", t_dn, t_dn0 + tc * NFC + fco + 1)
                pd = ps[:, fco * TCW:fco * TCW + TCW]
                xsl = x_ch(fco)[:, tc * TCW:(tc + 1) * TCW]
                if not last:
                    b.ins("vector", "scalar_tensor_tensor", xsl, pd,
                          scal[:, SC_SVD + fco:SC_SVD + fco + 1], xsl,
                          ALU.mult, ALU.add, inc=v_dn)
                    b.drain("vector")
                else:
                    u_ = tc * NFC + fco
                    opar = u_ % 3
                    if u_ >= 3:
                        b.wait("vector", y_o[opar], 16 * ((u_ - 3) // 3 + 1))
                    b.ins("vector", "scalar_tensor_tensor",
                          outst[:, opar * TCW:opar * TCW + TCW], pd,
                          scal[:, SC_SVD + fco:SC_SVD + fco + 1], xsl,
                          ALU.mult, ALU.add, inc=v_dn)
                    b.drain("vector")
                    b.wait("sync", v_dn, v_dn0 + u_ + 1)
                    b.dma("sync", d["outT"][fco * 128:(fco + 1) * 128,
                                            tc * TCW:(tc + 1) * TCW],
                          outst[:, opar * TCW:opar * TCW + TCW],
                          y_o[opar])
            if tc + 2 < NTC:
                b.wait("gpsimd", t_dn, t_dn0 + (tc + 1) * NFC)
                dstS = bufB[:, (tc % 2) * HQT:(tc % 2) * HQT + HQT
                            ].rearrange("p (ic j) -> p ic j", j=TCW)
                b.dma("gpsimd", dstS,
                      h_r[:, :, (tc + 2) * TCW:(tc + 3) * TCW],
                      g_hr3[tc % 2])
            if tc + 1 < NTC:
                stage_quant(tc + 1)
        b.barrier(bar)
        fs.close()

    def pt_col(i):
        return i * 1040

    def rec_attn():
        as_ = ExitStack()
        sqpp = as_.enter_context(nc.sbuf_tensor("a_sq", [128, 2 * TCW], BF16))
        m2rep = as_.enter_context(nc.sbuf_tensor("a_m2", [128, 2 * TCW], F32))
        rsrep = as_.enter_context(nc.sbuf_tensor("a_rs", [128, 2 * TCW], F32))
        ascr = as_.enter_context(nc.sbuf_tensor("a_scr", [128, 512], F32))
        wqkt = as_.enter_context(nc.sbuf_tensor("a_wqk", [128, 2 * 1024], BF16))
        wvt = as_.enter_context(nc.sbuf_tensor("a_wv", [128, 8 * 512], BF16))
        vS = as_.enter_context(nc.sbuf_tensor("a_v", [128, NPT * 1040], BF16))
        Epp = as_.enter_context(nc.sbuf_tensor("a_e", [128, 2 * PL], BF16))
        arr = as_.enter_context(nc.sbuf_tensor("a_rr", [128, PL], F32))
        Osb = as_.enter_context(nc.sbuf_tensor("a_o", [128, 2 * PL], BF16))
        t2b = as_.enter_context(nc.sbuf_tensor("a_t2", [128, PL], BF16))
        cv64 = as_.enter_context(nc.sbuf_tensor("a_cv64", [128, PL], BF16))

        b.dma("sync", cv64[64:65, :], d["cvec"][0:1, :], y_cn)
        rec_ln_stats(sqpp)
        wq_r = d["wqp"].rearrange("(fc p) c -> p fc c", p=128)
        wk_r = d["wkp"].rearrange("(fc p) c -> p fc c", p=128)
        wv_r = d["wvp"].rearrange("(fc p) c -> p fc c", p=128)
        Y2C, QSC, KSC = 0, NFC * PL, 2 * NFC * PL

        for s in range(3):
            qlen = QLEN[s]
            seg = SEG[s]
            if s == 0:
                rec_ln_apply(m2rep, rsrep, ascr, seg, qlen, Y2C, PL)
            b.wait("vector", v_ln)
            qk_list = [(li, wr, outc, qd)
                       for li, (wr, outc) in enumerate(((wq_r, QSC),
                                                        (wk_r, KSC)))
                       for qd in range(NFC)]

            def qk_load(j):
                li, wr, outc, qd = qk_list[j]
                par = j % 2
                if j >= 2 or s >= 1:
                    b.wait("gpsimd", t_qk)
                wdst = wqkt[:, par * 1024:par * 1024 + 1024].rearrange(
                    "p (fc j) -> p fc j", j=128)
                b.dma("gpsimd", wdst, wr[:, :, qd * 128:(qd + 1) * 128],
                      g_qk[par])

            qk_load(0)
            qk_load(1)
            for j, (li, wr, outc, qd) in enumerate(qk_list):
                par = j % 2
                b.wait("tensor", g_qk[par])
                b.wait("tensor", v_ln)
                for ci, (q0, qw) in enumerate(qchunks(qlen)):
                    ppar = ci % 2
                    pq = ps[:, ppar * TCW:ppar * TCW + TCW]
                    b.wait("tensor", v_qk)
                    for fc in range(NFC):
                        b.mm(pq[:, 0:qw],
                             wqkt[:, par * 1024 + fc * 128:
                                  par * 1024 + fc * 128 + 128],
                             bufB[:, Y2C + fc * PL + q0:Y2C + fc * PL + q0 + qw],
                             fc == 0, fc == NFC - 1,
                             inc=(t_qk if fc == NFC - 1 else None))
                    b.wait("vector", t_qk)
                    b.ins("vector", "tensor_copy",
                          bufB[:, outc + qd * PL + q0:outc + qd * PL + q0 + qw],
                          pq[:, 0:qw], inc=v_qk)
                if j + 2 < len(qk_list):
                    qk_load(j + 2)
            if s >= 1:
                b.ins("vector", "memset",
                      strided_cols(bufB, KSC + 1365, PL, NFC), 0.0)
                b.drain("vector")
            b.ins("vector", "memset", vS[64:128, 10 * 1040:11 * 1040], 0.0)
            b.drain("vector")
            b.ins("vector", "memset",
                  strided_cols(vS, 64, 65, NPT * 16), 1.0, inc=v_qk)
            b.drain("vector")
            def vbuf(vc, fc):
                if vc == 0:
                    return wvt[:, fc * TCW:(fc + 1) * TCW]
                bt = Epp if fc < 4 else Osb
                return bt[:, (fc % 4) * TCW:(fc % 4) * TCW + TCW]

            for vc in range(2):
                for fc in range(NFC):
                    b.dma("gpsimd", vbuf(vc, fc),
                          wv_r[:, fc, vc * TCW:(vc + 1) * TCW],
                          g_wv if vc == 0 else g_wv2)
            for vc in range(2):
                b.wait("tensor", g_wv if vc == 0 else g_wv2)
                for pt in range(NPT):
                    sw = min(128, qlen - pt * 128)
                    pv = ps[:, 1024 + (pt % 2) * TCW:1024 + (pt % 2) * TCW + TCW]
                    b.wait("tensor", v_qk)
                    for fc in range(NFC):
                        b.mm(pv[0:sw, :],
                             bufB[:, Y2C + fc * PL + pt * 128:
                                  Y2C + fc * PL + pt * 128 + sw],
                             vbuf(vc, fc),
                             fc == 0, fc == NFC - 1,
                             inc=(t_qk if fc == NFC - 1 else None))
                    b.wait("vector", t_qk)
                    vdst = vS[0:sw, pt_col(pt) + 65 * 8 * vc:
                              pt_col(pt) + 65 * 8 * vc + 520].rearrange(
                        "p (t j) -> p t j", j=65)[:, :, 0:64]
                    b.ins("vector", "tensor_copy", vdst,
                          pv[0:sw, :].rearrange("p (t j) -> p t j", j=64),
                          inc=v_qk)
            b.wait("tensor", v_qk)
            if s + 1 < 3:
                b.wait("vector", t_qk)
                rec_ln_apply(m2rep, rsrep, ascr, SEG[s + 1], QLEN[s + 1],
                             Y2C, PL)
            strips = [(0, 1024), (1024, qlen - 1024)]
            pending = []
            for h in range(NH):
                ch = h // 2
                poff = 64 * (h % 2)
                opar = h % 2
                osb = Osb[0:65, opar * PL:opar * PL + PL]
                for si_, (q0, qw) in enumerate(strips):
                    if si_ == 1 and pending:
                        pending.pop(0)()
                    s_ex0 = s_ex.v
                    AVp = ps[0:65, 2048:2048 + qw]

                    def rec_scores(i, q0=q0, qw=qw, s_ex0=s_ex0):
                        pw = 128 if i < NPT - 1 else 86
                        spar = i % 2
                        b.wait("tensor", s_ex, s_ex0 + max(0, i - 1))
                        Sp = ps[:, spar * 1024:spar * 1024 + 1024]
                        lo = max(q0, 128 * i - 2)
                        hi = min(q0 + qw, 128 * i + 130)
                        bp = []
                        c = lo
                        while c < hi:
                            nxt = min(hi, q0 + ((c - q0) // 512 + 1) * 512)
                            bp.append((c, nxt))
                            c = nxt
                        bbank = set((c0 - q0) // 512 for c0, _ in bp)
                        mms = []
                        for lq0, lqw in qchunks(qw):
                            mms.append((
                                Sp[0:pw, lq0:lq0 + lqw],
                                bufB[poff:poff + 64,
                                     KSC + ch * PL + 128 * i:
                                     KSC + ch * PL + 128 * i + pw],
                                bufB[poff:poff + 64,
                                     QSC + ch * PL + q0 + lq0:
                                     QSC + ch * PL + q0 + lq0 + lqw],
                                True, (lq0 // 512) not in bbank))
                        for c0, c1 in bp:
                            mms.append((
                                Sp[0:pw, c0 - q0:c1 - q0], cident[0:pw, 0:pw],
                                cband[0:pw, c0 - (128 * i - 2):
                                      c1 - (128 * i - 2)],
                                False, True))
                        for mj, (o_, l_, r_, st, sp) in enumerate(mms):
                            b.mm(o_, l_, r_, st, sp,
                                 inc=(t_sc if mj == len(mms) - 1 else None))

                    def rec_exp(i, qw=qw):
                        pw = 128 if i < NPT - 1 else 86
                        spar = i % 2
                        b.wait("scalar", t_av)
                        b.wait("scalar", t_sc)
                        b.ins("scalar", "activation",
                              Epp[0:pw, (i % 2) * PL:(i % 2) * PL + qw],
                              ps[0:pw, spar * 1024:spar * 1024 + qw], AF.Exp,
                              scale=scal[0:pw, SC_Q8:SC_Q8 + 1], inc=s_ex)

                    def rec_av(i, q0=q0, qw=qw, s_ex0=s_ex0, AVp=AVp):
                        pw = 128 if i < NPT - 1 else 86
                        b.wait("tensor", s_ex, s_ex0 + i + 1)
                        mms = []
                        if i == 0:
                            b.wait("tensor", v_oe)
                        for lq0, lqw in qchunks(qw):
                            mms.append((
                                AVp[:, lq0:lq0 + lqw],
                                vS[0:pw, pt_col(i) + 65 * h:
                                   pt_col(i) + 65 * h + 65],
                                Epp[0:pw, (i % 2) * PL + lq0:
                                    (i % 2) * PL + lq0 + lqw],
                                i == 0, i == NPT - 1))
                        for mj, (o_, l_, r_, st, sp) in enumerate(mms):
                            b.mm(o_, l_, r_, st, sp,
                                 inc=(t_av if mj == len(mms) - 1 else None))

                    for i in range(NPT):
                        rec_scores(i)
                        rec_exp(i)
                        if i >= 1:
                            rec_av(i - 1)
                    rec_av(NPT - 1)
                    b.wait("vector", t_av)
                    b.ins("vector", "tensor_copy", osb[:, q0:q0 + qw],
                          AVp[:, 0:qw], inc=v_oe)
                    b.drain("vector")
                b.wait("vector", y_cn)
                b.ins("vector", "tensor_tensor", t2b[64:65, 0:qlen],
                      osb[64:65, 0:qlen], cv64[64:65, 0:qlen], ALU.add)
                b.drain("vector")
                b.wait("vector", g_row)
                b.ins("vector", "reciprocal", rows[64:65, 0:qlen],
                      t2b[64:65, 0:qlen], inc=v_rc)
                b.wait("sync", v_rc)
                b.dma("sync", d["rows"][2:3, 0:qlen], rows[64:65, 0:qlen],
                      g_row)
                b.wait("sync", g_row)
                b.wait("sync", v_rc)
                b.dma("sync", arr[0:64, 0:qlen],
                      row_bcast_ap(2, 0, 64, qlen), y_rr)

                def chain_tail(h=h, ch=ch, poff=poff, osb=osb, qlen=qlen,
                               seg=seg):
                    b.wait("vector", y_rr)
                    b.wait("vector", g_xa)
                    b.ins("vector", "tensor_tensor", t2b[0:64, 0:qlen],
                          osb[0:64, 0:qlen], arr[0:64, 0:qlen], ALU.mult)
                    b.drain("vector")
                    xsl = x_ch(ch)[poff:poff + 64, seg:seg + qlen]
                    if poff == 0:
                        b.ins("vector", "tensor_tensor", xsl, xsl,
                              t2b[0:64, 0:qlen], ALU.add, inc=v_xa)
                        b.drain("vector")
                    else:
                        b.mark("vector", v_rc)
                        b.wait("gpsimd", v_rc)
                        b.ins("gpsimd", "dma_start", out=xsl,
                              in_=t2b[0:64, 0:qlen], accum_op=ALU.add,
                              inc=g_xa, incn=16)
                pending.append(chain_tail)
                if s >= 1 and h in (3, 7, 11):
                    k3 = (3, 7, 11).index(h)
                    sseg, sql = SEG[s - 1], QLEN[s - 1]
                    c0s, cws = qchunks(sql)[k3]
                    rec_stats_chunk(sqpp, sseg + c0s, cws)
            while pending:
                pending.pop(0)()
            b.wait("gpsimd", g_xa)
            if s == 2:
                b.wait("tensor", g_xa)
                b.wait("vector", v_xa)
                for c0s, cws in qchunks(qlen):
                    rec_stats_chunk(sqpp, seg + c0s, cws)
                b.barrier(bar)
            else:
                for e_, sm_ in (("gpsimd", t_av), ("gpsimd", v_xa),
                                ("gpsimd", v_rc), ("tensor", s_ex),
                                ("tensor", v_oe), ("vector", t_av)):
                    b.wait(e_, sm_)
        as_.close()

    rec_ffn(False)
    rec_attn()
    rec_ffn(True)
    for s4 in y_o:
        b.wait("sync", s4)
    b.barrier(bar)

    b.emit()
    stk.close()
    return nc


_PERM = np.concatenate([np.arange(s, NT, 3) for s in range(3)])
_IPERM = np.argsort(_PERM)


def _consts():
    bf = ml_dtypes.bfloat16
    band = np.zeros((128, BANDW), np.float32)
    for p in range(128):
        band[p, p:p + 5] = 8.0 * np.log(2.0)
    cv = np.zeros((1, PL), np.float32)
    cv[0, 0], cv[0, 1], cv[0, PL - 2], cv[0, PL - 1] = 2, 1, 1, 2

    return {
        "cband": band.astype(bf),
        "cident": np.eye(128, dtype=np.float32).astype(bf),
        "cvec": cv.astype(bf),
                "cones": np.ones((128, 1), np.float32).astype(bf),
    }


def kernel(L0_attn_x, L0_attn_W_q, L0_attn_W_k, L0_attn_W_v,
           L0_ffn_clip_x, L0_ffn_clip_down,
           L0_ffn_w_gate, L0_ffn_w_up, L0_ffn_w_down):
    nc = build_nc()
    x = np.asarray(L0_attn_x, dtype=np.float32)
    shared = {
        "wqp": np.ascontiguousarray(np.asarray(L0_attn_W_q, np.float32)),
        "wkp": np.ascontiguousarray(np.asarray(L0_attn_W_k, np.float32)),
        "wvp": np.ascontiguousarray(np.asarray(L0_attn_W_v, np.float32)),
        "wg_row": np.ascontiguousarray(np.asarray(L0_ffn_w_gate, np.float32)),
        "wu_row": np.ascontiguousarray(np.asarray(L0_ffn_w_up, np.float32)),
        "wd_row": np.ascontiguousarray(np.asarray(L0_ffn_w_down, np.float32)),
        "clips": np.array([[float(np.asarray(L0_ffn_clip_x).reshape(-1)[0]),
                            float(np.asarray(L0_ffn_clip_down).reshape(-1)[0])]],
                          np.float32),
    }
    shared.update(_consts())
    in_maps = []
    for bi in range(8):
        m = dict(shared)
        m["xT"] = np.ascontiguousarray(x[bi][_PERM].T)
        in_maps.append(m)
    res = run_bass_kernel_spmd(nc, in_maps, core_ids=list(range(8)))
    out = np.empty((8, NT, D), np.float32)
    for bi in range(8):
        out[bi] = res.results[bi]["outT"].T[_IPERM]
    return out


# version 33 (final)
# speedup vs baseline: 1.0454x; 1.0349x over previous
import numpy as np
import ml_dtypes
import concourse.bass as bass
import concourse.mybir as mybir
from concourse.bass_utils import run_bass_kernel_spmd
from contextlib import ExitStack

F32 = mybir.dt.float32
BF16 = mybir.dt.bfloat16
AF = mybir.ActivationFunctionType
ALU = mybir.AluOpType
AXX = mybir.AxisListType.X

D = 1024
NFC = 8
NT = 4096
TCW = 512
NTC = NT // TCW
II = 2816
NIC = II // 128
NH = 16
PL = 1366
NPT = 11
QLEN = (1366, 1365, 1365)
SEG = (0, 1366, 2731)
EPS = 1e-5
QMAXF = 127.0
BANDW = 136
ENGINES = ("tensor", "vector", "scalar", "gpsimd", "sync")


def qchunks(n):
    out, c = [], 0
    while c < n:
        out.append((c, min(512, n - c)))
        c += 512
    return out


class Sem:
    __slots__ = ("h", "v", "name")

    def __init__(self, h, name):
        self.h, self.v, self.name = h, 0, name


class B:
    def __init__(self, nc, stk):
        self.nc = nc
        self.stk = stk
        self.q = {k: [] for k in ENGINES}

    def sem(self, name):
        return Sem(self.stk.enter_context(self.nc.semaphore(name)), name)

    def ins(self, eng, meth, *args, inc=None, incn=1, **kw):
        self.q[eng].append((meth, args, kw, inc.h if inc else None, incn))
        if inc is not None:
            inc.v += incn

    def wait(self, eng, sem, val=None):
        v = sem.v if val is None else val
        assert v <= sem.v, f"wait beyond recorded increments: {sem.name} {v}>{sem.v}"
        self.q[eng].append(("wait_ge", (sem.h, v), {}, None, 0))

    def dma(self, eng, out, in_, sem):
        self.ins(eng, "dma_start", inc=sem, incn=16, out=out, in_=in_)

    def mm(self, out, lhsT, rhs, start, stop, inc=None):
        self.ins("tensor", "matmul", out, lhsT, rhs,
                 start=start, stop=stop, inc=inc)

    def drain(self, eng):
        self.q[eng].append(("drain", (), {}, None, 0))

    def mark(self, eng, sem):
        self.drain(eng)
        self.ins(eng, "nop", inc=sem)

    def barrier(self, sem):
        for eng in ENGINES:
            self.mark(eng, sem)
        tgt = sem.v
        for eng in ENGINES:
            self.wait(eng, sem, tgt)

    def emit(self):
        with self.nc.Block() as blk:
            for eng in ENGINES:
                getattr(blk, eng)(self._runner(eng))

    def _runner(self, eng):
        entries = self.q[eng]

        def run(e):
            for meth, args, kw, inch, incn in entries:
                r = getattr(e, meth)(*args, **kw)
                if inch is not None:
                    r.then_inc(inch, incn)
        return run


def strided_cols(t, start, step, count, width=1, nrows=128, row0=0):
    ap = t[row0:row0 + nrows, start:start + width]
    return bass.AP(ap.tensor, ap.offset, [ap.ap[0], [step, count], [1, width]])


def build_nc(ncores=8):
    nc = bass.Bass("TRN2", target_bir_lowering=False)
    d = {}
    for nm, shp, dt, kind in (
            ("xT", [D, NT], F32, "ExternalInput"),
            ("wqp", [D, D], F32, "ExternalInput"),
            ("wkp", [D, D], F32, "ExternalInput"),
            ("wvp", [D, D], F32, "ExternalInput"),
            ("wg_row", [II, D], F32, "ExternalInput"),
            ("wu_row", [II, D], F32, "ExternalInput"),
            ("wd_row", [D, II], F32, "ExternalInput"),
            ("clips", [1, 2], F32, "ExternalInput"),
            ("cband", [128, BANDW], BF16, "ExternalInput"),
            ("cident", [128, 128], BF16, "ExternalInput"),
            ("cvec", [1, PL], BF16, "ExternalInput"),
            ("cones", [128, 1], BF16, "ExternalInput"),
            ("outT", [D, NT], F32, "ExternalOutput"),
            ("mqg", [D, II], BF16, "Internal"),
            ("mqu", [D, II], BF16, "Internal"),
            ("mqgR", [II, D], BF16, "Internal"),
            ("mquR", [II, D], BF16, "Internal"),
            ("mqdR", [D, II], BF16, "Internal"),
            ("mqdC", [NFC, 128, NIC * 128], BF16, "Internal"),
            ("hscr", [II, NT], BF16, "Internal"),
            ("cc_in", [1, 128], F32, "Internal"),
            ("cc_out", [1, 128], F32, "Internal"),
            ("rows", [5, NT], F32, "Internal")):
        d[nm] = nc.dram_tensor(nm, shp, dt, kind=kind).ap()

    stk = ExitStack()
    b = B(nc, stk)

    g_x = b.sem("g_x")
    g_wr = (b.sem("g_wra"), b.sem("g_wrb"))
    g_wt2 = (b.sem("g_wta"), b.sem("g_wtb"))
    g_tp = (b.sem("g_tp0"), b.sem("g_tp1"))
    g_dw = b.sem("g_dw")
    g_dt = (b.sem("g_dt0"), b.sem("g_dt1"))
    t_cc = b.sem("t_cc")
    g_cc = b.sem("g_cc")
    g_hs = (b.sem("g_hsa"), b.sem("g_hsb"))
    g_hr3 = (b.sem("g_hr0"), b.sem("g_hr1"), b.sem("g_hr2"))
    g_row = b.sem("g_row")
    g_qk = (b.sem("g_qka"), b.sem("g_qkb"))
    g_wv = b.sem("g_wv")
    g_wv2 = b.sem("g_wv2")
    y_w = (b.sem("y_wa"), b.sem("y_wb"))
    y_o = (b.sem("y_oa"), b.sem("y_ob"), b.sem("y_oc"))
    y_cn = b.sem("y_cn")
    y_la = (b.sem("y_laa"), b.sem("y_lab"))
    y_sc = b.sem("y_sc")
    y_rr = b.sem("y_rr")
    y_m = (b.sem("y_ma"), b.sem("y_mb"))
    c_cc = b.sem("c_cc")
    t_st = b.sem("t_st")
    v_st = b.sem("v_st")
    s_sq = b.sem("s_sq")
    t_squ = b.sem("t_squ")
    v_ln = b.sem("v_ln")
    t_gu = b.sem("t_gu")
    t_uu = b.sem("t_uu")
    s_si = b.sem("s_si")
    v_h = b.sem("v_h")
    v_mx = b.sem("v_mx")
    t_wfree = b.sem("t_wfree")
    t_dn = b.sem("t_dn")
    v_dn = b.sem("v_dn")
    v_hq = b.sem("v_hq")
    v_qa = b.sem("v_qa")
    s_cv = b.sem("s_cv")
    t_qk = b.sem("t_qk")
    v_qk = b.sem("v_qk")
    t_sc = b.sem("t_sc")
    s_ex = b.sem("s_ex")
    t_av = b.sem("t_av")
    v_rc = b.sem("v_rc")
    v_oe = b.sem("v_oe")
    v_xa = b.sem("v_xa")
    g_xa = b.sem("g_xa")
    v_wr = b.sem("v_wr")
    v_wq = b.sem("v_wq")
    g_wb = b.sem("g_wb")
    bar = b.sem("bar")

    xs = stk.enter_context(nc.sbuf_tensor("xs", [128, NFC * NT], BF16))
    bufB = stk.enter_context(nc.sbuf_tensor("bufB", [128, 3 * NFC * PL], BF16))
    rows = stk.enter_context(nc.sbuf_tensor("rows_sb", [128, 1680], F32))
    scal = stk.enter_context(nc.sbuf_tensor("scal", [128, 160], F32))
    cband = stk.enter_context(nc.sbuf_tensor("cband_sb", [128, BANDW], BF16))
    cident = stk.enter_context(nc.sbuf_tensor("cident_sb", [128, 128], BF16))
    cones = stk.enter_context(nc.sbuf_tensor("cones_sb", [128, 1], BF16))
    ps = stk.enter_context(nc.psum_tensor("ps", [128, 4096], F32))

    def x_ch(fc):
        return xs[:, fc * NT:(fc + 1) * NT]

    SC_RX, SC_SX, SC_RH, SC_SH, SC_HMX, SC_TMP = 0, 1, 2, 3, 4, 5
    SC_WG, SC_WU, SC_WD = 8, 30, 52
    SC_SVG, SC_SVU, SC_SVD = 60, 82, 104
    SC_RTMP = 112
    SC_Q8 = 135
    SC_HALF = 7

    b.dma("sync", cband[:, :], d["cband"][:, :], y_cn)
    b.dma("sync", cident[:, :], d["cident"][:, :], y_cn)
    b.dma("sync", cones[:, :], d["cones"][:, :], y_cn)
    b.dma("sync", rows[0:1, 1540:1542], d["clips"][:, :], y_cn)
    for fc in range(NFC):
        b.dma("gpsimd", x_ch(fc), d["xT"][fc * 128:(fc + 1) * 128, :], g_x)

    b.ins("vector", "memset", scal[:, SC_Q8:SC_Q8 + 1], 0.125)
    b.drain("vector")
    b.ins("vector", "memset", scal[:, SC_HALF:SC_HALF + 1], 0.5)
    b.drain("vector")
    b.wait("tensor", y_cn)
    b.wait("scalar", y_cn)

    def row_bcast_ap(r, col, nrows, width):
        return bass.AP(d["rows"].tensor, d["rows"].offset + r * NT + col,
                       [[0, nrows], [1, width]])

    def rec_weight_quant(wfscr, wq2, m2rep, fint, fint2, mdc, hstg,
                         hooks, down_out=None):
        chunks = []
        for wi, spec in enumerate((("wg_row", "mqgR", NIC, D, SC_WG),
                                   ("wu_row", "mquR", NIC, D, SC_WU),
                                   ("wd_row", "mqdR", NFC, II, SC_WD))):
            for ic in range(spec[2]):
                chunks.append((wi,) + spec + (ic,))
        NCH = len(chunks)
        tasks = []

        def load(K):
            wi, rowt, outR, nch, wdt, scol, ic = chunks[K]
            par = K % 2
            if wdt > 1024:
                half = wdt // 2
                for eng in ("sync", "gpsimd"):
                    b.wait(eng, v_wq)
                    b.wait(eng, s_cv)
                    b.wait(eng, v_qa)
                b.dma("sync", wfscr[:, 0:half],
                      d[rowt][ic * 128:(ic + 1) * 128, 0:half], g_wt2[0])
                b.dma("gpsimd", wfscr[:, half:wdt],
                      d[rowt][ic * 128:(ic + 1) * 128, half:wdt], g_dw)
                return
            wrow = wfscr[:, 0:wdt] if par == 0 else wq2[:, 0:wdt]
            eng = "sync" if par == 0 else "gpsimd"
            if K >= 2:
                b.wait(eng, v_wq)
                b.wait(eng, s_cv)
            else:
                b.wait(eng, v_mx)
            b.dma(eng, wrow, d[rowt][ic * 128:(ic + 1) * 128, :],
                  g_wt2[par])

        def compute(K):
            wi, rowt, outR, nch, wdt, scol, ic = chunks[K]
            par = K % 2
            wide = wdt > 1024
            wrow = wfscr[:, 0:wdt] if (par == 0 or wide) else wq2[:, 0:wdt]
            qsl = mdc[:, par * II:par * II + wdt]
            rt = scal[:, SC_RTMP + par:SC_RTMP + par + 1]
            sc = scal[:, scol + ic:scol + ic + 1]
            if wide:
                b.wait("vector", g_wt2[0])
                b.wait("vector", g_dw)
            else:
                b.wait("vector", g_wt2[par])
            b.ins("vector", "tensor_reduce", sc, wrow, AXX, ALU.max,
                  apply_absolute_value=True)
            b.drain("vector")
            b.ins("vector", "tensor_single_scalar", rt, sc, 1.0 / QMAXF,
                  ALU.mult)
            b.drain("vector")
            b.ins("vector", "reciprocal", rt, rt, inc=v_wr)
            b.drain("vector")
            b.wait("vector", y_m[par])
            b.wait("scalar", v_wr)
            nsub = (wdt + 511) // 512
            qa0, sc0 = v_qa.v, s_cv.v
            for si in range(nsub):
                c0 = si * 512
                cw = min(512, wdt - c0)
                sp = si % 2
                u = m2rep[:, sp * 512:sp * 512 + cw]
                ti = (fint if sp == 0 else fint2)[:, 0:cw]
                if si >= 2:
                    b.wait("scalar", v_qa, qa0 + si - 1)
                b.ins("scalar", "activation", u, wrow[:, c0:c0 + cw],
                      AF.Identity, scale=rt,
                      bias=scal[:, SC_HALF:SC_HALF + 1])
                b.drain("scalar")
                b.ins("scalar", "copy", ti, u, inc=s_cv)
                b.drain("scalar")
                b.wait("vector", s_cv, sc0 + si + 1)
                b.ins("vector", "tensor_tensor", u, ti, u, ALU.is_gt)
                b.drain("vector")
                b.ins("vector", "tensor_tensor", qsl[:, c0:c0 + cw], ti, u,
                      ALU.subtract,
                      inc=(v_wq if si == nsub - 1 else v_qa))
                if si == nsub - 1:
                    b.mark("vector", v_qa)
            b.wait("sync", v_wq)
            b.dma("sync", d[outR][ic * 128:(ic + 1) * 128, :], qsl, y_m[par])
            if wi == 0 and ic == nch - 1:
                b.wait("sync", y_m[0])
                b.wait("sync", y_m[1])
                tasks.extend((outR, "mqg", fc) for fc in range(NFC))

        def transpose_task():
            outR, outT, fc = tasks.pop(0)
            tt = hstg[:, (fc % 2) * II:(fc % 2) * II + II]
            b.wait("sync", g_hs[fc % 2])
            b.ins("sync", "dma_start", out=tt,
                  in_=d[outR][:, fc * 128:(fc + 1) * 128],
                  transpose=True, inc=g_tp[fc % 2], incn=16)
            b.wait("gpsimd", g_tp[fc % 2])
            b.dma("gpsimd", d[outT][fc * 128:(fc + 1) * 128, :], tt,
                  g_hs[fc % 2])

        if down_out is not None:
            down_out["load"] = lambda k: load(44 + k)
            down_out["compute"] = lambda k: compute(44 + k)
            NCH = 44
        b.wait("sync", v_mx)
        load(0)
        load(1)
        for K in range(NCH):
            if chunks[K][4] > 1024:
                load(K)
            compute(K)
            if tasks:
                transpose_task()
            if K == 43:
                b.wait("sync", y_m[0])
                b.wait("sync", y_m[1])
                tasks.extend(("mquR", "mqu", fc) for fc in range(NFC))
                while tasks:
                    transpose_task()
            for cb in hooks.get(K, ()):
                cb()
            if K + 2 < NCH and chunks[K + 2][4] <= 1024:
                load(K + 2)
        assert not tasks
        for e in ("sync", "vector"):
            b.wait(e, g_hs[0])
            b.wait(e, g_hs[1])

    def rec_down_transpose_one(mdc, ic2):
        tt = mdc[:, (ic2 % 2) * II:(ic2 % 2) * II + D]
        if ic2 < 2:
            b.wait("sync", y_m[0])
            b.wait("sync", y_m[1])
        else:
            b.wait("sync", g_dt[ic2 % 2])
        b.ins("sync", "dma_start", out=tt,
              in_=d["mqdR"][:, ic2 * 128:(ic2 + 1) * 128],
              transpose=True, inc=g_tp[ic2 % 2], incn=16)
        b.wait("gpsimd", g_tp[ic2 % 2])
        dst = bass.AP(d["mqdC"].tensor, d["mqdC"].offset + ic2 * 128,
                      [[II, 128], [128 * II, NFC], [1, 128]])
        b.dma("gpsimd", dst, tt.rearrange("p (fco j) -> p fco j", j=128),
              g_dt[ic2 % 2])

    def rec_ln_stats(sqpp):
        v_st0 = v_st.v
        for tc in range(NTC):
            su = ps[0:1, (tc % 2) * TCW:(tc % 2) * TCW + TCW]
            sq = ps[0:1, 1024 + (tc % 2) * TCW:1024 + (tc % 2) * TCW + TCW]
            if tc >= 2:
                b.wait("tensor", v_st, v_st0 + tc - 1)
            for fc in range(NFC):
                xsl = x_ch(fc)[:, tc * TCW:(tc + 1) * TCW]
                sqs = sqpp[:, (fc % 2) * TCW:(fc % 2) * TCW + TCW]
                if fc >= 2 or tc > 0:
                    b.wait("scalar", t_squ)
                b.ins("scalar", "activation", sqs, xsl, AF.Square, inc=s_sq)
                b.wait("tensor", s_sq)
                if fc < NFC - 1:
                    b.mm(su, cones[:, :], xsl, fc == 0, False)
                    b.mm(sq, cones[:, :], sqs, fc == 0, False, inc=t_squ)
                else:
                    b.mm(sq, cones[:, :], sqs, False, True, inc=t_squ)
                    b.mm(su, cones[:, :], xsl, False, True, inc=t_st)
            b.wait("vector", t_st)
            b.wait("vector", g_row)
            r0 = rows[0:1, 0:TCW]
            r1 = rows[0:1, TCW:2 * TCW]
            r2 = rows[0:1, 2 * TCW:3 * TCW]
            b.ins("vector", "tensor_single_scalar", r0, su, 1.0 / D, ALU.mult)
            b.ins("vector", "tensor_single_scalar", r2, sq, 1.0 / D, ALU.mult)
            b.drain("vector")
            b.ins("vector", "tensor_tensor", r1, r0, r0, ALU.mult)
            b.drain("vector")
            b.ins("vector", "tensor_tensor", r2, r2, r1, ALU.subtract)
            b.drain("vector")
            b.ins("vector", "tensor_single_scalar", r2, r2, EPS, ALU.add,
                  inc=v_ln)
            b.drain("vector")
            b.wait("scalar", v_ln)
            b.ins("scalar", "activation", r2, r2, AF.Sqrt, inc=s_sq)
            b.wait("vector", s_sq)
            b.ins("vector", "reciprocal", r1, r2)
            b.drain("vector")
            b.ins("vector", "tensor_tensor", r0, r0, r1, ALU.mult, inc=v_st)
            b.drain("vector")
            b.wait("sync", v_st)
            b.dma("sync", d["rows"][0:1, tc * TCW:(tc + 1) * TCW], r0, g_row)
            b.dma("sync", d["rows"][1:2, tc * TCW:(tc + 1) * TCW], r1, g_row)
        b.wait("sync", g_row)

    def rec_stats_chunk(sqpp, c0, cw):
        su = ps[0:1, 3072:3072 + cw]
        sq = ps[0:1, 3584:3584 + cw]
        s_sq0, t_squ0 = s_sq.v, t_squ.v
        b.wait("tensor", v_st)

        def sqr(fc):
            sqs = sqpp[:, (fc % 2) * TCW:(fc % 2) * TCW + cw]
            b.wait("vector", t_squ,
                   t_squ0 + max(0, fc - 1) if fc >= 2 else t_squ0)
            b.ins("vector", "tensor_tensor", sqs,
                  x_ch(fc)[:, c0:c0 + cw], x_ch(fc)[:, c0:c0 + cw],
                  ALU.mult, inc=s_sq)
            b.drain("vector")

        sqr(0)
        sqr(1)
        for fc in range(NFC):
            xsl = x_ch(fc)[:, c0:c0 + cw]
            sqs = sqpp[:, (fc % 2) * TCW:(fc % 2) * TCW + cw]
            b.wait("tensor", s_sq, s_sq0 + fc + 1)
            if fc < NFC - 1:
                b.mm(su, cones[:, :], xsl, fc == 0, False)
                b.mm(sq, cones[:, :], sqs, fc == 0, False, inc=t_squ)
            else:
                b.mm(sq, cones[:, :], sqs, False, True, inc=t_squ)
                b.mm(su, cones[:, :], xsl, False, True, inc=t_st)
            if fc + 2 < NFC:
                sqr(fc + 2)
        b.wait("vector", t_st)
        b.wait("vector", g_row)
        r0 = rows[0:1, 0:TCW]
        r1 = rows[0:1, TCW:2 * TCW]
        r2 = rows[0:1, 2 * TCW:3 * TCW]
        b.ins("vector", "tensor_single_scalar", r0[:, 0:cw], su, 1.0 / D,
              ALU.mult)
        b.ins("vector", "tensor_single_scalar", r2[:, 0:cw], sq, 1.0 / D,
              ALU.mult)
        b.drain("vector")
        b.ins("vector", "tensor_tensor", r1[:, 0:cw], r0[:, 0:cw],
              r0[:, 0:cw], ALU.mult)
        b.drain("vector")
        b.ins("vector", "tensor_tensor", r2[:, 0:cw], r2[:, 0:cw],
              r1[:, 0:cw], ALU.subtract)
        b.drain("vector")
        b.ins("vector", "tensor_single_scalar", r2[:, 0:cw], r2[:, 0:cw],
              EPS, ALU.add, inc=v_ln)
        b.drain("vector")
        b.wait("scalar", v_ln)
        b.ins("scalar", "activation", r2[:, 0:cw], r2[:, 0:cw], AF.Sqrt,
              inc=s_sq)
        b.wait("vector", s_sq)
        b.ins("vector", "reciprocal", r1[:, 0:cw], r2[:, 0:cw])
        b.drain("vector")
        b.ins("vector", "tensor_tensor", r0[:, 0:cw], r0[:, 0:cw],
              r1[:, 0:cw], ALU.mult, inc=v_st)
        b.drain("vector")
        b.wait("sync", v_st)
        b.dma("sync", d["rows"][3:4, c0:c0 + cw], r0[:, 0:cw], g_row)
        b.dma("sync", d["rows"][4:5, c0:c0 + cw], r1[:, 0:cw], g_row)

    def rec_ln_apply(m2rep, rsrep, scr, seg0, width, ycol, ystride,
                     rm=0, rr=1):
        b.wait("sync", g_row)
        chs = qchunks(width)

        def rload(t):
            c0r, cw = chs[t]
            p = t % 2
            b.wait("sync", v_ln)
            b.dma("sync", m2rep[:, p * TCW:p * TCW + cw],
                  row_bcast_ap(rm, seg0 + c0r, 128, cw), y_la[p])
            b.dma("sync", rsrep[:, p * TCW:p * TCW + cw],
                  row_bcast_ap(rr, seg0 + c0r, 128, cw), y_la[p])

        rload(0)
        if len(chs) > 1:
            rload(1)
        for t, (c0r, cw) in enumerate(chs):
            c0 = seg0 + c0r
            p = t % 2
            b.wait("vector", y_la[p])
            for fc in range(NFC):
                xsl = x_ch(fc)[:, c0:c0 + cw]
                ysl = bufB[:, ycol + fc * ystride + c0r:
                           ycol + fc * ystride + c0r + cw]
                b.ins("vector", "tensor_tensor", scr[:, 0:cw], xsl,
                      rsrep[:, p * TCW:p * TCW + cw], ALU.mult)
                b.drain("vector")
                b.ins("vector", "tensor_tensor", ysl, scr[:, 0:cw],
                      m2rep[:, p * TCW:p * TCW + cw], ALU.subtract,
                      inc=(v_ln if fc == NFC - 1 else None))
                b.drain("vector")
            if t + 2 < len(chs):
                rload(t + 2)

    def rec_allreduce_issue(mx_col):
        b.wait("gpsimd", v_mx)
        b.dma("gpsimd", d["cc_in"][:, :], scal[:, mx_col:mx_col + 1], g_cc)
        b.wait("gpsimd", g_cc)
        b.ins("gpsimd", "collective_compute", "AllReduce", ALU.max,
              [list(range(ncores))], [d["cc_in"][:, :]],
              [d["cc_out"][:, :]], inc=c_cc)

    def rec_allreduce_finish(rcol, scol, clip_col):
        b.wait("scalar", c_cc)
        b.dma("scalar", rows[0:1, 1552:1680], d["cc_out"][:, :], t_cc)
        b.wait("vector", t_cc)
        b.ins("vector", "tensor_reduce", rows[0:1, 1544:1545],
              rows[0:1, 1552:1680], AXX, ALU.max, inc=v_ln)
        b.wait("scalar", v_ln)
        b.wait("scalar", y_cn, 64)
        b.ins("scalar", "activation", rows[0:1, 1545:1546],
              rows[0:1, 1540 + clip_col:1541 + clip_col], AF.Sigmoid, inc=s_sq)
        b.wait("vector", s_sq)
        b.ins("vector", "tensor_tensor", rows[0:1, 1546:1547],
              rows[0:1, 1544:1545], rows[0:1, 1545:1546], ALU.mult)
        b.drain("vector")
        b.ins("vector", "tensor_single_scalar", rows[0:1, 1546:1547],
              rows[0:1, 1546:1547], 1.0 / QMAXF, ALU.mult)
        b.drain("vector")
        b.ins("vector", "reciprocal", rows[0:1, 1547:1548],
              rows[0:1, 1546:1547], inc=v_ln)
        b.wait("scalar", v_ln)
        b.dma("scalar", d["rows"][4:5, 129:131], rows[0:1, 1546:1548], t_cc)
        b.wait("scalar", t_cc)
        b.dma("scalar", scal[:, scol:scol + 1],
              row_bcast_ap(4, 129, 128, 1), y_sc)
        b.dma("scalar", scal[:, rcol:rcol + 1],
              row_bcast_ap(4, 130, 128, 1), y_sc)
        b.wait("vector", y_sc)
        b.wait("scalar", y_sc)

    def rec_allreduce_scale(mx_col, rcol, scol, clip_col):
        rec_allreduce_issue(mx_col)
        rec_allreduce_finish(rcol, scol, clip_col)

    def rec_quant_inplace(cols, rcol, wfscr, wint, wint2, sem):
        subs = []
        for bt, col0, width in cols:
            for c0 in range(0, width, 1024):
                subs.append((bt, col0 + c0, min(1024, width - c0), False))
        subs[-1] = subs[-1][:3] + (True,)
        v_qa0, s_cv0 = v_qa.v, s_cv.v

        def s_part(k):
            bt, c0, cw, last = subs[k]
            par = k % 2
            wc = wfscr[:, par * 1024:par * 1024 + cw]
            ti = (wint if par == 0 else wint2)[:, 0:cw]
            b.wait("scalar", v_qa, v_qa0 + max(0, k - 1))
            b.ins("scalar", "activation", wc, bt[:, c0:c0 + cw], AF.Identity,
                  scale=scal[:, rcol:rcol + 1],
                  bias=scal[:, SC_HALF:SC_HALF + 1])
            b.drain("scalar")
            b.ins("scalar", "copy", ti, wc, inc=s_cv)
            b.drain("scalar")

        def v_part2(k):
            bt, c0, cw, last = subs[k]
            par = k % 2
            wc = wfscr[:, par * 1024:par * 1024 + cw]
            ti = (wint if par == 0 else wint2)[:, 0:cw]
            b.wait("vector", s_cv, s_cv0 + k + 1)
            b.ins("vector", "tensor_tensor", wc, ti, wc, ALU.is_gt)
            b.drain("vector")
            b.ins("vector", "tensor_tensor", bt[:, c0:c0 + cw], ti, wc,
                  ALU.subtract)
            b.drain("vector")
            b.ins("vector", "tensor_scalar", bt[:, c0:c0 + cw],
                  bt[:, c0:c0 + cw], QMAXF, -128.0, ALU.min, ALU.max,
                  inc=(sem if last else v_qa))
            if last:
                b.mark("vector", v_qa)
        for k in range(len(subs)):
            s_part(k)
            if k >= 1:
                v_part2(k - 1)
        v_part2(len(subs) - 1)

    def rec_ffn(last):
        tag = '2' if last else '1'
        fs = ExitStack()
        m2rep = fs.enter_context(nc.sbuf_tensor("f_m2" + tag,
                                                [128, 2 * TCW], F32))
        rsrep = fs.enter_context(nc.sbuf_tensor("f_rs" + tag,
                                                [128, 2 * TCW], F32))
        wfscr = fs.enter_context(nc.sbuf_tensor("f_wf" + tag, [128, II], F32))
        fint = fs.enter_context(nc.sbuf_tensor("f_int" + tag, [128, 1024],
                                               mybir.dt.int32))
        hstg = fs.enter_context(nc.sbuf_tensor("f_h" + tag, [128, 2 * NT], BF16))
        silu = fs.enter_context(nc.sbuf_tensor("f_si" + tag, [128, 2 * TCW], BF16))
        sqpp = silu
        mgut = fs.enter_context(nc.sbuf_tensor("f_w" + tag, [128, 2 * 2048], BF16))
        mdc = fs.enter_context(nc.sbuf_tensor("f_wd" + tag, [128, 2 * II], BF16))
        outst = (fs.enter_context(nc.sbuf_tensor("f_o" + tag,
                                                  [128, 3 * TCW], F32))
                 if last else None)
        wq2 = (fs.enter_context(nc.sbuf_tensor("f_wq2", [128, 1024], F32))
               if not last else None)
        fint2 = fs.enter_context(nc.sbuf_tensor("f_int2" + tag, [128, 1024],
                                                mybir.dt.int32))

        b.wait("scalar", g_x)
        b.wait("tensor", g_x)
        b.wait("vector", g_x)
        if not last:
            rec_ln_stats(sqpp)
            rec_ln_apply(m2rep, rsrep, wfscr, 0, NT, 0, NT)
        else:
            rec_ln_apply(m2rep, rsrep, wfscr, 0, NT, 0, NT, rm=3, rr=4)
        b.wait("vector", v_ln)
        b.ins("vector", "tensor_reduce", scal[:, SC_TMP:SC_TMP + 1],
              bufB[:, 0:NFC * NT], AXX, ALU.max, apply_absolute_value=True,
              inc=v_mx)
        b.drain("vector")
        rec_allreduce_issue(SC_TMP)
        st_c = {}

        def rec_xscale():
            rec_allreduce_finish(SC_RX, SC_SX, 0)
            b.ins("vector", "tensor_scalar", scal[:, SC_SVG:SC_SVG + NIC],
                  scal[:, SC_WG:SC_WG + NIC], scal[:, SC_SX:SC_SX + 1],
                  1.0 / QMAXF, ALU.mult, ALU.mult)
            b.drain("vector")

        def rec_xquant():
            b.ins("vector", "tensor_scalar", scal[:, SC_SVU:SC_SVU + NIC],
                  scal[:, SC_WU:SC_WU + NIC], scal[:, SC_SX:SC_SX + 1],
                  1.0 / QMAXF, ALU.mult, ALU.mult)
            b.drain("vector")
            st_c["vlnb"] = v_ln.v
            st_c["xqb"] = v_qa.v
            rec_quant_inplace([(bufB, fc * NT + tq * 1024, 1024)
                               for tq in range(NT // 1024)
                               for fc in range(NFC)],
                              SC_RX, wfscr, fint, fint2, v_ln)
            b.ins("vector", "memset", scal[:, SC_HMX:SC_HMX + 1], 0.0)

        wq_down = {}
        if not last:
            rec_weight_quant(wfscr, wq2, m2rep, fint, fint2, mdc, hstg,
                             {21: (rec_xscale,), 43: (rec_xquant,)},
                             wq_down)
        else:
            rec_xscale()
            rec_xquant()
        vlnb, xqb = st_c["vlnb"], st_c["xqb"]
        b.wait("tensor", v_ln, vlnb)

        t_gu0, t_uu0, s_si0, v_h0 = t_gu.v, t_uu.v, s_si.v, v_h.v
        mg_r = d["mqg"].rearrange("(fc p) i -> p fc i", p=128)
        mu_r = d["mqu"].rearrange("(fc p) i -> p fc i", p=128)
        for ic in range(NIC):
            par = ic % 2
            if ic >= 2:
                b.wait("sync", t_uu, t_uu0 + NTC * (ic - 1))
            wg_dst = mgut[:, par * 2048:par * 2048 + 1024].rearrange(
                "p (fc j) -> p fc j", j=128)
            wu_dst = mgut[:, par * 2048 + 1024:par * 2048 + 2048].rearrange(
                "p (fc j) -> p fc j", j=128)
            b.dma("sync", wg_dst, mg_r[:, :, ic * 128:(ic + 1) * 128],
                  y_w[ic % 2])
            b.dma("sync", wu_dst, mu_r[:, :, ic * 128:(ic + 1) * 128],
                  y_w[ic % 2])
            if not last:
                if ic % 2 == 0 and ic // 2 < NFC:
                    wq_down["load"](ic // 2)
                elif ic % 2 == 1 and ic // 2 < NFC:
                    wq_down["compute"](ic // 2)
                if ic >= 16:
                    for j_ in range(4):
                        t_ = (ic - 16) * 4 + j_
                        if t_ < NIC:
                            rec_down_transpose_one(mdc, t_)
            b.wait("tensor", y_w[ic % 2])
            for tc in range(NTC):
                idx = ic * NTC + tc
                tpar = tc % 4
                pg = ps[:, tpar * TCW:tpar * TCW + TCW]
                pu = ps[:, 2048 + tpar * TCW:2048 + tpar * TCW + TCW]
                if ic == 0:
                    b.wait("tensor", v_qa, xqb + NFC * (tc // 2 + 1))
                if idx >= 4:
                    b.wait("tensor", v_h, v_h0 + idx - 3)
                for fc in range(NFC):
                    b.mm(pg, mgut[:, par * 2048 + fc * 128:par * 2048 + fc * 128 + 128],
                         bufB[:, fc * NT + tc * TCW:fc * NT + tc * TCW + TCW],
                         fc == 0, fc == NFC - 1,
                         inc=(t_gu if fc == NFC - 1 else None))
                for fc in range(NFC):
                    b.mm(pu, mgut[:, par * 2048 + 1024 + fc * 128:
                                  par * 2048 + 1024 + fc * 128 + 128],
                         bufB[:, fc * NT + tc * TCW:fc * NT + tc * TCW + TCW],
                         fc == 0, fc == NFC - 1,
                         inc=(t_uu if fc == NFC - 1 else None))
                b.wait("scalar", t_gu, t_gu0 + idx + 1)
                b.wait("scalar", v_h, v_h0 + max(0, idx - 1))
                b.ins("scalar", "activation",
                      silu[:, (tc % 2) * TCW:(tc % 2) * TCW + TCW],
                      pg, AF.Silu, scale=scal[:, SC_SVG + ic:SC_SVG + ic + 1],
                      inc=s_si)
                b.wait("vector", s_si, s_si0 + idx + 1)
                b.wait("vector", t_uu, t_uu0 + idx + 1)
                if tc == 0 and ic >= 2:
                    b.wait("vector", g_hs[ic % 2])
                b.ins("vector", "scalar_tensor_tensor",
                      hstg[:, par * NT + tc * TCW:par * NT + tc * TCW + TCW],
                      pu, scal[:, SC_SVU + ic:SC_SVU + ic + 1],
                      silu[:, (tc % 2) * TCW:(tc % 2) * TCW + TCW],
                      ALU.mult, ALU.mult, inc=v_h)
            b.drain("vector")
            b.ins("vector", "tensor_reduce", scal[:, SC_TMP:SC_TMP + 1],
                  hstg[:, par * NT:par * NT + NT], AXX, ALU.max,
                  apply_absolute_value=True)
            b.drain("vector")
            b.ins("vector", "tensor_tensor", scal[:, SC_HMX:SC_HMX + 1],
                  scal[:, SC_HMX:SC_HMX + 1], scal[:, SC_TMP:SC_TMP + 1],
                  ALU.max, inc=v_mx)
            b.drain("vector")
            b.wait("gpsimd", v_h, v_h0 + (ic + 1) * NTC)
            b.dma("gpsimd", d["hscr"][ic * 128:(ic + 1) * 128, :],
                  hstg[:, par * NT:par * NT + NT], g_hs[ic % 2])

        t_dn0, v_dn0 = t_dn.v, v_dn.v
        h_r = d["hscr"].rearrange("(ic p) c -> p ic c", p=128)
        HQT = NIC * TCW
        wblk = [(mdc, 0), (mdc, II), (hstg, 0), (hstg, II), (mgut, 0),
                (bufB, 2 * HQT), (bufB, 2 * HQT + II), (bufB, 2 * HQT + 2 * II)]
        for eng in ("sync", "gpsimd"):
            b.wait(eng, t_uu, t_uu0 + NIC * NTC)
            b.wait(eng, v_mx)
            b.wait(eng, y_m[0])
            b.wait(eng, y_m[1])
            b.wait(eng, g_hs[0])
            b.wait(eng, g_hs[1])
            b.wait(eng, g_dt[0])
            b.wait(eng, g_dt[1])
        for fco in range(NFC):
            bt, c0 = wblk[fco]
            if fco % 2 == 0:
                b.dma("sync", bt[:, c0:c0 + II], d["mqdC"][fco], y_w[0])
            else:
                b.dma("gpsimd", bt[:, c0:c0 + II], d["mqdC"][fco], g_dw)
        for s in (0, 1):
            dstS = bufB[:, s * HQT:s * HQT + HQT].rearrange(
                "p (ic j) -> p ic j", j=TCW)
            b.dma("sync", dstS, h_r[:, :, s * TCW:(s + 1) * TCW], y_la[s])

        rec_allreduce_scale(SC_HMX, SC_RH, SC_SH, 1)
        b.ins("vector", "tensor_scalar", scal[:, SC_SVD:SC_SVD + NFC],
              scal[:, SC_WD:SC_WD + NFC], scal[:, SC_SH:SC_SH + 1],
              0.5 / QMAXF, ALU.mult, ALU.mult)
        b.drain("vector")

        qb = [0] * NTC

        def stage_quant(s):
            ssem = y_la[s] if s < 2 else g_hr3[s % 2]
            b.wait("vector", ssem)
            b.wait("scalar", ssem)
            qb[s] = v_qa.v
            rec_quant_inplace([(bufB, (s % 2) * HQT, HQT)], SC_RH,
                              wfscr, fint, fint2, v_hq)

        stage_quant(0)
        for tc in range(NTC):
            par = tc % 2
            for ic in range(NIC):
                if ic % 2 == 0:
                    b.wait("tensor", v_qa, qb[tc] + ic // 2 + 1)
                for fco in range(NFC):
                    bt, c0 = wblk[fco]
                    if ic == 0:
                        if tc == 0:
                            if fco == 0:
                                b.wait("tensor", y_w[0])
                                b.wait("tensor", g_dw)
                                b.wait("tensor", s_si, s_si0 + NIC * NTC)
                                b.wait("tensor", v_h, v_h0 + NIC * NTC)
                        else:
                            b.wait("tensor", v_dn,
                                   v_dn0 + (tc - 1) * NFC + fco + 1)
                    lastmm = ic == NIC - 1
                    b.mm(ps[:, fco * TCW:fco * TCW + TCW],
                         bt[:, c0 + ic * 128:c0 + ic * 128 + 128],
                         bufB[:, par * HQT + ic * TCW:par * HQT + ic * TCW + TCW],
                         ic == 0, lastmm, inc=(t_dn if lastmm else None))
            for fco in range(NFC):
                b.wait("vector", t_dn, t_dn0 + tc * NFC + fco + 1)
                pd = ps[:, fco * TCW:fco * TCW + TCW]
                xsl = x_ch(fco)[:, tc * TCW:(tc + 1) * TCW]
                if not last:
                    b.ins("vector", "scalar_tensor_tensor", xsl, pd,
                          scal[:, SC_SVD + fco:SC_SVD + fco + 1], xsl,
                          ALU.mult, ALU.add, inc=v_dn)
                    b.drain("vector")
                else:
                    u_ = tc * NFC + fco
                    opar = u_ % 3
                    if u_ >= 3:
                        b.wait("vector", y_o[opar], 16 * ((u_ - 3) // 3 + 1))
                    b.ins("vector", "scalar_tensor_tensor",
                          outst[:, opar * TCW:opar * TCW + TCW], pd,
                          scal[:, SC_SVD + fco:SC_SVD + fco + 1], xsl,
                          ALU.mult, ALU.add, inc=v_dn)
                    b.drain("vector")
                    b.wait("sync", v_dn, v_dn0 + u_ + 1)
                    b.dma("sync", d["outT"][fco * 128:(fco + 1) * 128,
                                            tc * TCW:(tc + 1) * TCW],
                          outst[:, opar * TCW:opar * TCW + TCW],
                          y_o[opar])
            if tc + 2 < NTC:
                b.wait("gpsimd", t_dn, t_dn0 + (tc + 1) * NFC)
                dstS = bufB[:, (tc % 2) * HQT:(tc % 2) * HQT + HQT
                            ].rearrange("p (ic j) -> p ic j", j=TCW)
                b.dma("gpsimd", dstS,
                      h_r[:, :, (tc + 2) * TCW:(tc + 3) * TCW],
                      g_hr3[tc % 2])
            if tc + 1 < NTC:
                stage_quant(tc + 1)
        b.barrier(bar)
        fs.close()

    def pt_col(i):
        return i * 1040

    def rec_attn():
        as_ = ExitStack()
        sqpp = as_.enter_context(nc.sbuf_tensor("a_sq", [128, 2 * TCW], BF16))
        m2rep = as_.enter_context(nc.sbuf_tensor("a_m2", [128, 2 * TCW], F32))
        rsrep = as_.enter_context(nc.sbuf_tensor("a_rs", [128, 2 * TCW], F32))
        ascr = as_.enter_context(nc.sbuf_tensor("a_scr", [128, 512], F32))
        wqkt = as_.enter_context(nc.sbuf_tensor("a_wqk", [128, 2 * 1024], BF16))
        wvt = as_.enter_context(nc.sbuf_tensor("a_wv", [128, 8 * 512], BF16))
        vS = as_.enter_context(nc.sbuf_tensor("a_v", [128, NPT * 1040], BF16))
        Epp = as_.enter_context(nc.sbuf_tensor("a_e", [128, 2 * PL], BF16))
        arr = as_.enter_context(nc.sbuf_tensor("a_rr", [128, PL], F32))
        Osb = as_.enter_context(nc.sbuf_tensor("a_o", [128, 2 * PL], BF16))
        t2b = as_.enter_context(nc.sbuf_tensor("a_t2", [128, PL], BF16))
        cv64 = as_.enter_context(nc.sbuf_tensor("a_cv64", [128, PL], BF16))

        b.dma("sync", cv64[64:65, :], d["cvec"][0:1, :], y_cn)
        rec_ln_stats(sqpp)
        wq_r = d["wqp"].rearrange("(fc p) c -> p fc c", p=128)
        wk_r = d["wkp"].rearrange("(fc p) c -> p fc c", p=128)
        wv_r = d["wvp"].rearrange("(fc p) c -> p fc c", p=128)
        Y2C, QSC, KSC = 0, NFC * PL, 2 * NFC * PL

        for s in range(3):
            qlen = QLEN[s]
            seg = SEG[s]
            if s == 0:
                rec_ln_apply(m2rep, rsrep, ascr, seg, qlen, Y2C, PL)
            b.wait("vector", v_ln)
            qk_list = [(li, wr, outc, qd)
                       for li, (wr, outc) in enumerate(((wq_r, QSC),
                                                        (wk_r, KSC)))
                       for qd in range(NFC)]

            def qk_load(j):
                li, wr, outc, qd = qk_list[j]
                par = j % 2
                if j >= 2 or s >= 1:
                    b.wait("gpsimd", t_qk)
                wdst = wqkt[:, par * 1024:par * 1024 + 1024].rearrange(
                    "p (fc j) -> p fc j", j=128)
                b.dma("gpsimd", wdst, wr[:, :, qd * 128:(qd + 1) * 128],
                      g_qk[par])

            qk_load(0)
            qk_load(1)
            for j, (li, wr, outc, qd) in enumerate(qk_list):
                par = j % 2
                b.wait("tensor", g_qk[par])
                b.wait("tensor", v_ln)
                for ci, (q0, qw) in enumerate(qchunks(qlen)):
                    ppar = ci % 2
                    pq = ps[:, ppar * TCW:ppar * TCW + TCW]
                    b.wait("tensor", v_qk,
                           v_qk.v - 1 if ci >= 1 else v_qk.v)
                    for fc in range(NFC):
                        b.mm(pq[:, 0:qw],
                             wqkt[:, par * 1024 + fc * 128:
                                  par * 1024 + fc * 128 + 128],
                             bufB[:, Y2C + fc * PL + q0:Y2C + fc * PL + q0 + qw],
                             fc == 0, fc == NFC - 1,
                             inc=(t_qk if fc == NFC - 1 else None))
                    b.wait("vector", t_qk)
                    b.ins("vector", "tensor_copy",
                          bufB[:, outc + qd * PL + q0:outc + qd * PL + q0 + qw],
                          pq[:, 0:qw], inc=v_qk)
                if j + 2 < len(qk_list):
                    qk_load(j + 2)
            if s >= 1:
                b.ins("vector", "memset",
                      strided_cols(bufB, KSC + 1365, PL, NFC), 0.0)
                b.drain("vector")
            b.ins("vector", "memset", vS[64:128, 10 * 1040:11 * 1040], 0.0)
            b.drain("vector")
            b.ins("vector", "memset",
                  strided_cols(vS, 64, 65, NPT * 16), 1.0, inc=v_qk)
            b.drain("vector")
            def vbuf(vc, fc):
                if vc == 0:
                    return wvt[:, fc * TCW:(fc + 1) * TCW]
                bt = Epp if fc < 4 else Osb
                return bt[:, (fc % 4) * TCW:(fc % 4) * TCW + TCW]

            for vc in range(2):
                for fc in range(NFC):
                    b.dma("gpsimd", vbuf(vc, fc),
                          wv_r[:, fc, vc * TCW:(vc + 1) * TCW],
                          g_wv if vc == 0 else g_wv2)
            for vc in range(2):
                b.wait("tensor", g_wv if vc == 0 else g_wv2)
                for pt in range(NPT):
                    sw = min(128, qlen - pt * 128)
                    pv = ps[:, 1024 + (pt % 2) * TCW:1024 + (pt % 2) * TCW + TCW]
                    b.wait("tensor", v_qk,
                           v_qk.v - 1 if pt >= 1 else v_qk.v)
                    for fc in range(NFC):
                        b.mm(pv[0:sw, :],
                             bufB[:, Y2C + fc * PL + pt * 128:
                                  Y2C + fc * PL + pt * 128 + sw],
                             vbuf(vc, fc),
                             fc == 0, fc == NFC - 1,
                             inc=(t_qk if fc == NFC - 1 else None))
                    b.wait("vector", t_qk)
                    vdst = vS[0:sw, pt_col(pt) + 65 * 8 * vc:
                              pt_col(pt) + 65 * 8 * vc + 520].rearrange(
                        "p (t j) -> p t j", j=65)[:, :, 0:64]
                    b.ins("vector", "tensor_copy", vdst,
                          pv[0:sw, :].rearrange("p (t j) -> p t j", j=64),
                          inc=v_qk)
            b.wait("tensor", v_qk)
            if s + 1 < 3:
                b.wait("vector", t_qk)
                rec_ln_apply(m2rep, rsrep, ascr, SEG[s + 1], QLEN[s + 1],
                             Y2C, PL)
            strips = [(0, 1024), (1024, qlen - 1024)]
            pending = []
            for h in range(NH):
                ch = h // 2
                poff = 64 * (h % 2)
                opar = h % 2
                osb = Osb[0:65, opar * PL:opar * PL + PL]
                for si_, (q0, qw) in enumerate(strips):
                    if si_ == 1 and pending:
                        pending.pop(0)()
                    s_ex0 = s_ex.v
                    AVp = ps[0:65, 2048:2048 + qw]

                    def rec_scores(i, q0=q0, qw=qw, s_ex0=s_ex0):
                        pw = 128 if i < NPT - 1 else 86
                        spar = i % 2
                        b.wait("tensor", s_ex, s_ex0 + max(0, i - 1))
                        Sp = ps[:, spar * 1024:spar * 1024 + 1024]
                        lo = max(q0, 128 * i - 2)
                        hi = min(q0 + qw, 128 * i + 130)
                        bp = []
                        c = lo
                        while c < hi:
                            nxt = min(hi, q0 + ((c - q0) // 512 + 1) * 512)
                            bp.append((c, nxt))
                            c = nxt
                        bbank = set((c0 - q0) // 512 for c0, _ in bp)
                        mms = []
                        for lq0, lqw in qchunks(qw):
                            mms.append((
                                Sp[0:pw, lq0:lq0 + lqw],
                                bufB[poff:poff + 64,
                                     KSC + ch * PL + 128 * i:
                                     KSC + ch * PL + 128 * i + pw],
                                bufB[poff:poff + 64,
                                     QSC + ch * PL + q0 + lq0:
                                     QSC + ch * PL + q0 + lq0 + lqw],
                                True, (lq0 // 512) not in bbank))
                        for c0, c1 in bp:
                            mms.append((
                                Sp[0:pw, c0 - q0:c1 - q0], cident[0:pw, 0:pw],
                                cband[0:pw, c0 - (128 * i - 2):
                                      c1 - (128 * i - 2)],
                                False, True))
                        for mj, (o_, l_, r_, st, sp) in enumerate(mms):
                            b.mm(o_, l_, r_, st, sp,
                                 inc=(t_sc if mj == len(mms) - 1 else None))

                    def rec_exp(i, qw=qw):
                        pw = 128 if i < NPT - 1 else 86
                        spar = i % 2
                        b.wait("scalar", t_av)
                        b.wait("scalar", t_sc)
                        b.ins("scalar", "activation",
                              Epp[0:pw, (i % 2) * PL:(i % 2) * PL + qw],
                              ps[0:pw, spar * 1024:spar * 1024 + qw], AF.Exp,
                              scale=scal[0:pw, SC_Q8:SC_Q8 + 1], inc=s_ex)

                    def rec_av(i, q0=q0, qw=qw, s_ex0=s_ex0, AVp=AVp):
                        pw = 128 if i < NPT - 1 else 86
                        b.wait("tensor", s_ex, s_ex0 + i + 1)
                        mms = []
                        if i == 0:
                            b.wait("tensor", v_oe)
                        for lq0, lqw in qchunks(qw):
                            mms.append((
                                AVp[:, lq0:lq0 + lqw],
                                vS[0:pw, pt_col(i) + 65 * h:
                                   pt_col(i) + 65 * h + 65],
                                Epp[0:pw, (i % 2) * PL + lq0:
                                    (i % 2) * PL + lq0 + lqw],
                                i == 0, i == NPT - 1))
                        for mj, (o_, l_, r_, st, sp) in enumerate(mms):
                            b.mm(o_, l_, r_, st, sp,
                                 inc=(t_av if mj == len(mms) - 1 else None))

                    for i in range(NPT):
                        rec_scores(i)
                        rec_exp(i)
                        if i >= 1:
                            rec_av(i - 1)
                    rec_av(NPT - 1)
                    b.wait("vector", t_av)
                    b.ins("vector", "tensor_copy", osb[:, q0:q0 + qw],
                          AVp[:, 0:qw], inc=v_oe)
                    b.drain("vector")
                b.wait("vector", y_cn)
                b.ins("vector", "tensor_tensor", t2b[64:65, 0:qlen],
                      osb[64:65, 0:qlen], cv64[64:65, 0:qlen], ALU.add)
                b.drain("vector")
                b.wait("vector", g_row)
                b.ins("vector", "reciprocal", rows[64:65, 0:qlen],
                      t2b[64:65, 0:qlen], inc=v_rc)
                b.wait("sync", v_rc)
                b.dma("sync", d["rows"][2:3, 0:qlen], rows[64:65, 0:qlen],
                      g_row)
                b.wait("sync", g_row)
                b.wait("sync", v_rc)
                b.dma("sync", arr[0:64, 0:qlen],
                      row_bcast_ap(2, 0, 64, qlen), y_rr)

                def chain_tail(h=h, ch=ch, poff=poff, osb=osb, qlen=qlen,
                               seg=seg):
                    b.wait("vector", y_rr)
                    b.wait("vector", g_xa)
                    b.ins("vector", "tensor_tensor", t2b[0:64, 0:qlen],
                          osb[0:64, 0:qlen], arr[0:64, 0:qlen], ALU.mult)
                    b.drain("vector")
                    xsl = x_ch(ch)[poff:poff + 64, seg:seg + qlen]
                    if poff == 0:
                        b.ins("vector", "tensor_tensor", xsl, xsl,
                              t2b[0:64, 0:qlen], ALU.add, inc=v_xa)
                        b.drain("vector")
                    else:
                        b.mark("vector", v_rc)
                        b.wait("gpsimd", v_rc)
                        b.ins("gpsimd", "dma_start", out=xsl,
                              in_=t2b[0:64, 0:qlen], accum_op=ALU.add,
                              inc=g_xa, incn=16)
                pending.append(chain_tail)
                if s >= 1 and h in (3, 7, 11):
                    k3 = (3, 7, 11).index(h)
                    sseg, sql = SEG[s - 1], QLEN[s - 1]
                    c0s, cws = qchunks(sql)[k3]
                    rec_stats_chunk(sqpp, sseg + c0s, cws)
            while pending:
                pending.pop(0)()
            b.wait("gpsimd", g_xa)
            if s == 2:
                b.wait("tensor", g_xa)
                b.wait("vector", v_xa)
                for c0s, cws in qchunks(qlen):
                    rec_stats_chunk(sqpp, seg + c0s, cws)
                b.barrier(bar)
            else:
                for e_, sm_ in (("gpsimd", t_av), ("gpsimd", v_xa),
                                ("gpsimd", v_rc), ("tensor", s_ex),
                                ("tensor", v_oe), ("tensor", v_xa),
                                ("tensor", g_xa), ("vector", t_av),
                                ("vector", g_xa)):
                    b.wait(e_, sm_)
        as_.close()

    rec_ffn(False)
    rec_attn()
    rec_ffn(True)
    for s4 in y_o:
        b.wait("sync", s4)
    b.barrier(bar)

    b.emit()
    stk.close()
    return nc


_PERM = np.concatenate([np.arange(s, NT, 3) for s in range(3)])
_IPERM = np.argsort(_PERM)


def _consts():
    bf = ml_dtypes.bfloat16
    band = np.zeros((128, BANDW), np.float32)
    for p in range(128):
        band[p, p:p + 5] = 8.0 * np.log(2.0)
    cv = np.zeros((1, PL), np.float32)
    cv[0, 0], cv[0, 1], cv[0, PL - 2], cv[0, PL - 1] = 2, 1, 1, 2

    return {
        "cband": band.astype(bf),
        "cident": np.eye(128, dtype=np.float32).astype(bf),
        "cvec": cv.astype(bf),
                "cones": np.ones((128, 1), np.float32).astype(bf),
    }


def kernel(L0_attn_x, L0_attn_W_q, L0_attn_W_k, L0_attn_W_v,
           L0_ffn_clip_x, L0_ffn_clip_down,
           L0_ffn_w_gate, L0_ffn_w_up, L0_ffn_w_down):
    nc = build_nc()
    x = np.asarray(L0_attn_x, dtype=np.float32)
    shared = {
        "wqp": np.ascontiguousarray(np.asarray(L0_attn_W_q, np.float32)),
        "wkp": np.ascontiguousarray(np.asarray(L0_attn_W_k, np.float32)),
        "wvp": np.ascontiguousarray(np.asarray(L0_attn_W_v, np.float32)),
        "wg_row": np.ascontiguousarray(np.asarray(L0_ffn_w_gate, np.float32)),
        "wu_row": np.ascontiguousarray(np.asarray(L0_ffn_w_up, np.float32)),
        "wd_row": np.ascontiguousarray(np.asarray(L0_ffn_w_down, np.float32)),
        "clips": np.array([[float(np.asarray(L0_ffn_clip_x).reshape(-1)[0]),
                            float(np.asarray(L0_ffn_clip_down).reshape(-1)[0])]],
                          np.float32),
    }
    shared.update(_consts())
    in_maps = []
    for bi in range(8):
        m = dict(shared)
        m["xT"] = np.ascontiguousarray(x[bi][_PERM].T)
        in_maps.append(m)
    res = run_bass_kernel_spmd(nc, in_maps, core_ids=list(range(8)))
    out = np.empty((8, NT, D), np.float32)
    for bi in range(8):
        out[bi] = res.results[bi]["outT"].T[_IPERM]
    return out
